# version 14
# baseline (speedup 1.0000x reference)
import numpy as np
import ml_dtypes
import concourse.bass as bass
import concourse.mybir as mybir
from concourse.bass_utils import run_bass_kernel_spmd
from concourse.alu_op_type import AluOpType as AO
from contextlib import ExitStack

F32 = mybir.dt.float32
BF16 = mybir.dt.bfloat16
I8 = mybir.dt.int8
AF = mybir.ActivationFunctionType
AX = mybir.AxisListType.X

T = 2048
E = 2048
H = 16
KVH = 4
NEG = -1e30
LAM = 0.8 - 0.6 * float(np.exp(-0.3 * 0))
CR = 1.0 - LAM
IFF = 4096
NE = E // 128
NF2 = IFF // 128
NW_ELEM = float(E * IFF)
C0 = 1.0 / (127.0 ** 3 * 2048.0 * 64.0)
ISQ = float(1.0 / np.sqrt(128.0))


def _host_consts():
    th = 10000.0 ** (-np.arange(128, dtype=np.float64) / 128.0)
    ang = th[:, None] * np.arange(T, dtype=np.float64)[None, :]
    cos, sin = np.cos(ang), np.sin(ang)
    c_lo = np.concatenate([cos[0:64], cos[0:64]], 0).astype(np.float32)
    s_lo = np.concatenate([-sin[0:64], sin[0:64]], 0).astype(np.float32)
    c_hi = np.concatenate([cos[64:128], cos[64:128]], 0).astype(np.float32)
    s_hi = np.concatenate([-sin[64:128], sin[64:128]], 0).astype(np.float32)
    tri = np.where(np.arange(128)[None, :] > np.arange(128)[:, None],
                   np.float32(NEG), np.float32(0.0)).astype(np.float32)
    ii, jj = np.meshgrid(np.arange(128), np.arange(128), indexing="ij")
    v = (ii & jj).astype(np.int64)
    par = np.zeros((128, 128), dtype=np.int64)
    while v.max() > 0:
        par ^= (v & 1)
        v >>= 1
    h128 = np.where(par == 0, 1.0, -1.0)
    return {"c_lo": c_lo, "s_lo": s_lo, "c_hi": c_hi, "s_hi": s_hi, "tri": tri,
            "h128b": h128.astype(ml_dtypes.bfloat16),
            "h128f": h128.astype(np.float32),
            "idn": np.eye(128).astype(ml_dtypes.bfloat16),
            "ones_r": np.ones((1, 128), dtype=np.float32),
            "ones_c": np.ones((128, 1), dtype=np.float32)}


def _perm_cols(nheads):
    cols = []
    for h in range(nheads):
        b = h * 256
        cols += [b + 2 * j for j in range(64)] + [b + 2 * j + 1 for j in range(64)]
        cols += [b + 2 * j for j in range(64, 128)] + [b + 2 * j + 1 for j in range(64, 128)]
    return np.array(cols)


class Step:
    __slots__ = ("eng", "inc", "fn", "after", "lane", "_lane")

    def __init__(self, eng, inc, fn, after=None, lane=None):
        self.eng, self.inc, self.fn, self.after, self.lane = eng, inc, fn, after, lane


class Ch:
    ENG_LANE = {"tensor": "te", "vector": "ve", "scalar": "sc",
                "sync": "sy", "gpsimd": "gd"}

    def __init__(self, nc, mksem):
        self.nc = nc
        self.mksem = mksem
        self.sems = {}
        self.cnt = {}

    def lane(self, name):
        if name not in self.sems:
            self.sems[name] = self.mksem(name)
            self.cnt[name] = 0
        return self.sems[name]

    def run(self, steps):
        nc = self.nc
        lanes = []
        for s in steps:
            s._lane = s.lane if getattr(s, "lane", None) else self.ENG_LANE[s.eng]
            self.lane(s._lane)
            lanes.append(s._lane)
        base = dict(self.cnt)
        run_cnt = dict(self.cnt)
        post = []
        for s in steps:
            run_cnt[s._lane] += s.inc
            post.append((s._lane, run_cnt[s._lane]))
        waited = set()
        def _preds(i, s):
            a = s.after
            if a is None:
                return [i - 1] if i > 0 else []
            if isinstance(a, int):
                return [a] if a >= 0 else []
            return [j for j in a if j >= 0]
        for i, s in enumerate(steps):
            pr = _preds(i, s)
            if not pr:
                continue
            best = {}
            for j in pr:
                ln, c = post[j]
                if c > best.get(ln, (-1, -1))[0]:
                    best[ln] = (c, j)
            for ln, (c, j) in best.items():
                waited.add(j)
        engs = []
        for e in ("sync", "tensor", "vector", "scalar", "gpsimd"):
            if any(s.eng == e for s in steps):
                engs.append(e)
        with nc.Block() as blk:
            for ename in engs:
                def body(eng, _e=ename):
                    for i, s in enumerate(steps):
                        if s.eng != _e:
                            continue
                        preds = _preds(i, s)
                        if preds:
                            tg = {}
                            for j in preds:
                                ln, c = post[j]
                                tg[ln] = max(tg.get(ln, 0), c)
                            for ln, c in tg.items():
                                eng.wait_ge(self.sems[ln], c)
                        else:
                            for ln, c in base.items():
                                if c > 0:
                                    eng.wait_ge(self.sems[ln], c)
                        mysem = self.sems[s._lane]
                        got = {"n": 0}

                        def inc(instr, k=1, _g=got, _s=mysem):
                            _g["n"] += k
                            return instr.then_inc(_s, k)
                        s.fn(eng, inc)
                        assert got["n"] == s.inc, \
                            f"step {i} ({s.eng}): declared inc {s.inc} != emitted {got['n']}"
                        if s.eng in ("sync", "gpsimd") and i in waited:
                            eng.wait_ge(mysem, post[i][1])
                getattr(blk, ename)(body)
        self.cnt = run_cnt


def build_program(taps=(), upto=99):
    nc = bass.Bass("TRN2", target_bir_lowering=False)
    dt = lambda n, sh, ty, kind: nc.dram_tensor(n, sh, ty, kind=kind).ap()
    tk = lambda n: "ExternalOutput" if n in taps else "Internal"

    x = dt("x", [T, E], F32, "ExternalInput")
    wq = dt("wq", [E, H * 256], F32, "ExternalInput")
    wk = dt("wk", [E, KVH * 256], F32, "ExternalInput")
    wv = dt("wv", [E, KVH * 128], F32, "ExternalInput")
    wo = dt("wo", [E, E], F32, "ExternalInput")
    wupT = dt("wupT", [E, IFF], F32, "ExternalInput")
    wdnT = dt("wdnT", [IFF, E], F32, "ExternalInput")
    lq1d = dt("lq1", [128, 16], F32, "ExternalInput")
    lq2d = dt("lq2", [128, 16], F32, "ExternalInput")
    lk1d = dt("lk1", [128, 16], F32, "ExternalInput")
    lk2d = dt("lk2", [128, 16], F32, "ExternalInput")
    rmswd = dt("rmsw", [128, 1], F32, "ExternalInput")
    c_lod = dt("c_lo", [128, T], F32, "ExternalInput")
    s_lod = dt("s_lo", [128, T], F32, "ExternalInput")
    c_hid = dt("c_hi", [128, T], F32, "ExternalInput")
    s_hid = dt("s_hi", [128, T], F32, "ExternalInput")
    trid = dt("trid", [128, 128], F32, "ExternalInput")
    h128bd = dt("h128bd", [128, 128], BF16, "ExternalInput")
    h128fd = dt("h128fd", [128, 128], F32, "ExternalInput")
    idnd = dt("idnd", [128, 128], BF16, "ExternalInput")
    ones_rd = dt("ones_rd", [1, 128], F32, "ExternalInput")
    ones_cd = dt("ones_cd", [128, 1], F32, "ExternalInput")
    eps1d = dt("eps1d", [128, 1], F32, "ExternalInput")
    eps2d = dt("eps2d", [128, 1], F32, "ExternalInput")
    ohd = dt("ohd", [128, 127], F32, "ExternalInput")
    out = dt("out", [T, E], F32, "ExternalOutput")

    xnT_h = dt("xnT_h", [128, NE * T], BF16, tk("xnT_h"))
    xq_h = dt("xq_h", [128, NE * T], BF16, tk("xq_h"))
    qT_h = dt("qT_h", [128, H * 2 * T], BF16, tk("qT_h"))
    kT_h = dt("kT_h", [128, KVH * 2 * T], BF16, tk("kT_h"))
    v_h = dt("v_h", [128, KVH * T], BF16, tk("v_h"))
    s_hbm = dt("s_hbm", [128, NE * T], BF16, tk("s_hbm"))
    h2_h = dt("h2_h", [128, NF2 * T], F32, tk("h2_h"))
    att_t = dt("att_t", [128, H * T], BF16, "ExternalOutput") if "att_t" in taps else None
    k2_t = dt("k2_t", [128, NF2 * T], BF16, "ExternalOutput") if "k2_t" in taps else None
    aux_t = dt("aux_t", [8, 4096], F32, "ExternalOutput") if "aux_t" in taps else None

    st = ExitStack()
    with st:
        sbuf = lambda n, sh, ty, **kw: st.enter_context(nc.sbuf_tensor(n, sh, ty, **kw))
        ch = Ch(nc, lambda n: st.enter_context(nc.semaphore("L_" + n)))
        vssem = st.enter_context(nc.semaphore("vs"))
        vs_n = [0]

        def vsync(e, instr):
            instr.then_inc(vssem, 1)
            vs_n[0] += 1
            e.wait_ge(vssem, vs_n[0])
            return instr

        lamrep = sbuf("lamrep", [128, 16], F32)
        rms_s = sbuf("rms_s", [128, 1], F32)
        rwsu = sbuf("rwsu", [128, 1], F32)
        rwsd = sbuf("rwsd", [128, 1], F32)
        wsu_s = sbuf("wsu_s", [1, 1], F32)
        wsd_s = sbuf("wsd_s", [1, 1], F32)
        combo = sbuf("combo", [1, 1], F32)
        mrow1 = sbuf("mrow1", [1, T], F32)
        betar = sbuf("betar", [1, T], F32)
        tri_s = sbuf("tri_s", [128, 128], F32)
        h128b_s = sbuf("h128b_s", [128, 128], BF16)
        h128f_s = sbuf("h128f_s", [128, 128], F32)
        idn_s = sbuf("idn_s", [128, 128], BF16)
        onr = sbuf("onr", [1, 128], F32)
        onc = sbuf("onc", [128, 1], F32)
        eps1_s = sbuf("eps1_s", [128, 1], F32)
        eps2_s = sbuf("eps2_s", [128, 1], F32)
        oh_s = sbuf("oh_s", [128, 127], F32)

        with ExitStack() as ph:
            sb = lambda n, sh, ty: ph.enter_context(nc.sbuf_tensor(n, sh, ty))
            pt = lambda n, sh, ty=F32: ph.enter_context(nc.psum_tensor(n, sh, ty))
            l1, l2 = sb("l1", [128, 16], F32), sb("l2", [128, 16], F32)
            l3, l4 = sb("l3", [128, 16], F32), sb("l4", [128, 16], F32)
            p0, p1, p2 = pt("p0", [128, 16]), pt("p1", [128, 16]), pt("p2", [128, 16])
            row1, row2 = sb("row1", [1, 16], F32), sb("row2", [1, 16], F32)
            lamrow = sb("lamrow", [1, 16], F32)

            loads = [(tri_s, trid), (h128b_s, h128bd), (h128f_s, h128fd),
                     (idn_s, idnd), (onr, ones_rd), (onc, ones_cd),
                     (rms_s, rmswd), (l1, lq1d), (l2, lq2d), (l3, lk1d), (l4, lk2d),
                     (eps1_s, eps1d), (eps2_s, eps2d), (oh_s, ohd)]
            steps = [Step("sync", 16 * len(loads),
                          lambda e, inc, _l=loads: [inc(e.dma_start(out=d[:, :], in_=s[:, :]), 16)
                                                    for d, s in _l])]
            steps += [
                Step("vector", 1, lambda e, inc: (
                    e.tensor_mul(l1[:, :], l1[:, :], l3[:, :]),
                    inc(e.tensor_mul(l2[:, :], l2[:, :], l4[:, :])))),
                Step("tensor", 1, lambda e, inc: (
                    e.matmul(p0[0:1, :], onc[:, :], l1[:, :], start=True, stop=True),
                    inc(e.matmul(p1[0:1, :], onc[:, :], l2[:, :], start=True, stop=True)))),
                Step("scalar", 1, lambda e, inc: (
                    e.activation(row1[:, :], p0[0:1, :], AF.Exp),
                    inc(e.activation(row2[:, :], p1[0:1, :], AF.Exp)))),
                Step("vector", 1, lambda e, inc: (
                    vsync(e, e.tensor_sub(lamrow[:, :], row1[:, :], row2[:, :])),
                    inc(e.tensor_scalar_add(lamrow[:, :], lamrow[:, :], float(LAM))))),
                Step("tensor", 1, lambda e, inc:
                     inc(e.matmul(p2[:, :], onr[:, :], lamrow[:, :], start=True, stop=True))),
                Step("scalar", 1, lambda e, inc:
                     inc(e.copy(out=lamrep[:, :], in_=p2[:, :]))),
            ]
            ch.run(steps)

        if upto < 1:
            return _finish(nc, ch)
        with ExitStack() as ph:
            sb = lambda n, sh, ty: ph.enter_context(nc.sbuf_tensor(n, sh, ty))
            pt = lambda n, sh: ph.enter_context(nc.psum_tensor(n, sh, F32))
            wst = [sb(f"wst{i}", [128, 2048], F32) for i in range(2)]
            acc = sb("acc", [128, 64], F32)
            ucol, dcol = sb("ucol", [128, 1], F32), sb("dcol", [128, 1], F32)
            rtmp = sb("rtmp", [1, 1], F32)
            pA, pC = pt("pA", [128, 1]), pt("pC", [128, 1])
            tiles = [(wupT, r, c) for r in range(16) for c in range(2)] + \
                    [(wdnT, r, 0) for r in range(32)]
            steps = []
            ld_idx = {}
    
            for i, (w, r, c) in enumerate(tiles):
                eng = "sync" if i % 2 == 0 else "gpsimd"
                prev_red = 2 * (i - 2) + 1 if i >= 2 else -1
                ld_idx[i] = len(steps)
                steps.append(Step(eng, 16,
                                  lambda e, inc, _w=w, _r=r, _c=c, _i=i:
                                  inc(e.dma_start(out=wst[_i % 2][:, :],
                                                  in_=_w[_r * 128:(_r + 1) * 128,
                                                         _c * 2048:(_c + 1) * 2048]), 16),
                                  after=[prev_red] if prev_red >= 0 else -1))
                steps.append(Step("vector", 1,
                                  lambda e, inc, _i=i:
                                  inc(e.reduce_sum(acc[:, _i:_i + 1], wst[_i % 2][:, :],
                                                   axis=AX, apply_absolute_value=True)),
                                  after=[ld_idx[i]]))
            for name, lo, hi, col, wsx, rws in [("u", 0, 32, ucol, wsu_s, rwsu),
                                                ("d", 32, 64, dcol, wsd_s, rwsd)]:
                steps += [
                    Step("vector", 1, lambda e, inc, _lo=lo, _hi=hi, _col=col:
                         inc(e.reduce_sum(_col[:, :], acc[:, _lo:_hi], axis=AX))),
                    Step("tensor", 1, lambda e, inc, _col=col:
                         inc(e.matmul(pA[0:1, :], onc[:, :], _col[:, :], start=True, stop=True))),
                    Step("vector", 1, lambda e, inc, _w=wsx:
                         inc(e.tensor_scalar(_w[:, :], pA[0:1, :], 1.0 / NW_ELEM, 1e-5,
                                             AO.mult, AO.max))),
                    Step("vector", 1, lambda e, inc, _w=wsx:
                         inc(e.reciprocal(rtmp[:, :], _w[:, :]))),
                    Step("tensor", 1, lambda e, inc:
                         inc(e.matmul(pC[:, :], onr[:, :], rtmp[:, :], start=True, stop=True))),
                    Step("scalar", 1, lambda e, inc, _r=rws:
                         inc(e.copy(out=_r[:, :], in_=pC[:, :]))),
                ]
            steps.append(Step("vector", 1, lambda e, inc: (
                vsync(e, e.tensor_mul(combo[:, :], wsu_s[:, :], wsu_s[:, :])),
                vsync(e, e.tensor_mul(combo[:, :], combo[:, :], wsd_s[:, :])),
                inc(e.tensor_scalar_mul(combo[:, :], combo[:, :], float(C0))))))
            ch.run(steps)

        if upto < 2:
            return _finish(nc, ch)
        with ExitStack() as ph:
            sb = lambda n, sh, ty: ph.enter_context(nc.sbuf_tensor(n, sh, ty))
            xt = sb("xt", [128, 2048], F32)
            dump = sb("dump", [128, 2048], F32)
            sqc = sb("sqc", [128, 1], F32)
            sumc = sb("sumc", [128, 1], F32)
            mean = sb("mean", [128, 1], F32)
            msq = sb("msq", [128, 1], F32)
            var = sb("var", [128, 1], F32)
            rstd = sb("rstd", [128, 1], F32)
            negmr = sb("negmr", [128, 1], F32)
            xnbf = sb("xnbf", [128, 2048], BF16)
            xstg = sb("xstg", [128, 2048], BF16)
            psT = [ph.enter_context(nc.psum_tensor(f"psT{i}", [128, 128], BF16)) for i in range(2)]
            for tt in range(16):
                steps = [
                    Step("sync", 16, lambda e, inc, _tt=tt:
                         inc(e.dma_start(out=xt[:, :], in_=x[_tt * 128:(_tt + 1) * 128, :]), 16),
                         after=-1),
                    Step("scalar", 1, lambda e, inc:
                         inc(e.activation(dump[:, :], xt[:, :], AF.Square, accum_out=sqc[:, :]))),
                    Step("vector", 1, lambda e, inc: (
                        vsync(e, e.reduce_sum(sumc[:, :], xt[:, :], axis=AX)),
                        vsync(e, e.tensor_scalar_mul(mean[:, :], sumc[:, :], 1.0 / 2048.0)),
                        vsync(e, e.tensor_mul(msq[:, :], mean[:, :], mean[:, :])),
                        vsync(e, e.tensor_scalar_mul(var[:, :], sqc[:, :], 1.0 / 2048.0)),
                        inc(e.tensor_sub(var[:, :], var[:, :], msq[:, :])))),
                    Step("scalar", 1, lambda e, inc:
                         inc(e.activation(rstd[:, :], var[:, :], AF.Sqrt, bias=eps1_s[:, :]))),
                    Step("vector", 1, lambda e, inc: (
                        vsync(e, e.reciprocal(rstd[:, :], rstd[:, :])),
                        vsync(e, e.tensor_mul(negmr[:, :], mean[:, :], rstd[:, :])),
                        vsync(e, e.tensor_scalar_mul(negmr[:, :], negmr[:, :], -1.0)),
                        inc(e.tensor_scalar(xnbf[:, :], xt[:, :], rstd[:, :], negmr[:, :],
                                            AO.mult, AO.add)))),
                ]
                for fp in range(8):
                    steps.append(Step("tensor", 1, lambda e, inc, _f=2 * fp: (
                        e.transpose(psT[0][:, :], xnbf[:, _f * 128:(_f + 1) * 128], idn_s[:, :]),
                        inc(e.transpose(psT[1][:, :], xnbf[:, (_f + 1) * 128:(_f + 2) * 128],
                                        idn_s[:, :])))))
                    steps.append(Step("scalar", 1, lambda e, inc, _f=2 * fp: (
                        e.copy(out=xstg[:, _f * 128:(_f + 1) * 128], in_=psT[0][:, :]),
                        inc(e.copy(out=xstg[:, (_f + 1) * 128:(_f + 2) * 128], in_=psT[1][:, :])))))
                steps.append(Step("gpsimd", 16, lambda e, inc, _tt=tt:
                                  inc(e.dma_start(
                                      out=xnT_h.rearrange("p (f t) -> p f t", f=16)[
                                          :, :, _tt * 128:(_tt + 1) * 128],
                                      in_=xstg[:, :].rearrange("p (f c) -> p f c", f=16)), 16)))
                ch.run(steps)

        if upto < 3:
            return _finish(nc, ch)
        with ExitStack() as ph:
            sb = lambda n, sh, ty: ph.enter_context(nc.sbuf_tensor(n, sh, ty))
            W = [sb(f"Wf{i}", [128, 512], F32) for i in range(24)]
            xsl = sb("xsl", [128, 512], BF16)
            mx = sb("mx", [128, 512], F32)
            mn = sb("mn", [128, 512], F32)
            ptmp = sb("ptmp", [64, 512], F32)
            Sb2 = sb("Sb2", [128, 512], F32)
            qi8 = sb("qi8", [128, 512], I8)
            qbf = sb("qbf", [128, 512], BF16)
            psH = ph.enter_context(nc.psum_tensor("psH", [128, 512], F32))
            psR = ph.enter_context(nc.psum_tensor("psR", [128, 512], F32))
            for tc in range(4):
                steps = []
                for f in range(16):
                    steps.append(Step("sync", 16, lambda e, inc, _f=f, _tc=tc:
                                      inc(e.dma_start(out=xsl[:, :],
                                                      in_=xnT_h[:, _f * T + _tc * 512:
                                                                _f * T + (_tc + 1) * 512]), 16)))
                    steps.append(Step("tensor", 1, lambda e, inc:
                                      inc(e.matmul(psH[:, :], h128b_s[:, :], xsl[:, :],
                                                   start=True, stop=True))))
                    steps.append(Step("scalar", 1, lambda e, inc, _f=f:
                                      inc(e.copy(out=W[_f][:, :], in_=psH[:, :]))))

                def butterflies(e, inc, nt=16):
                    slot = list(range(nt))
                    free = list(range(nt, nt + nt // 2))
                    nbits = int(np.log2(nt))
                    for p in range(nbits):
                        b = 1 << p
                        pairs = [(i, i | b) for i in range(nt) if not (i & b)]
                        last = None
                        for idx, (i, j) in enumerate(pairs):
                            last = e.tensor_sub(W[free[idx]][:, :], W[slot[i]][:, :],
                                                W[slot[j]][:, :])
                        vsync(e, last)
                        last = None
                        for i, j in pairs:
                            last = e.tensor_add(W[slot[i]][:, :], W[slot[i]][:, :],
                                                W[slot[j]][:, :])
                        vsync(e, last)
                        for idx, (i, j) in enumerate(pairs):
                            slot[j], free[idx] = free[idx], slot[j]
                    for op, dstbuf, neg in ((AO.max, mx, False), (AO.min, mn, True)):
                        last = None
                        for t in range(nt // 2):
                            last = e.tensor_tensor(W[free[t]][:, :], W[slot[2 * t]][:, :],
                                                   W[slot[2 * t + 1]][:, :], op)
                        vsync(e, last)
                        gap = 1
                        while gap < nt // 2:
                            last = None
                            for t in range(0, nt // 2, 2 * gap):
                                last = e.tensor_tensor(W[free[t]][:, :], W[free[t]][:, :],
                                                       W[free[t + gap]][:, :], op)
                            vsync(e, last)
                            gap *= 2
                        if neg:
                            vsync(e, e.tensor_scalar_mul(dstbuf[:, :], W[free[0]][:, :], -1.0))
                        else:
                            vsync(e, e.tensor_copy(dstbuf[:, :], W[free[0]][:, :]))
                    inc(e.tensor_max(mx[:, :], mx[:, :], mn[:, :]))
                    return slot
                slot_store = {}
                steps.append(Step("vector", 1,
                                  lambda e, inc: slot_store.__setitem__("s", butterflies(e, inc))))
                for r in range(7):
                    p = 64 >> r
                    steps.append(Step("sync", 16, lambda e, inc, _p=p:
                                      inc(e.dma_start(out=ptmp[0:_p, :], in_=mx[_p:2 * _p, :]), 16)))
                    steps.append(Step("vector", 1, lambda e, inc, _p=p:
                                      inc(e.tensor_max(mx[0:_p, :], mx[0:_p, :], ptmp[0:_p, :]))))
                steps.append(Step("vector", 1, lambda e, inc, _tc=tc:
                                  inc(e.tensor_scalar_max(mrow1[0:1, _tc * 512:(_tc + 1) * 512],
                                                          mx[0:1, :], 1e-5))))
                steps.append(Step("tensor", 1, lambda e, inc, _tc=tc:
                                  inc(e.matmul(psR[:, :], onr[:, :],
                                               mrow1[0:1, _tc * 512:(_tc + 1) * 512],
                                               start=True, stop=True))))
                steps.append(Step("vector", 1, lambda e, inc: (
                    vsync(e, e.tensor_scalar_mul(Sb2[:, :], psR[:, :], 1.0 / 127.0)),
                    inc(e.reciprocal(Sb2[:, :], Sb2[:, :])))))
                for f in range(16):
                    steps.append(Step("vector", 1, lambda e, inc, _f=f: (
                        vsync(e, e.tensor_mul(qi8[:, :], W[slot_store["s"][_f]][:, :], Sb2[:, :])),
                        inc(e.tensor_copy(qbf[:, :], qi8[:, :])))))
                    steps.append(Step("gpsimd", 16, lambda e, inc, _f=f, _tc=tc:
                                      inc(e.dma_start(out=xq_h[:, _f * T + _tc * 512:
                                                               _f * T + (_tc + 1) * 512],
                                                      in_=qbf[:, :]), 16)))
                ch.run(steps)

        if upto < 4:
            return _finish(nc, ch)
        with ExitStack() as ph:
            sb = lambda n, sh, ty: ph.enter_context(nc.sbuf_tensor(n, sh, ty))
            ctab = [sb("ctab0", [128, T], F32), sb("ctab1", [128, T], F32)]
            stab = [sb("stab0", [128, T], F32), sb("stab1", [128, T], F32)]
            wqst = [sb(f"wqst{i}", [128, 8 * 2048], BF16) for i in range(2)]
            wkst = [sb(f"wkst{i}", [128, 2 * 2048], BF16) for i in range(2)]
            wvst = [sb(f"wvst{i}", [128, 2048], BF16) for i in range(2)]
            xns = [sb(f"xns{i}", [128, 16 * 512], BF16) for i in range(2)]
            Xc = [sb(f"Xc{i}", [128, 512], F32) for i in range(2)]
            Xs = [sb(f"Xs{i}", [128, 512], F32) for i in range(2)]
            m1b = [sb(f"m1b{i}", [128, 512], F32) for i in range(2)]
            qkst = [sb(f"qkst{i}", [128, 512], BF16) for i in range(2)]
            vbfb = sb("vbfb", [128, 512], BF16)
            vstg = [sb(f"vstg{i}", [128, 128], BF16) for i in range(2)]
            psP = [ph.enter_context(nc.psum_tensor(f"psP{i}", [128, 512], F32)) for i in range(2)]
            psV = [ph.enter_context(nc.psum_tensor(f"psV{i}", [128, 128], BF16)) for i in range(2)]

            ch.run([Step("sync", 16 * 4, lambda e, inc: [
                inc(e.dma_start(out=ctab[0][:, :], in_=c_lod[:, :]), 16),
                inc(e.dma_start(out=stab[0][:, :], in_=s_lod[:, :]), 16),
                inc(e.dma_start(out=ctab[1][:, :], in_=c_hid[:, :]), 16),
                inc(e.dma_start(out=stab[1][:, :], in_=s_hid[:, :]), 16)])])

            steps = []

            def add(stp):
                steps.append(stp)
                return len(steps) - 1

            i_wl_prev = {0: -1, 1: -1}
            i_xl_prev = {0: -1, 1: -1}
            i_ev_bank = {0: -1, 1: -1}
            i_vv_bank = {0: -1, 1: -1}
            i_rope_buf = {0: -1, 1: -1}
            i_vstg_buf = {0: -1, 1: -1}
            i_vbf = [-1]
            ot_ctr = [0]
            v_ctr = [0]
            for g in range(KVH):
                gpar = g % 2

                def wload(e, inc, _g=g, _p=gpar):
                    for c2 in range(2):
                        for ec in range(16):
                            inc(e.dma_start(
                                out=wkst[_p][:, c2 * 2048 + ec * 128: c2 * 2048 + (ec + 1) * 128],
                                in_=wk[ec * 128:(ec + 1) * 128,
                                       _g * 256 + c2 * 128: _g * 256 + (c2 + 1) * 128]), 16)
                    for ec in range(16):
                        inc(e.dma_start(out=wvst[_p][:, ec * 128:(ec + 1) * 128],
                                        in_=wv[ec * 128:(ec + 1) * 128,
                                               _g * 128:(_g + 1) * 128]), 16)
                    for hl in range(4):
                        for c2 in range(2):
                            for ec in range(16):
                                col = (4 * _g + hl) * 256 + c2 * 128
                                inc(e.dma_start(
                                    out=wqst[_p][:, (hl * 2 + c2) * 2048 + ec * 128:
                                                 (hl * 2 + c2) * 2048 + (ec + 1) * 128],
                                    in_=wq[ec * 128:(ec + 1) * 128, col:col + 128]), 16)
                i_wl = add(Step("gpsimd", 16 * 176, wload, after=[i_wl_prev[gpar]]))

                for tc in range(4):
                    xpar = (g * 4 + tc) % 2

                    def xload(e, inc, _tc=tc, _p=xpar):
                        for f in range(16):
                            inc(e.dma_start(out=xns[_p][:, f * 512:(f + 1) * 512],
                                            in_=xnT_h[:, f * T + _tc * 512:
                                                      f * T + (_tc + 1) * 512]), 16)
                    i_xl = add(Step("sync", 16 * 16, xload, after=[i_xl_prev[xpar]]))

                    ots = [("k", c2, None) for c2 in range(2)] + [("v", None, None)] + \
                          [("q", hl, c2) for hl in range(4) for c2 in range(2)]
                    for kind, a1, a2 in ots:
                        bank = ot_ctr[0] % 2
                        ot_ctr[0] += 1
                        if kind == "v":
                            def mmv(e, inc, _p=gpar, _xp=xpar):
                                for ec in range(16):
                                    mm = e.matmul(psP[0][:, :] if False else psP[_b][:, :],
                                                  wvst[_p][:, ec * 128:(ec + 1) * 128],
                                                  xns[_xp][:, ec * 512:(ec + 1) * 512],
                                                  start=(ec == 0), stop=(ec == 15))
                                inc(mm)
                            def mmv2(e, inc, _p=gpar, _xp=xpar, _b=bank):
                                for ec in range(16):
                                    mm = e.matmul(psP[_b][:, :],
                                                  wvst[_p][:, ec * 128:(ec + 1) * 128],
                                                  xns[_xp][:, ec * 512:(ec + 1) * 512],
                                                  start=(ec == 0), stop=(ec == 15))
                                inc(mm)
                            i_mm = add(Step("tensor", 1, mmv2,
                                            after=[i_wl, i_xl, i_ev_bank[bank]]))
                            i_ev = add(Step("scalar", 1, lambda e, inc, _b=bank:
                                            inc(e.copy(out=vbfb[:, :], in_=psP[_b][:, :])),
                                            after=[i_mm, i_vbf[0]]))
                            i_ev_bank[bank] = i_ev
                            prev_tp = i_ev
                            for i in range(4):
                                vb = v_ctr[0] % 2
                                v_ctr[0] += 1
                                i_t = add(Step("tensor", 1, lambda e, inc, _i=i, _vb=vb:
                                               inc(e.transpose(psV[_vb][:, :],
                                                               vbfb[:, _i * 128:(_i + 1) * 128],
                                                               idn_s[:, :])),
                                               after=[prev_tp, i_vv_bank[vb]]))
                                i_e2 = add(Step("scalar", 1, lambda e, inc, _vb=vb:
                                               inc(e.copy(out=vstg[_vb][:, :], in_=psV[_vb][:, :])),
                                               after=[i_t, i_vstg_buf[vb]]))
                                i_vv_bank[vb] = i_e2
                                i_d = add(Step("gpsimd", 16, lambda e, inc, _g=g, _tc=tc, _i=i, _vb=vb:
                                               inc(e.dma_start(
                                                   out=v_h[:, _g * T + (_tc * 4 + _i) * 128:
                                                           _g * T + (_tc * 4 + _i + 1) * 128],
                                                   in_=vstg[_vb][:, :]), 16),
                                               after=[i_e2]))
                                i_vstg_buf[vb] = i_d
                            i_vbf[0] = i_t
                        else:
                            if kind == "k":
                                wsrc, wcol = wkst[gpar], a1 * 2048
                                comp = a1
                                dsth, dcol = kT_h, (g * 2 + a1) * T + tc * 512
                            else:
                                wsrc, wcol = wqst[gpar], (a1 * 2 + a2) * 2048
                                comp = a2
                                dsth, dcol = qT_h, ((4 * g + a1) * 2 + a2) * T + tc * 512
                            rpar = ot_ctr[0] % 2

                            def mmp(e, inc, _w=wsrc, _wc=wcol, _xp=xpar, _b=bank):
                                for ec in range(16):
                                    mm = e.matmul(psP[_b][:, :],
                                                  _w[:, _wc + ec * 128: _wc + (ec + 1) * 128],
                                                  xns[_xp][:, ec * 512:(ec + 1) * 512],
                                                  start=(ec == 0), stop=(ec == 15))
                                inc(mm)
                            i_mm = add(Step("tensor", 1, mmp,
                                            after=[i_wl, i_xl, i_ev_bank[bank]]))
                            i_ev = add(Step("scalar", 1, lambda e, inc, _b=bank, _r=rpar: (
                                e.copy(out=Xc[_r][:, :], in_=psP[_b][:, :]),
                                e.copy(out=Xs[_r][0:64, :], in_=psP[_b][64:128, :]),
                                inc(e.copy(out=Xs[_r][64:128, :], in_=psP[_b][0:64, :]))),
                                after=[i_mm, i_rope_buf[rpar]]))
                            i_ev_bank[bank] = i_ev
                            i_ro = add(Step("vector", 1, lambda e, inc, _c=comp, _tc=tc, _r=rpar: (
                                e.tensor_mul(m1b[_r][:, :], Xc[_r][:, :],
                                             ctab[_c][:, _tc * 512:(_tc + 1) * 512]),
                                vsync(e, e.tensor_mul(Xs[_r][:, :], Xs[_r][:, :],
                                                      stab[_c][:, _tc * 512:(_tc + 1) * 512])),
                                inc(e.tensor_add(qkst[_r][:, :], m1b[_r][:, :], Xs[_r][:, :]))),
                                after=[i_ev]))
                            i_d = add(Step("gpsimd", 16, lambda e, inc, _d=dsth, _dc=dcol, _r=rpar:
                                           inc(e.dma_start(out=_d[:, _dc:_dc + 512],
                                                           in_=qkst[_r][:, :]), 16),
                                           after=[i_ro]))
                            i_rope_buf[rpar] = i_d
                    i_xl_prev[xpar] = i_mm if True else i_xl
                i_wl_prev[gpar] = i_mm
            ch.run(steps)

        with ExitStack() as ph:
            sb = lambda n, sh, ty, **kw: ph.enter_context(nc.sbuf_tensor(n, sh, ty, **kw))
            attT = sb("attT", [128, H * T], BF16)
            ph4i = ph.enter_context(ExitStack())
            sb4 = sb
            sb = lambda n, sh, ty, **kw: ph4i.enter_context(nc.sbuf_tensor(n, sh, ty, **kw))
            kg = [sb(f"kg{i}", [128, 2 * T], BF16) for i in range(2)]
            vg = [sb(f"vg{i}", [128, T], BF16) for i in range(2)]
            qh = [sb(f"qh{i}", [128, 2 * T], BF16) for i in range(2)]
            e0b = sb("e0b", [128, 2 * T], BF16)
            e1b = sb("e1b", [128, 2 * T], BF16)
            Pf = sb("Pf", [128, T], F32)
            Pb = sb("Pb", [128, 4 * T], BF16)
            PT = sb("PT", [128, 16 * 512], BF16, is_transpose_dest=True)
            A0 = sb("A0", [128, 8], F32)
            A1 = sb("A1", [128, 8], F32)
            Zc = sb("Zc", [128, 8], F32)
            S0 = sb("S0", [128, 2], F32)
            S1 = sb("S1", [128, 2], F32)
            SZ = sb("SZ", [128, 2], F32)
            r0 = sb("r0", [128, 2], F32)
            lr1 = sb("lr1", [128, 2], F32)
            r2 = sb("r2", [128, 2], F32)
            sqb = sb("sqb", [128, 512], F32)
            msb = sb("msb", [128, 512], F32)
            msrow = sb("msrow", [1, 512], F32)
            taub = sb("taub", [128, 512], F32)
            taurow = sb("taurow", [1, 2 * 512], F32)
            psE0 = [ph.enter_context(nc.psum_tensor(f"psE0{i}", [128, 512], F32)) for i in range(2)]
            psE0.append(ph4i.enter_context(nc.psum_tensor("psE02", [128, 512], F32)))
            psE1 = [ph4i.enter_context(nc.psum_tensor(f"psE1{i}", [128, 512], F32)) for i in range(3)]
            psAt = ph4i.enter_context(nc.psum_tensor("psAt", [128, 512], F32))
            psRo = ph4i.enter_context(nc.psum_tensor("psRo", [128, 512], F32))
            psRp = psRo

            qc_glob = [0]
            pair_glob = [0]
            tptb_ctr = [0]
            tp_evac_b = {0: -1, 1: -1}
            steps = []
            idx_exp_bank = {0: -1, 1: -1, 2: -1}
            idx_attmm_prev = -1
            idx_attts_prev = -1
            idx_sq_prev = -1
            idx_rowmm_prev = -1
            idx_mscp_prev = -1
            idx_msdma_prev = -1
            idx_ptp_prev = {0: [], 1: []}
            i_kvload_p = {0: -1, 1: -1}
            i_qload_p = {0: -1, 1: -1}
            kv_used = {0: [], 1: []}
            q_used = {0: [], 1: []}

            import os
            _SER = os.environ.get("PH4_SER", "")
            _PIPE = os.environ.get("PH4_PIPE")

            def add(stp, tag=""):
                if os.environ.get("PH4_SERIAL") or (tag and tag in _SER):
                    stp.after = None
                if _PIPE is not None and (not tag or tag not in _PIPE):
                    stp.after = None
                steps.append(stp)
                return len(steps) - 1

            for g in range(KVH):
                gp_ = g % 2
                for hl in range(4):
                    h = 4 * g + hl
                    hp_ = h % 2
                    last_pn = {0: -1, 1: -1}
                    last_tp = {0: -1, 1: -1}

                    if hl == 0:
                        i_kvl = add(Step("sync", 48, lambda e, inc, _g=g, _p=gp_: [
                            inc(e.dma_start(out=kg[_p][:, 0:T],
                                            in_=kT_h[:, (2 * _g) * T:(2 * _g + 1) * T]), 16),
                            inc(e.dma_start(out=kg[_p][:, T:2 * T],
                                            in_=kT_h[:, (2 * _g + 1) * T:(2 * _g + 2) * T]), 16),
                            inc(e.dma_start(out=vg[_p][:, :],
                                            in_=v_h[:, _g * T:(_g + 1) * T]), 16)],
                            after=kv_used[gp_] if kv_used[gp_] else -1))
                        i_kvload_p[gp_] = i_kvl
                        kv_used[gp_] = []
                    i_ql = add(Step("sync", 32, lambda e, inc, _h=h, _p=hp_: [
                        inc(e.dma_start(out=qh[_p][:, 0:T],
                                        in_=qT_h[:, (2 * _h) * T:(2 * _h + 1) * T]), 16),
                        inc(e.dma_start(out=qh[_p][:, T:2 * T],
                                        in_=qT_h[:, (2 * _h + 1) * T:(2 * _h + 2) * T]), 16)],
                        after=q_used[hp_] if q_used[hp_] else -1))
                    i_qload_p[hp_] = i_ql
                    q_used[hp_] = []
                    i_qload = i_ql
                    i_kvload = i_kvload_p[gp_]

                    for qc in range(4):
                        ptpar = 0
                        qc_glob[0] += 1
                        PTof = 0
                        idx_ptp_cur = []
                        idx_expP_last = -1
                        for qs in range(4):
                            qt = 4 * qc + qs
                            par = qt % 2
                            nch = qt // 4 + 1
                            idx_exp_last = -1
                            for kc in range(nch):
                                cw = 512 if kc < qt // 4 else (qt % 4 + 1) * 128
                                isdiag = (kc == qt // 4)
                                bank = pair_glob[0] % 3
                                pair_glob[0] += 1

                                def mme(e, inc, _qt=qt, _kc=kc, _cw=cw, _b=bank,
                                        _gp=gp_, _hp=hp_):
                                    e.matmul(psE0[_b][:, 0:_cw],
                                             qh[_hp][:, _qt * 128:(_qt + 1) * 128],
                                             kg[_gp][:, _kc * 512:_kc * 512 + _cw],
                                             start=True, stop=True)
                                    inc(e.matmul(psE1[_b][:, 0:_cw],
                                                 qh[_hp][:, T + _qt * 128: T + (_qt + 1) * 128],
                                                 kg[_gp][:, T + _kc * 512: T + _kc * 512 + _cw],
                                                 start=True, stop=True))
                                i_mm = add(Step("tensor", 1, mme,
                                                after=[i_qload, i_kvload,
                                                       idx_exp_bank[bank]]), tag="E")
                                kv_used[gp_].append(i_mm)
                                q_used[hp_].append(i_mm)
                                i_pre = i_mm
                                if isdiag:
                                    def trif(e, inc, _cw=cw, _b=bank):
                                        e.tensor_add(psE0[_b][:, _cw - 128:_cw],
                                                     psE0[_b][:, _cw - 128:_cw], tri_s[:, :])
                                        inc(e.tensor_add(psE1[_b][:, _cw - 128:_cw],
                                                         psE1[_b][:, _cw - 128:_cw], tri_s[:, :]))
                                    i_pre = add(Step("vector", 1, trif, after=[i_mm]), tag="E")

                                def expf(e, inc, _kc=kc, _cw=cw, _b=bank, _p=par):
                                    e.activation(e0b[:, _p * T + _kc * 512: _p * T + _kc * 512 + _cw],
                                                 psE0[_b][:, 0:_cw], AF.Exp, scale=ISQ,
                                                 accum_out=A0[:, _p * 4 + _kc:_p * 4 + _kc + 1])
                                    inc(e.activation(e1b[:, _p * T + _kc * 512: _p * T + _kc * 512 + _cw],
                                                     psE1[_b][:, 0:_cw], AF.Exp, scale=ISQ,
                                                     accum_out=A1[:, _p * 4 + _kc:_p * 4 + _kc + 1]))
                                i_exp = add(Step("scalar", 1, expf,
                                                             after=[i_pre, last_pn[par]]), tag="E")
                                idx_exp_bank[bank] = i_exp
                                idx_exp_last = i_exp

                            def rstep(e, inc, _n=nch, _h=h, _p=par):
                                e.reduce_sum(S0[:, _p:_p + 1], A0[:, _p * 4:_p * 4 + _n], axis=AX)
                                vsync(e, e.reduce_sum(S1[:, _p:_p + 1], A1[:, _p * 4:_p * 4 + _n],
                                                      axis=AX))
                                e.reciprocal(r0[:, _p:_p + 1], S0[:, _p:_p + 1])
                                vsync(e, e.reciprocal(lr1[:, _p:_p + 1], S1[:, _p:_p + 1]))
                                inc(e.tensor_mul(lr1[:, _p:_p + 1], lr1[:, _p:_p + 1],
                                                 lamrep[:, _h:_h + 1]))
                            i_r = add(Step("vector", 1, rstep,
                                           after=[idx_exp_last, last_pn[par]]), tag="R")

                            idx_dif_last = -1
                            for kc in range(nch):
                                cw = 512 if kc < qt // 4 else (qt % 4 + 1) * 128
                                isdiag = (kc == qt // 4)

                                def dif(e, inc, _kc=kc, _cw=cw, _d=isdiag, _qt=qt, _p=par):
                                    sl = slice(_p * T + _kc * 512, _p * T + _kc * 512 + _cw)
                                    e.tensor_scalar(e1b[:, sl], e1b[:, sl], lr1[:, _p:_p + 1],
                                                    None, AO.mult)
                                    vsync(e, e.tensor_scalar(e0b[:, sl], e0b[:, sl],
                                                             r0[:, _p:_p + 1], None, AO.mult))
                                    last = e.tensor_sub(e0b[:, sl], e0b[:, sl], e1b[:, sl])
                                    if _d:
                                        vsync(e, last)
                                        dsl = slice(_p * T + _qt * 128, _p * T + (_qt + 1) * 128)
                                        last = e.tensor_add(e0b[:, dsl], e0b[:, dsl], tri_s[:, :])
                                    inc(last)
                                i_dif = add(Step("vector", 1, dif, after=[i_r]), tag="D")

                                def expP(e, inc, _kc=kc, _cw=cw, _p=par):
                                    inc(e.activation(Pf[:, _kc * 512: _kc * 512 + _cw],
                                                     e0b[:, _p * T + _kc * 512: _p * T + _kc * 512 + _cw],
                                                     AF.Exp,
                                                     accum_out=Zc[:, _p * 4 + _kc:_p * 4 + _kc + 1]))
                                idx_expP_last2 = add(Step("scalar", 1, expP,
                                                          after=[i_dif, last_pn[0], last_pn[1]]),
                                                     tag="D")
                                idx_dif_last = i_dif
                                idx_expP_last = idx_expP_last2

                            def pnorm(e, inc, _n=nch, _qt=qt, _p=par, _qs=qs):
                                vsync(e, e.reduce_sum(SZ[:, _p:_p + 1], Zc[:, _p * 4:_p * 4 + _n],
                                                      axis=AX))
                                vsync(e, e.reciprocal(r2[:, _p:_p + 1], SZ[:, _p:_p + 1]))
                                inc(e.tensor_scalar(Pb[:, _qs * T: _qs * T + (_qt + 1) * 128],
                                                    Pf[:, 0:(_qt + 1) * 128],
                                                    r2[:, _p:_p + 1], None, AO.mult))
                            i_pn = add(Step("vector", 1, pnorm,
                                            after=[idx_expP_last, idx_attmm_prev]), tag="P")
                            last_pn[par] = i_pn

                            n_even = qt + 1
                            n_odd = 0

                            def ptp_e(e, inc, _qt=qt, _qs=qs, _of=PTof):
                                for kt in range(0, _qt + 1, 1):
                                    inc(e.dma_start_transpose(
                                        PT[:, _of + kt * 512 + _qs * 128: _of + kt * 512 + (_qs + 1) * 128],
                                        Pb[:, _qs * T + kt * 128: _qs * T + (kt + 1) * 128]), 16)
                            i_tp = add(Step("sync", 16 * n_even, ptp_e,
                                            after=[i_pn] + idx_ptp_prev[ptpar]), tag="T")
                            idx_ptp_cur.append(i_tp)
                            last_tp[par] = i_tp
                        zblocks = [(kt, qs) for kt in range(4 * qc + 4) for qs in range(4)
                                   if 4 * qc + qs < kt]

                        def zset(e, inc, _zb=zblocks, _of=PTof):
                            last = None
                            for kt, qs in _zb:
                                last = e.memset(
                                    PT[:, _of + kt * 512 + qs * 128: _of + kt * 512 + (qs + 1) * 128], 0.0)
                            if last is None:
                                last = e.tensor_copy(S0[:, :], S0[:, :])
                            inc(last)
                        i_z = add(Step("vector", 1, zset,
                                       after=[steps.__len__() - 1] + idx_ptp_prev[ptpar]), tag="Z")

                        def attmm(e, inc, _qc=qc, _of=PTof, _gp=gp_):
                            for kt in range(4 * _qc + 4):
                                mm = e.matmul(psAt[:, :], vg[_gp][:, kt * 128:(kt + 1) * 128],
                                              PT[:, _of + kt * 512: _of + (kt + 1) * 512],
                                              start=(kt == 0), stop=(kt == 4 * _qc + 3))
                            inc(mm)
                        i_am = add(Step("tensor", 1, attmm,
                                        after=idx_ptp_cur + [i_z, idx_attts_prev, idx_sq_prev]),
                                   tag="A")
                        idx_ptp_prev[ptpar] = [i_am]

                        i_sq = add(Step("scalar", 1, lambda e, inc:
                                        inc(e.activation(sqb[:, :], psAt[:, :], AF.Square)),
                                        after=[i_am, idx_rowmm_prev]), tag="A")
                        i_row = add(Step("tensor", 1, lambda e, inc:
                                         inc(e.matmul(psRo[0:1, :], onc[:, :], sqb[:, :],
                                                      start=True, stop=True)),
                                         after=[i_sq, idx_mscp_prev]), tag="Q")
                        i_mscp = add(Step("scalar", 1, lambda e, inc:
                                          inc(e.copy(out=msrow[0:1, :], in_=psRo[0:1, :])),
                                          after=[i_row, idx_msdma_prev]), tag="Q")
                        i_msdma = add(Step("sync", 16, lambda e, inc, _h=h, _qc=qc:
                                           inc(e.dma_start(
                                               out=msb[_h * 4 + _qc:_h * 4 + _qc + 1, :],
                                               in_=msrow[0:1, :]), 16),
                                           after=[i_mscp]), tag="Q")
                        i_ts = add(Step("vector", 1, lambda e, inc, _h=h, _qc=qc:
                                        inc(e.tensor_scalar(
                                            attT[:, _h * T + _qc * 512: _h * T + (_qc + 1) * 512],
                                            psAt[:, :], rms_s[:, :], None, AO.mult)),
                                        after=[i_am, i_sq]), tag="S")
                        idx_attmm_prev = i_am
                        idx_attts_prev = i_ts
                        idx_sq_prev = i_sq
                        idx_rowmm_prev = i_row
                        idx_mscp_prev = i_mscp
                        idx_msdma_prev = i_msdma
                        kv_used[gp_].append(i_am)
            ch.run(steps)

            steps = [
                Step("scalar", 1, lambda e, inc:
                     inc(e.activation(taub[0:64, :], msb[0:64, :], AF.Sqrt,
                                      scale=1.0 / (128.0 * CR * CR), bias=eps2_s[0:64, :]))),
                Step("vector", 1, lambda e, inc:
                     inc(e.reciprocal(taub[0:64, :], taub[0:64, :]))),
            ]
            for sb8 in range(32):
                steps.append(Step("sync", 16, lambda e, inc, _s=sb8:
                                  inc(e.dma_start(
                                      out=taurow[0:1, :].rearrange("p (r c) -> p r c", r=2),
                                      in_=taub[2 * _s:2 * _s + 2, :]), 16)))
                for rr in range(2):
                    r = sb8 * 2 + rr
                    steps.append(Step("tensor", 1, lambda e, inc, _rr=rr:
                                      inc(e.matmul(psRp[:, :], onr[:, :],
                                                   taurow[0:1, _rr * 512:(_rr + 1) * 512],
                                                   start=True, stop=True))))
                    h, qc = r // 4, r % 4
                    steps.append(Step("vector", 1, lambda e, inc, _h=h, _qc=qc:
                                      inc(e.tensor_mul(
                                          attT[:, _h * T + _qc * 512: _h * T + (_qc + 1) * 512],
                                          attT[:, _h * T + _qc * 512: _h * T + (_qc + 1) * 512],
                                          psRp[:, :]))))
            ch.run(steps)
            if att_t is not None:
                ch.run([Step("gpsimd", 16, lambda e, inc:
                             inc(e.dma_start(out=att_t[:, :], in_=attT[:, :]), 16))])

            if upto < 6:
                return _finish(nc, ch)
            ph4i.close()
            sb = sb4
            wost = [sb("wost0", [128, 2048], BF16), sb("wost1", [128, 2048], BF16)]
            sstg = [sb("sstg0", [128, 512], BF16), sb("sstg1", [128, 512], BF16)]
            psS = [psE0[0], psE0[1]]
            steps = []

            def add5(stp):
                steps.append(stp)
                return len(steps) - 1

            i_wl_p = {0: -1, 1: -1}
            i_ev_p = {0: -1, 1: -1}
            i_st_p = {0: -1, 1: -1}
            mm_ctr = [0]
            i_mm_last_p = {0: -1, 1: -1}
            for ot in range(16):
                wpar = ot % 2

                def wl5(e, inc, _ot=ot, _p=wpar):
                    for ec in range(16):
                        inc(e.dma_start(out=wost[_p][:, ec * 128:(ec + 1) * 128],
                                        in_=wo[ec * 128:(ec + 1) * 128,
                                               _ot * 128:(_ot + 1) * 128]), 16)
                i_wl = add5(Step("gpsimd", 16 * 16, wl5, after=[i_mm_last_p[wpar]]))
                for tc in range(4):
                    b = mm_ctr[0] % 2
                    mm_ctr[0] += 1

                    def mmo(e, inc, _tc=tc, _p=wpar, _b=b):
                        for ec in range(16):
                            mm = e.matmul(psS[_b][:, :], wost[_p][:, ec * 128:(ec + 1) * 128],
                                          attT[:, ec * T + _tc * 512: ec * T + (_tc + 1) * 512],
                                          start=(ec == 0), stop=(ec == 15))
                        inc(mm)
                    i_mm = add5(Step("tensor", 1, mmo, after=[i_wl, i_ev_p[b]]))
                    i_ev = add5(Step("scalar", 1, lambda e, inc, _b=b, _sp=b:
                                     inc(e.copy(out=sstg[_sp][:, :], in_=psS[_b][:, :])),
                                     after=[i_mm, i_st_p[b]]))
                    i_ev_p[b] = i_ev
                    i_st = add5(Step("gpsimd", 16, lambda e, inc, _ot=ot, _tc=tc, _sp=b:
                                     inc(e.dma_start(out=s_hbm[:, _ot * T + _tc * 512:
                                                              _ot * T + (_tc + 1) * 512],
                                                     in_=sstg[_sp][:, :]), 16),
                                     after=[i_ev]))
                    i_st_p[b] = i_st
                i_mm_last_p[wpar] = i_mm
            ch.run(steps)

        with ExitStack() as ph:
            sb = lambda n, sh, ty: ph.enter_context(nc.sbuf_tensor(n, sh, ty))
            xqs = sb("xqs", [128, NE * T], BF16)
            wstg = [sb(f"wstg{i}", [128, 2048], F32) for i in range(2)]
            i8sl = [sb(f"i8sl{i}", [128, 2048], I8) for i in range(2)]
            tsl = [sb(f"tsl{i}", [128, 2048], BF16) for i in range(2)]
            rlb = [sb(f"rlb{i}", [128, 512], F32) for i in range(2)]
            h2b = [sb(f"h2b{i}", [128, 512], F32) for i in range(2)]
            psU = [ph.enter_context(nc.psum_tensor(f"psU{i}", [128, 512], F32)) for i in range(2)]
            ch.run([Step("sync", 16 * 16, lambda e, inc: [
                inc(e.dma_start(out=xqs[:, f * T:(f + 1) * T], in_=xq_h[:, f * T:(f + 1) * T]), 16)
                for f in range(16)])])
            steps = []

            def add6(stp):
                steps.append(stp)
                return len(steps) - 1

            i_i8_p = {0: -1, 1: -1}
            i_cl_p = {0: -1, 1: -1}
            i_mml_p = {0: -1, 1: -1}
            i_tt_b = {0: -1, 1: -1}
            i_dma_b = {0: -1, 1: -1}
            mm_ctr = [0]
            for o in range(NF2):
                p = o % 2

                def wl6(e, inc, _o=o, _p=p):
                    for ec in range(16):
                        inc(e.dma_start(out=wstg[_p][:, ec * 128:(ec + 1) * 128],
                                        in_=wupT[ec * 128:(ec + 1) * 128,
                                                 _o * 128:(_o + 1) * 128]), 16)
                i_wl = add6(Step("sync", 16 * 16, wl6, after=[i_i8_p[p]]))
                i_i8 = add6(Step("scalar", 1, lambda e, inc, _p=p:
                                 inc(e.activation(i8sl[_p][:, :], wstg[_p][:, :], AF.Copy,
                                                  scale=rwsu[:, :])),
                                 after=[i_wl, i_cl_p[p]]))
                i_i8_p[p] = i_i8
                i_cl = add6(Step("vector", 1, lambda e, inc, _p=p:
                                 inc(e.tensor_scalar(tsl[_p][:, :], i8sl[_p][:, :], 1.0, -1.0,
                                                     AO.min, AO.max)),
                                 after=[i_i8, i_mml_p[p]]))
                i_cl_p[p] = i_cl
                for tc in range(4):
                    b = mm_ctr[0] % 2
                    mm_ctr[0] += 1

                    def mmu(e, inc, _tc=tc, _p=p, _b=b):
                        for ec in range(16):
                            mm = e.matmul(psU[_b][:, :], tsl[_p][:, ec * 128:(ec + 1) * 128],
                                          xqs[:, ec * T + _tc * 512: ec * T + (_tc + 1) * 512],
                                          start=(ec == 0), stop=(ec == 15))
                        inc(mm)
                    i_mm = add6(Step("tensor", 1, mmu, after=[i_cl, i_tt_b[b]]))
                    i_rl = add6(Step("scalar", 1, lambda e, inc, _b=b:
                                     inc(e.activation(rlb[_b][:, :], psU[_b][:, :], AF.Relu)),
                                     after=[i_mm]))
                    i_tt = add6(Step("vector", 1, lambda e, inc, _b=b:
                                     inc(e.tensor_mul(h2b[_b][:, :], rlb[_b][:, :], psU[_b][:, :])),
                                     after=[i_rl, i_dma_b[b]]))
                    i_tt_b[b] = i_tt
                    i_dma = add6(Step("gpsimd", 16, lambda e, inc, _o=o, _tc=tc, _b=b:
                                      inc(e.dma_start(out=h2_h[:, _o * T + _tc * 512:
                                                               _o * T + (_tc + 1) * 512],
                                                      in_=h2b[_b][:, :]), 16),
                                      after=[i_tt]))
                    i_dma_b[b] = i_dma
                i_mml_p[p] = i_mm
            ch.run(steps)

        with ExitStack() as ph:
            sb = lambda n, sh, ty: ph.enter_context(nc.sbuf_tensor(n, sh, ty))
            K2 = sb("K2", [128, NF2 * T], BF16)
            ph7 = ph.enter_context(ExitStack())
            sb7 = lambda n, sh, ty: ph7.enter_context(nc.sbuf_tensor(n, sh, ty))
            W = [sb7(f"W2f{i}", [128, 256], F32) for i in range(48)]
            mx = sb7("mx2", [128, 256], F32)
            mn = sb7("mn2", [128, 256], F32)
            ptmp = sb7("ptmp2", [64, 256], F32)
            Sb2 = sb7("Sb22", [128, 256], F32)
            qi8 = sb7("qi82", [128, 256], I8)
            mrow2 = sb7("mrow2", [1, 256], F32)
            psH = ph7.enter_context(nc.psum_tensor("psH2", [128, 256], F32))
            psR = ph7.enter_context(nc.psum_tensor("psR2", [128, 256], F32))


            def butterflies2(e, inc, nt=32):
                slot = list(range(nt))
                free = list(range(nt, nt + nt // 2))
                for p in range(5):
                    b = 1 << p
                    pairs = [(i, i | b) for i in range(nt) if not (i & b)]
                    last = None
                    for idx, (i, j) in enumerate(pairs):
                        last = e.tensor_sub(W[free[idx]][:, :], W[slot[i]][:, :],
                                            W[slot[j]][:, :])
                    vsync(e, last)
                    last = None
                    for i, j in pairs:
                        last = e.tensor_add(W[slot[i]][:, :], W[slot[i]][:, :],
                                            W[slot[j]][:, :])
                    vsync(e, last)
                    for idx, (i, j) in enumerate(pairs):
                        slot[j], free[idx] = free[idx], slot[j]
                for op, dstbuf, neg in ((AO.max, mx, False), (AO.min, mn, True)):
                    last = None
                    for t in range(nt // 2):
                        last = e.tensor_tensor(W[free[t]][:, :], W[slot[2 * t]][:, :],
                                               W[slot[2 * t + 1]][:, :], op)
                    vsync(e, last)
                    gap = 1
                    while gap < nt // 2:
                        last = None
                        for t in range(0, nt // 2, 2 * gap):
                            last = e.tensor_tensor(W[free[t]][:, :], W[free[t]][:, :],
                                                   W[free[t + gap]][:, :], op)
                        vsync(e, last)
                        gap *= 2
                    if neg:
                        vsync(e, e.tensor_scalar_mul(dstbuf[:, :], W[free[0]][:, :], -1.0))
                    else:
                        vsync(e, e.tensor_copy(dstbuf[:, :], W[free[0]][:, :]))
                inc(e.tensor_max(mx[:, :], mx[:, :], mn[:, :]))
                return slot

            ld_prev = [-1]
            mm_prev_b = {0: -1, 1: -1}
            ev_prev = [-1]
            qt_prev = [-1]
            srep_prev = [-1]
            steps = []

            def add7(stp):
                steps.append(stp)
                return len(steps) - 1

            for cc in range(8):
                def ld7(e, inc, _cc=cc):
                    for o in range(NF2):
                        inc(e.dma_start(out=W[o][:, :],
                                        in_=h2_h[:, o * T + _cc * 256:
                                                 o * T + (_cc + 1) * 256]), 16)
                i_ld = add7(Step("sync", 16 * 32, ld7, after=[qt_prev[0]]))
                i_ev_last = -1
                for o in range(NF2):
                    b = o % 2
                    i_mm = add7(Step("tensor", 1, lambda e, inc, _o=o, _b=b:
                                     inc(e.matmul(psH[:, 0:256] if False else
                                                  (psH if _b == 0 else psR)[:, :],
                                                  h128f_s[:, :], W[_o][:, :],
                                                  start=True, stop=True)),
                                     after=[i_ld, mm_prev_b[b]]))
                    i_ev = add7(Step("scalar", 1, lambda e, inc, _o=o, _b=b:
                                     inc(e.copy(out=W[_o][:, :],
                                                in_=(psH if _b == 0 else psR)[:, :])),
                                     after=[i_mm]))
                    mm_prev_b[b] = i_ev
                    i_ev_last = i_ev
                sstore = {}
                i_bf = add7(Step("vector", 1,
                                 lambda e, inc: sstore.__setitem__("s", butterflies2(e, inc)),
                                 after=[i_ev_last]))
                i_prev = i_bf
                for r in range(7):
                    p = 64 >> r
                    i_d = add7(Step("sync", 16, lambda e, inc, _p=p:
                                    inc(e.dma_start(out=ptmp[0:_p, :], in_=mx[_p:2 * _p, :]), 16),
                                    after=[i_prev]))
                    i_prev = add7(Step("vector", 1, lambda e, inc, _p=p:
                                       inc(e.tensor_max(mx[0:_p, :], mx[0:_p, :], ptmp[0:_p, :])),
                                       after=[i_d]))
                i_bt = add7(Step("vector", 1, lambda e, inc, _cc=cc: (
                    vsync(e, e.tensor_scalar_max(mrow2[0:1, :], mx[0:1, :], 1e-5)),
                    vsync(e, e.tensor_mul(betar[0:1, _cc * 256:(_cc + 1) * 256],
                                          mrow1[0:1, _cc * 256:(_cc + 1) * 256],
                                          mrow1[0:1, _cc * 256:(_cc + 1) * 256])),
                    vsync(e, e.tensor_mul(betar[0:1, _cc * 256:(_cc + 1) * 256],
                                          betar[0:1, _cc * 256:(_cc + 1) * 256], mrow2[0:1, :])),
                    inc(e.tensor_scalar(betar[0:1, _cc * 256:(_cc + 1) * 256],
                                        betar[0:1, _cc * 256:(_cc + 1) * 256],
                                        combo[0:1, :], None, AO.mult))),
                    after=[i_prev, srep_prev[0]]))
                i_rp = add7(Step("tensor", 1, lambda e, inc:
                                 inc(e.matmul(psR[:, 0:256], onr[:, :], mrow2[0:1, :],
                                              start=True, stop=True)),
                                 after=[i_bt]))
                i_sr = add7(Step("vector", 1, lambda e, inc: (
                    vsync(e, e.tensor_scalar_mul(Sb2[:, :], psR[:, 0:256], 1.0 / 127.0)),
                    inc(e.reciprocal(Sb2[:, :], Sb2[:, :]))),
                    after=[i_rp]))
                srep_prev[0] = i_sr
                i_q_last = -1
                for o in range(NF2):
                    i_q = add7(Step("vector", 1, lambda e, inc, _o=o, _cc=cc: (
                        vsync(e, e.tensor_mul(qi8[:, :], W[sstore["s"][_o]][:, :], Sb2[:, :])),
                        inc(e.tensor_copy(K2[:, _o * T + _cc * 256: _o * T + (_cc + 1) * 256],
                                          qi8[:, :]))),
                        after=[i_sr] if o == 0 else None))
                    i_q_last = i_q
                qt_prev[0] = i_q_last
            ch.run(steps)
            if k2_t is not None:
                ch.run([Step("gpsimd", 16 * 8, lambda e, inc: [
                    inc(e.dma_start(out=k2_t[:, i * 8192:(i + 1) * 8192],
                                    in_=K2[:, i * 8192:(i + 1) * 8192]), 16) for i in range(8)])])

            ph7.close()

            if upto < 9:
                return _finish(nc, ch)
            brep = sb("brep", [128, T], F32)
            brepp = ph.enter_context(nc.psum_tensor("psB0", [128, 512], F32))
            steps = []
            for tc in range(4):
                steps.append(Step("tensor", 1, lambda e, inc, _tc=tc:
                                  inc(e.matmul(brepp[:, :], onr[:, :],
                                               betar[0:1, _tc * 512:(_tc + 1) * 512],
                                               start=True, stop=True))))
                steps.append(Step("scalar", 1, lambda e, inc, _tc=tc:
                                  inc(e.copy(out=brep[:, _tc * 512:(_tc + 1) * 512],
                                             in_=brepp[:, :]))))
            ch.run(steps)

            wst2 = sb("wst2", [128, 4096], F32)
            i8s2 = sb("i8s2", [128, 4096], I8)
            tsl2 = [sb(f"tsl2{i}", [128, 4096], BF16) for i in range(2)]
            ub = [sb(f"ub{i}", [128, 512], F32) for i in range(2)]
            gb = ub
            stile = [sb(f"stile{i}", [128, 512], BF16) for i in range(2)]
            sgb = [sb(f"sgb{i}", [128, 512], BF16) for i in range(2)]
            xblk4 = [sb(f"xblk4{i}", [128, 512], F32) for i in range(2)]
            obuf4 = [sb(f"obuf4{i}", [128, 512], F32) for i in range(2)]
            psD = [ph.enter_context(nc.psum_tensor(f"psD{i}", [128, 512], F32)) for i in range(2)]
            psX = [ph.enter_context(nc.psum_tensor(f"psX{i}", [128, 128], BF16)) for i in range(2)]
            steps = []

            def add8(stp):
                steps.append(stp)
                return len(steps) - 1

            i_i8_l = [-1]
            i_cl_p = {0: -1, 1: -1}
            i_mml_p = {0: -1, 1: -1}
            i_u_b = {0: -1, 1: -1}
            i_sg_p = {0: -1, 1: -1}
            i_sl_p = {0: -1, 1: -1}
            i_xl_p = {0: -1, 1: -1}
            i_tp_b = {0: -1, 1: -1}
            i_xa_b = {0: -1, 1: -1}
            i_ob_p = {0: -1, 1: -1}
            i_od_p = {0: -1, 1: -1}
            mm_ctr = [0]
            tp_ctr = [0]
            for o2 in range(16):
                p = o2 % 2

                def wl8(e, inc, _o2=o2):
                    for ec in range(32):
                        inc(e.dma_start(out=wst2[:, ec * 128:(ec + 1) * 128],
                                        in_=wdnT[ec * 128:(ec + 1) * 128,
                                                 _o2 * 128:(_o2 + 1) * 128]), 16)
                i_wl = add8(Step("sync", 16 * 32, wl8, after=[i_i8_l[0]]))
                i_i8 = add8(Step("scalar", 1, lambda e, inc:
                                 inc(e.activation(i8s2[:, :], wst2[:, :], AF.Copy,
                                                  scale=rwsd[:, :])),
                                 after=[i_wl, i_cl_p[1 - p]]))
                i_i8_l[0] = i_i8
                i_cl = add8(Step("vector", 1, lambda e, inc, _p=p:
                                 inc(e.tensor_scalar(tsl2[_p][:, :], i8s2[:, :], 1.0, -1.0,
                                                     AO.min, AO.max)),
                                 after=[i_i8, i_mml_p[p]]))
                i_cl_p[p] = i_cl
                for tc in range(4):
                    b = mm_ctr[0] % 2
                    mm_ctr[0] += 1
                    i_sl = add8(Step("gpsimd", 16, lambda e, inc, _o2=o2, _tc=tc, _b=b:
                                     inc(e.dma_start(out=stile[_b][:, :],
                                                     in_=s_hbm[:, _o2 * T + _tc * 512:
                                                              _o2 * T + (_tc + 1) * 512]), 16),
                                     after=[i_sg_p[b]]))
                    i_xl = add8(Step("gpsimd", 16, lambda e, inc, _o2=o2, _tc=tc, _b=b:
                                     inc(e.dma_start(
                                         out=xblk4[_b][:, :].rearrange("p (i c) -> p i c", i=4),
                                         in_=x[_tc * 512:(_tc + 1) * 512,
                                               _o2 * 128:(_o2 + 1) * 128]
                                         .rearrange("(i p) c -> p i c", i=4)), 16),
                                     after=[i_xa_b[b]]))

                    def mmd(e, inc, _tc=tc, _p=p, _b=b):
                        for ec in range(32):
                            mm = e.matmul(psD[_b][:, :], tsl2[_p][:, ec * 128:(ec + 1) * 128],
                                          K2[:, ec * T + _tc * 512: ec * T + (_tc + 1) * 512],
                                          start=(ec == 0), stop=(ec == 31))
                        inc(mm)
                    i_mm = add8(Step("tensor", 1, mmd, after=[i_cl, i_u_b[b]]))
                    i_u = add8(Step("vector", 1, lambda e, inc, _tc=tc, _b=b:
                                    inc(e.tensor_mul(ub[_b][:, :], psD[_b][:, :],
                                                     brep[:, _tc * 512:(_tc + 1) * 512])),
                                    after=[i_mm, i_sg_p[b]]))
                    i_u_b[b] = i_u
                    i_g = add8(Step("scalar", 1, lambda e, inc, _b=b:
                                    inc(e.activation(gb[_b][:, :], ub[_b][:, :], AF.Silu)),
                                    after=[i_u]))
                    i_sg = add8(Step("vector", 1, lambda e, inc, _b=b:
                                     inc(e.tensor_mul(sgb[_b][:, :], gb[_b][:, :],
                                                      stile[_b][:, :])),
                                     after=[i_g, i_sl]))
                    i_sg_p[b] = i_sg
                    i_xa_last = -1
                    for i in range(4):
                        xb = tp_ctr[0] % 2
                        tp_ctr[0] += 1
                        i_t = add8(Step("tensor", 1, lambda e, inc, _i=i, _xb=xb, _b=b:
                                        inc(e.transpose(psX[_xb][:, :],
                                                        sgb[_b][:, _i * 128:(_i + 1) * 128],
                                                        idn_s[:, :])),
                                        after=[i_sg, i_tp_b[xb]]))
                        i_xa = add8(Step("vector", 1, lambda e, inc, _i=i, _xb=xb, _b=b:
                                         inc(e.tensor_add(obuf4[_b][:, _i * 128:(_i + 1) * 128],
                                                          psX[_xb][:, :],
                                                          xblk4[_b][:, _i * 128:(_i + 1) * 128])),
                                         after=[i_t, i_xl, i_od_p[b]]))
                        i_tp_b[xb] = i_xa
                        i_xa_last = i_xa
                    i_xa_b[b] = i_xa_last
                    i_od = add8(Step("gpsimd", 16, lambda e, inc, _o2=o2, _tc=tc, _b=b:
                                     inc(e.dma_start(
                                         out=out[_tc * 512:(_tc + 1) * 512,
                                                 _o2 * 128:(_o2 + 1) * 128]
                                         .rearrange("(i p) c -> p i c", i=4),
                                         in_=obuf4[_b][:, :].rearrange("p (i c) -> p i c", i=4)), 16),
                                     after=[i_xa_last]))
                    i_od_p[b] = i_od
                i_mml_p[p] = i_mm
            ch.run(steps)

        return _finish(nc, ch)


def _finish(nc, ch):
    fin = dict(ch.cnt)
    sems = dict(ch.sems)
    with nc.Block() as blk:
        @blk.gpsimd
        def _(gp):
            for ln, c in fin.items():
                if c > 0:
                    gp.wait_ge(sems[ln], c)
    return nc


def _prep_inputs(L0_attn_x, L0_attn_Wq, L0_attn_Wk, L0_attn_Wv, L0_attn_Wo,
                 L0_attn_lq1, L0_attn_lq2, L0_attn_lk1, L0_attn_lk2,
                 L0_attn_rms_w, L0_ffn_w_up, L0_ffn_w_down):
    cons = _host_consts()
    f32 = np.float32
    wq = np.ascontiguousarray(np.asarray(L0_attn_Wq, dtype=f32)[:, _perm_cols(H)])
    wk = np.ascontiguousarray(np.asarray(L0_attn_Wk, dtype=f32)[:, _perm_cols(KVH)])
    wv = np.ascontiguousarray(np.asarray(L0_attn_Wv, dtype=f32))
    wo = np.ascontiguousarray(np.asarray(L0_attn_Wo, dtype=f32))
    wupT = np.ascontiguousarray(np.asarray(L0_ffn_w_up, dtype=f32).T)
    wdnT = np.ascontiguousarray(np.asarray(L0_ffn_w_down, dtype=f32).T)
    base = {
        "wq": wq, "wk": wk, "wv": wv, "wo": wo, "wupT": wupT, "wdnT": wdnT,
        "lq1": np.ascontiguousarray(np.asarray(L0_attn_lq1, dtype=f32).T),
        "lq2": np.ascontiguousarray(np.asarray(L0_attn_lq2, dtype=f32).T),
        "lk1": np.ascontiguousarray(np.asarray(L0_attn_lk1, dtype=f32).T),
        "lk2": np.ascontiguousarray(np.asarray(L0_attn_lk2, dtype=f32).T),
        "rmsw": np.ascontiguousarray(np.asarray(L0_attn_rms_w, dtype=f32).reshape(128, 1)),
        "c_lo": cons["c_lo"], "s_lo": cons["s_lo"],
        "c_hi": cons["c_hi"], "s_hi": cons["s_hi"],
        "trid": cons["tri"], "h128bd": cons["h128b"], "h128fd": cons["h128f"],
        "idnd": cons["idn"], "ones_rd": cons["ones_r"], "ones_cd": cons["ones_c"],
        "eps1d": np.full((128, 1), 1e-5, dtype=f32),
        "eps2d": np.full((128, 1), 1e-8 / (CR * CR), dtype=f32),
        "ohd": (np.arange(127)[None, :] == 63).astype(f32) * np.ones((128, 1), f32),
    }
    xs = np.asarray(L0_attn_x, dtype=f32)
    in_maps = []
    for b in range(8):
        m = dict(base)
        m["x"] = np.ascontiguousarray(xs[b])
        in_maps.append(m)
    return in_maps


def kernel(L0_attn_x, L0_attn_Wq, L0_attn_Wk, L0_attn_Wv, L0_attn_Wo,
           L0_attn_lq1, L0_attn_lq2, L0_attn_lk1, L0_attn_lk2,
           L0_attn_rms_w, L0_ffn_w_up, L0_ffn_w_down):
    in_maps = _prep_inputs(L0_attn_x, L0_attn_Wq, L0_attn_Wk, L0_attn_Wv, L0_attn_Wo,
                           L0_attn_lq1, L0_attn_lq2, L0_attn_lk1, L0_attn_lk2,
                           L0_attn_rms_w, L0_ffn_w_up, L0_ffn_w_down)
    nc = build_program()
    res = run_bass_kernel_spmd(nc, in_maps, core_ids=list(range(8)))
    return np.stack([res.results[b]["out"] for b in range(8)], axis=0)


# version 15
# speedup vs baseline: 1.0014x; 1.0014x over previous
import numpy as np
import ml_dtypes
import concourse.bass as bass
import concourse.mybir as mybir
from concourse.bass_utils import run_bass_kernel_spmd
from concourse.alu_op_type import AluOpType as AO
from contextlib import ExitStack

F32 = mybir.dt.float32
BF16 = mybir.dt.bfloat16
I8 = mybir.dt.int8
AF = mybir.ActivationFunctionType
AX = mybir.AxisListType.X

T = 2048
E = 2048
H = 16
KVH = 4
NEG = -1e30
LAM = 0.8 - 0.6 * float(np.exp(-0.3 * 0))
CR = 1.0 - LAM
IFF = 4096
NE = E // 128
NF2 = IFF // 128
NW_ELEM = float(E * IFF)
C0 = 1.0 / (127.0 ** 3 * 2048.0 * 64.0)
ISQ = float(1.0 / np.sqrt(128.0))


def _host_consts():
    th = 10000.0 ** (-np.arange(128, dtype=np.float64) / 128.0)
    ang = th[:, None] * np.arange(T, dtype=np.float64)[None, :]
    cos, sin = np.cos(ang), np.sin(ang)
    c_lo = np.concatenate([cos[0:64], cos[0:64]], 0).astype(np.float32)
    s_lo = np.concatenate([-sin[0:64], sin[0:64]], 0).astype(np.float32)
    c_hi = np.concatenate([cos[64:128], cos[64:128]], 0).astype(np.float32)
    s_hi = np.concatenate([-sin[64:128], sin[64:128]], 0).astype(np.float32)
    tri = np.where(np.arange(128)[None, :] > np.arange(128)[:, None],
                   np.float32(NEG), np.float32(0.0)).astype(np.float32)
    ii, jj = np.meshgrid(np.arange(128), np.arange(128), indexing="ij")
    v = (ii & jj).astype(np.int64)
    par = np.zeros((128, 128), dtype=np.int64)
    while v.max() > 0:
        par ^= (v & 1)
        v >>= 1
    h128 = np.where(par == 0, 1.0, -1.0)
    return {"c_lo": c_lo, "s_lo": s_lo, "c_hi": c_hi, "s_hi": s_hi, "tri": tri,
            "h128b": h128.astype(ml_dtypes.bfloat16),
            "h128f": h128.astype(np.float32),
            "idn": np.eye(128).astype(ml_dtypes.bfloat16),
            "ones_r": np.ones((1, 128), dtype=np.float32),
            "ones_c": np.ones((128, 1), dtype=np.float32)}


def _perm_cols(nheads):
    cols = []
    for h in range(nheads):
        b = h * 256
        cols += [b + 2 * j for j in range(64)] + [b + 2 * j + 1 for j in range(64)]
        cols += [b + 2 * j for j in range(64, 128)] + [b + 2 * j + 1 for j in range(64, 128)]
    return np.array(cols)


class Step:
    __slots__ = ("eng", "inc", "fn", "after", "lane", "_lane")

    def __init__(self, eng, inc, fn, after=None, lane=None):
        self.eng, self.inc, self.fn, self.after, self.lane = eng, inc, fn, after, lane


class Ch:
    ENG_LANE = {"tensor": "te", "vector": "ve", "scalar": "sc",
                "sync": "sy", "gpsimd": "gd"}

    def __init__(self, nc, mksem):
        self.nc = nc
        self.mksem = mksem
        self.sems = {}
        self.cnt = {}

    def lane(self, name):
        if name not in self.sems:
            self.sems[name] = self.mksem(name)
            self.cnt[name] = 0
        return self.sems[name]

    def run(self, steps):
        nc = self.nc
        lanes = []
        for s in steps:
            s._lane = s.lane if getattr(s, "lane", None) else self.ENG_LANE[s.eng]
            self.lane(s._lane)
            lanes.append(s._lane)
        base = dict(self.cnt)
        run_cnt = dict(self.cnt)
        post = []
        for s in steps:
            run_cnt[s._lane] += s.inc
            post.append((s._lane, run_cnt[s._lane]))
        waited = set()
        def _preds(i, s):
            a = s.after
            if a is None:
                return [i - 1] if i > 0 else []
            if isinstance(a, int):
                return [a] if a >= 0 else []
            return [j for j in a if j >= 0]
        for i, s in enumerate(steps):
            pr = _preds(i, s)
            if not pr:
                continue
            best = {}
            for j in pr:
                ln, c = post[j]
                if c > best.get(ln, (-1, -1))[0]:
                    best[ln] = (c, j)
            for ln, (c, j) in best.items():
                waited.add(j)
        engs = []
        for e in ("sync", "tensor", "vector", "scalar", "gpsimd"):
            if any(s.eng == e for s in steps):
                engs.append(e)
        with nc.Block() as blk:
            for ename in engs:
                def body(eng, _e=ename):
                    for i, s in enumerate(steps):
                        if s.eng != _e:
                            continue
                        preds = _preds(i, s)
                        if preds:
                            tg = {}
                            for j in preds:
                                ln, c = post[j]
                                tg[ln] = max(tg.get(ln, 0), c)
                            for ln, c in tg.items():
                                eng.wait_ge(self.sems[ln], c)
                        else:
                            for ln, c in base.items():
                                if c > 0:
                                    eng.wait_ge(self.sems[ln], c)
                        mysem = self.sems[s._lane]
                        got = {"n": 0}

                        def inc(instr, k=1, _g=got, _s=mysem):
                            _g["n"] += k
                            return instr.then_inc(_s, k)
                        s.fn(eng, inc)
                        assert got["n"] == s.inc, \
                            f"step {i} ({s.eng}): declared inc {s.inc} != emitted {got['n']}"
                        if s.eng in ("sync", "gpsimd") and i in waited:
                            eng.wait_ge(mysem, post[i][1])
                getattr(blk, ename)(body)
        self.cnt = run_cnt


def build_program(taps=(), upto=99):
    nc = bass.Bass("TRN2", target_bir_lowering=False)
    dt = lambda n, sh, ty, kind: nc.dram_tensor(n, sh, ty, kind=kind).ap()
    tk = lambda n: "ExternalOutput" if n in taps else "Internal"

    x = dt("x", [T, E], F32, "ExternalInput")
    wq = dt("wq", [E, H * 256], F32, "ExternalInput")
    wk = dt("wk", [E, KVH * 256], F32, "ExternalInput")
    wv = dt("wv", [E, KVH * 128], F32, "ExternalInput")
    wo = dt("wo", [E, E], F32, "ExternalInput")
    wupT = dt("wupT", [E, IFF], F32, "ExternalInput")
    wdnT = dt("wdnT", [IFF, E], F32, "ExternalInput")
    lq1d = dt("lq1", [128, 16], F32, "ExternalInput")
    lq2d = dt("lq2", [128, 16], F32, "ExternalInput")
    lk1d = dt("lk1", [128, 16], F32, "ExternalInput")
    lk2d = dt("lk2", [128, 16], F32, "ExternalInput")
    rmswd = dt("rmsw", [128, 1], F32, "ExternalInput")
    c_lod = dt("c_lo", [128, T], F32, "ExternalInput")
    s_lod = dt("s_lo", [128, T], F32, "ExternalInput")
    c_hid = dt("c_hi", [128, T], F32, "ExternalInput")
    s_hid = dt("s_hi", [128, T], F32, "ExternalInput")
    trid = dt("trid", [128, 128], F32, "ExternalInput")
    h128bd = dt("h128bd", [128, 128], BF16, "ExternalInput")
    h128fd = dt("h128fd", [128, 128], F32, "ExternalInput")
    idnd = dt("idnd", [128, 128], BF16, "ExternalInput")
    ones_rd = dt("ones_rd", [1, 128], F32, "ExternalInput")
    ones_cd = dt("ones_cd", [128, 1], F32, "ExternalInput")
    eps1d = dt("eps1d", [128, 1], F32, "ExternalInput")
    eps2d = dt("eps2d", [128, 1], F32, "ExternalInput")
    ohd = dt("ohd", [128, 127], F32, "ExternalInput")
    out = dt("out", [T, E], F32, "ExternalOutput")

    xnT_h = dt("xnT_h", [128, NE * T], BF16, tk("xnT_h"))
    xq_h = dt("xq_h", [128, NE * T], BF16, tk("xq_h"))
    qT_h = dt("qT_h", [128, H * 2 * T], BF16, tk("qT_h"))
    kT_h = dt("kT_h", [128, KVH * 2 * T], BF16, tk("kT_h"))
    v_h = dt("v_h", [128, KVH * T], BF16, tk("v_h"))
    s_hbm = dt("s_hbm", [128, NE * T], BF16, tk("s_hbm"))
    h2_h = dt("h2_h", [128, NF2 * T], F32, tk("h2_h"))
    att_t = dt("att_t", [128, H * T], BF16, "ExternalOutput") if "att_t" in taps else None
    k2_t = dt("k2_t", [128, NF2 * T], BF16, "ExternalOutput") if "k2_t" in taps else None
    aux_t = dt("aux_t", [8, 4096], F32, "ExternalOutput") if "aux_t" in taps else None

    st = ExitStack()
    with st:
        sbuf = lambda n, sh, ty, **kw: st.enter_context(nc.sbuf_tensor(n, sh, ty, **kw))
        ch = Ch(nc, lambda n: st.enter_context(nc.semaphore("L_" + n)))
        vssem = st.enter_context(nc.semaphore("vs"))
        vs_n = [0]

        def vsync(e, instr):
            instr.then_inc(vssem, 1)
            vs_n[0] += 1
            e.wait_ge(vssem, vs_n[0])
            return instr

        lamrep = sbuf("lamrep", [128, 16], F32)
        rms_s = sbuf("rms_s", [128, 1], F32)
        rwsu = sbuf("rwsu", [128, 1], F32)
        rwsd = sbuf("rwsd", [128, 1], F32)
        wsu_s = sbuf("wsu_s", [1, 1], F32)
        wsd_s = sbuf("wsd_s", [1, 1], F32)
        combo = sbuf("combo", [1, 1], F32)
        mrow1 = sbuf("mrow1", [1, T], F32)
        betar = sbuf("betar", [1, T], F32)
        tri_s = sbuf("tri_s", [128, 128], F32)
        h128b_s = sbuf("h128b_s", [128, 128], BF16)
        h128f_s = sbuf("h128f_s", [128, 128], F32)
        idn_s = sbuf("idn_s", [128, 128], BF16)
        onr = sbuf("onr", [1, 128], F32)
        onc = sbuf("onc", [128, 1], F32)
        eps1_s = sbuf("eps1_s", [128, 1], F32)
        eps2_s = sbuf("eps2_s", [128, 1], F32)
        oh_s = sbuf("oh_s", [128, 127], F32)

        with ExitStack() as ph:
            sb = lambda n, sh, ty: ph.enter_context(nc.sbuf_tensor(n, sh, ty))
            pt = lambda n, sh, ty=F32: ph.enter_context(nc.psum_tensor(n, sh, ty))
            l1, l2 = sb("l1", [128, 16], F32), sb("l2", [128, 16], F32)
            l3, l4 = sb("l3", [128, 16], F32), sb("l4", [128, 16], F32)
            p0, p1, p2 = pt("p0", [128, 16]), pt("p1", [128, 16]), pt("p2", [128, 16])
            row1, row2 = sb("row1", [1, 16], F32), sb("row2", [1, 16], F32)
            lamrow = sb("lamrow", [1, 16], F32)

            loads = [(tri_s, trid), (h128b_s, h128bd), (h128f_s, h128fd),
                     (idn_s, idnd), (onr, ones_rd), (onc, ones_cd),
                     (rms_s, rmswd), (l1, lq1d), (l2, lq2d), (l3, lk1d), (l4, lk2d),
                     (eps1_s, eps1d), (eps2_s, eps2d), (oh_s, ohd)]
            steps = [Step("sync", 16 * len(loads),
                          lambda e, inc, _l=loads: [inc(e.dma_start(out=d[:, :], in_=s[:, :]), 16)
                                                    for d, s in _l])]
            steps += [
                Step("vector", 1, lambda e, inc: (
                    e.tensor_mul(l1[:, :], l1[:, :], l3[:, :]),
                    inc(e.tensor_mul(l2[:, :], l2[:, :], l4[:, :])))),
                Step("tensor", 1, lambda e, inc: (
                    e.matmul(p0[0:1, :], onc[:, :], l1[:, :], start=True, stop=True),
                    inc(e.matmul(p1[0:1, :], onc[:, :], l2[:, :], start=True, stop=True)))),
                Step("scalar", 1, lambda e, inc: (
                    e.activation(row1[:, :], p0[0:1, :], AF.Exp),
                    inc(e.activation(row2[:, :], p1[0:1, :], AF.Exp)))),
                Step("vector", 1, lambda e, inc: (
                    vsync(e, e.tensor_sub(lamrow[:, :], row1[:, :], row2[:, :])),
                    inc(e.tensor_scalar_add(lamrow[:, :], lamrow[:, :], float(LAM))))),
                Step("tensor", 1, lambda e, inc:
                     inc(e.matmul(p2[:, :], onr[:, :], lamrow[:, :], start=True, stop=True))),
                Step("scalar", 1, lambda e, inc:
                     inc(e.copy(out=lamrep[:, :], in_=p2[:, :]))),
            ]
            ch.run(steps)

        if upto < 1:
            return _finish(nc, ch)
        with ExitStack() as ph:
            sb = lambda n, sh, ty: ph.enter_context(nc.sbuf_tensor(n, sh, ty))
            pt = lambda n, sh: ph.enter_context(nc.psum_tensor(n, sh, F32))
            wst = [sb(f"wst{i}", [128, 2048], F32) for i in range(2)]
            acc = sb("acc", [128, 64], F32)
            ucol, dcol = sb("ucol", [128, 1], F32), sb("dcol", [128, 1], F32)
            rtmp = sb("rtmp", [1, 1], F32)
            pA, pC = pt("pA", [128, 1]), pt("pC", [128, 1])
            tiles = [(wupT, r, c) for r in range(16) for c in range(2)] + \
                    [(wdnT, r, 0) for r in range(32)]
            steps = []
            ld_idx = {}
    
            for i, (w, r, c) in enumerate(tiles):
                eng = "sync" if i % 2 == 0 else "gpsimd"
                prev_red = 2 * (i - 2) + 1 if i >= 2 else -1
                ld_idx[i] = len(steps)
                steps.append(Step(eng, 16,
                                  lambda e, inc, _w=w, _r=r, _c=c, _i=i:
                                  inc(e.dma_start(out=wst[_i % 2][:, :],
                                                  in_=_w[_r * 128:(_r + 1) * 128,
                                                         _c * 2048:(_c + 1) * 2048]), 16),
                                  after=[prev_red] if prev_red >= 0 else -1))
                steps.append(Step("vector", 1,
                                  lambda e, inc, _i=i:
                                  inc(e.reduce_sum(acc[:, _i:_i + 1], wst[_i % 2][:, :],
                                                   axis=AX, apply_absolute_value=True)),
                                  after=[ld_idx[i]]))
            for name, lo, hi, col, wsx, rws in [("u", 0, 32, ucol, wsu_s, rwsu),
                                                ("d", 32, 64, dcol, wsd_s, rwsd)]:
                steps += [
                    Step("vector", 1, lambda e, inc, _lo=lo, _hi=hi, _col=col:
                         inc(e.reduce_sum(_col[:, :], acc[:, _lo:_hi], axis=AX))),
                    Step("tensor", 1, lambda e, inc, _col=col:
                         inc(e.matmul(pA[0:1, :], onc[:, :], _col[:, :], start=True, stop=True))),
                    Step("vector", 1, lambda e, inc, _w=wsx:
                         inc(e.tensor_scalar(_w[:, :], pA[0:1, :], 1.0 / NW_ELEM, 1e-5,
                                             AO.mult, AO.max))),
                    Step("vector", 1, lambda e, inc, _w=wsx:
                         inc(e.reciprocal(rtmp[:, :], _w[:, :]))),
                    Step("tensor", 1, lambda e, inc:
                         inc(e.matmul(pC[:, :], onr[:, :], rtmp[:, :], start=True, stop=True))),
                    Step("scalar", 1, lambda e, inc, _r=rws:
                         inc(e.copy(out=_r[:, :], in_=pC[:, :]))),
                ]
            steps.append(Step("vector", 1, lambda e, inc: (
                vsync(e, e.tensor_mul(combo[:, :], wsu_s[:, :], wsu_s[:, :])),
                vsync(e, e.tensor_mul(combo[:, :], combo[:, :], wsd_s[:, :])),
                inc(e.tensor_scalar_mul(combo[:, :], combo[:, :], float(C0))))))
            ch.run(steps)

        if upto < 2:
            return _finish(nc, ch)
        with ExitStack() as ph:
            sb = lambda n, sh, ty: ph.enter_context(nc.sbuf_tensor(n, sh, ty))
            xt = sb("xt", [128, 2048], F32)
            dump = sb("dump", [128, 2048], F32)
            sqc = sb("sqc", [128, 1], F32)
            sumc = sb("sumc", [128, 1], F32)
            mean = sb("mean", [128, 1], F32)
            msq = sb("msq", [128, 1], F32)
            var = sb("var", [128, 1], F32)
            rstd = sb("rstd", [128, 1], F32)
            negmr = sb("negmr", [128, 1], F32)
            xnbf = sb("xnbf", [128, 2048], BF16)
            xstg = sb("xstg", [128, 2048], BF16)
            psT = [ph.enter_context(nc.psum_tensor(f"psT{i}", [128, 128], BF16)) for i in range(2)]
            for tt in range(16):
                steps = [
                    Step("sync", 16, lambda e, inc, _tt=tt:
                         inc(e.dma_start(out=xt[:, :], in_=x[_tt * 128:(_tt + 1) * 128, :]), 16),
                         after=-1),
                    Step("scalar", 1, lambda e, inc:
                         inc(e.activation(dump[:, :], xt[:, :], AF.Square, accum_out=sqc[:, :]))),
                    Step("vector", 1, lambda e, inc: (
                        vsync(e, e.reduce_sum(sumc[:, :], xt[:, :], axis=AX)),
                        vsync(e, e.tensor_scalar_mul(mean[:, :], sumc[:, :], 1.0 / 2048.0)),
                        vsync(e, e.tensor_mul(msq[:, :], mean[:, :], mean[:, :])),
                        vsync(e, e.tensor_scalar_mul(var[:, :], sqc[:, :], 1.0 / 2048.0)),
                        inc(e.tensor_sub(var[:, :], var[:, :], msq[:, :])))),
                    Step("scalar", 1, lambda e, inc:
                         inc(e.activation(rstd[:, :], var[:, :], AF.Sqrt, bias=eps1_s[:, :]))),
                    Step("vector", 1, lambda e, inc: (
                        vsync(e, e.reciprocal(rstd[:, :], rstd[:, :])),
                        vsync(e, e.tensor_mul(negmr[:, :], mean[:, :], rstd[:, :])),
                        vsync(e, e.tensor_scalar_mul(negmr[:, :], negmr[:, :], -1.0)),
                        inc(e.tensor_scalar(xnbf[:, :], xt[:, :], rstd[:, :], negmr[:, :],
                                            AO.mult, AO.add)))),
                ]
                for fp in range(8):
                    steps.append(Step("tensor", 1, lambda e, inc, _f=2 * fp: (
                        e.transpose(psT[0][:, :], xnbf[:, _f * 128:(_f + 1) * 128], idn_s[:, :]),
                        inc(e.transpose(psT[1][:, :], xnbf[:, (_f + 1) * 128:(_f + 2) * 128],
                                        idn_s[:, :])))))
                    steps.append(Step("scalar", 1, lambda e, inc, _f=2 * fp: (
                        e.copy(out=xstg[:, _f * 128:(_f + 1) * 128], in_=psT[0][:, :]),
                        inc(e.copy(out=xstg[:, (_f + 1) * 128:(_f + 2) * 128], in_=psT[1][:, :])))))
                steps.append(Step("gpsimd", 16, lambda e, inc, _tt=tt:
                                  inc(e.dma_start(
                                      out=xnT_h.rearrange("p (f t) -> p f t", f=16)[
                                          :, :, _tt * 128:(_tt + 1) * 128],
                                      in_=xstg[:, :].rearrange("p (f c) -> p f c", f=16)), 16)))
                ch.run(steps)

        if upto < 3:
            return _finish(nc, ch)
        with ExitStack() as ph:
            sb = lambda n, sh, ty: ph.enter_context(nc.sbuf_tensor(n, sh, ty))
            W = [sb(f"Wf{i}", [128, 512], F32) for i in range(24)]
            xsl = sb("xsl", [128, 512], BF16)
            mx = sb("mx", [128, 512], F32)
            mn = sb("mn", [128, 512], F32)
            ptmp = sb("ptmp", [64, 512], F32)
            Sb2 = sb("Sb2", [128, 512], F32)
            qi8 = sb("qi8", [128, 512], I8)
            qbf = sb("qbf", [128, 512], BF16)
            psH = ph.enter_context(nc.psum_tensor("psH", [128, 512], F32))
            psR = ph.enter_context(nc.psum_tensor("psR", [128, 512], F32))
            for tc in range(4):
                steps = []
                for f in range(16):
                    steps.append(Step("sync", 16, lambda e, inc, _f=f, _tc=tc:
                                      inc(e.dma_start(out=xsl[:, :],
                                                      in_=xnT_h[:, _f * T + _tc * 512:
                                                                _f * T + (_tc + 1) * 512]), 16)))
                    steps.append(Step("tensor", 1, lambda e, inc:
                                      inc(e.matmul(psH[:, :], h128b_s[:, :], xsl[:, :],
                                                   start=True, stop=True))))
                    steps.append(Step("scalar", 1, lambda e, inc, _f=f:
                                      inc(e.copy(out=W[_f][:, :], in_=psH[:, :]))))

                def butterflies(e, inc, nt=16):
                    slot = list(range(nt))
                    free = list(range(nt, nt + nt // 2))
                    nbits = int(np.log2(nt))
                    for p in range(nbits):
                        b = 1 << p
                        pairs = [(i, i | b) for i in range(nt) if not (i & b)]
                        last = None
                        for idx, (i, j) in enumerate(pairs):
                            last = e.tensor_sub(W[free[idx]][:, :], W[slot[i]][:, :],
                                                W[slot[j]][:, :])
                        vsync(e, last)
                        last = None
                        for i, j in pairs:
                            last = e.tensor_add(W[slot[i]][:, :], W[slot[i]][:, :],
                                                W[slot[j]][:, :])
                        vsync(e, last)
                        for idx, (i, j) in enumerate(pairs):
                            slot[j], free[idx] = free[idx], slot[j]
                    for op, dstbuf, neg in ((AO.max, mx, False), (AO.min, mn, True)):
                        last = None
                        for t in range(nt // 2):
                            last = e.tensor_tensor(W[free[t]][:, :], W[slot[2 * t]][:, :],
                                                   W[slot[2 * t + 1]][:, :], op)
                        vsync(e, last)
                        gap = 1
                        while gap < nt // 2:
                            last = None
                            for t in range(0, nt // 2, 2 * gap):
                                last = e.tensor_tensor(W[free[t]][:, :], W[free[t]][:, :],
                                                       W[free[t + gap]][:, :], op)
                            vsync(e, last)
                            gap *= 2
                        if neg:
                            vsync(e, e.tensor_scalar_mul(dstbuf[:, :], W[free[0]][:, :], -1.0))
                        else:
                            vsync(e, e.tensor_copy(dstbuf[:, :], W[free[0]][:, :]))
                    inc(e.tensor_max(mx[:, :], mx[:, :], mn[:, :]))
                    return slot
                slot_store = {}
                steps.append(Step("vector", 1,
                                  lambda e, inc: slot_store.__setitem__("s", butterflies(e, inc))))
                for r in range(7):
                    p = 64 >> r
                    steps.append(Step("sync", 16, lambda e, inc, _p=p:
                                      inc(e.dma_start(out=ptmp[0:_p, :], in_=mx[_p:2 * _p, :]), 16)))
                    steps.append(Step("vector", 1, lambda e, inc, _p=p:
                                      inc(e.tensor_max(mx[0:_p, :], mx[0:_p, :], ptmp[0:_p, :]))))
                steps.append(Step("vector", 1, lambda e, inc, _tc=tc:
                                  inc(e.tensor_scalar_max(mrow1[0:1, _tc * 512:(_tc + 1) * 512],
                                                          mx[0:1, :], 1e-5))))
                steps.append(Step("tensor", 1, lambda e, inc, _tc=tc:
                                  inc(e.matmul(psR[:, :], onr[:, :],
                                               mrow1[0:1, _tc * 512:(_tc + 1) * 512],
                                               start=True, stop=True))))
                steps.append(Step("vector", 1, lambda e, inc: (
                    vsync(e, e.tensor_scalar_mul(Sb2[:, :], psR[:, :], 1.0 / 127.0)),
                    inc(e.reciprocal(Sb2[:, :], Sb2[:, :])))))
                for f in range(16):
                    steps.append(Step("vector", 1, lambda e, inc, _f=f: (
                        vsync(e, e.tensor_mul(qi8[:, :], W[slot_store["s"][_f]][:, :], Sb2[:, :])),
                        inc(e.tensor_copy(qbf[:, :], qi8[:, :])))))
                    steps.append(Step("gpsimd", 16, lambda e, inc, _f=f, _tc=tc:
                                      inc(e.dma_start(out=xq_h[:, _f * T + _tc * 512:
                                                               _f * T + (_tc + 1) * 512],
                                                      in_=qbf[:, :]), 16)))
                ch.run(steps)

        if upto < 4:
            return _finish(nc, ch)
        with ExitStack() as ph:
            sb = lambda n, sh, ty: ph.enter_context(nc.sbuf_tensor(n, sh, ty))
            ctab = [sb("ctab0", [128, T], F32), sb("ctab1", [128, T], F32)]
            stab = [sb("stab0", [128, T], F32), sb("stab1", [128, T], F32)]
            wqst = [sb(f"wqst{i}", [128, 8 * 2048], BF16) for i in range(2)]
            wkst = [sb(f"wkst{i}", [128, 2 * 2048], BF16) for i in range(2)]
            wvst = [sb(f"wvst{i}", [128, 2048], BF16) for i in range(2)]
            xns = [sb(f"xns{i}", [128, 16 * 512], BF16) for i in range(2)]
            Xc = [sb(f"Xc{i}", [128, 512], F32) for i in range(2)]
            Xs = [sb(f"Xs{i}", [128, 512], F32) for i in range(2)]
            m1b = [sb(f"m1b{i}", [128, 512], F32) for i in range(2)]
            qkst = [sb(f"qkst{i}", [128, 512], BF16) for i in range(2)]
            vbfb = sb("vbfb", [128, 512], BF16)
            vstg = [sb(f"vstg{i}", [128, 128], BF16) for i in range(2)]
            psP = [ph.enter_context(nc.psum_tensor(f"psP{i}", [128, 512], F32)) for i in range(2)]
            psV = [ph.enter_context(nc.psum_tensor(f"psV{i}", [128, 128], BF16)) for i in range(2)]

            ch.run([Step("sync", 16 * 4, lambda e, inc: [
                inc(e.dma_start(out=ctab[0][:, :], in_=c_lod[:, :]), 16),
                inc(e.dma_start(out=stab[0][:, :], in_=s_lod[:, :]), 16),
                inc(e.dma_start(out=ctab[1][:, :], in_=c_hid[:, :]), 16),
                inc(e.dma_start(out=stab[1][:, :], in_=s_hid[:, :]), 16)])])

            steps = []

            def add(stp):
                steps.append(stp)
                return len(steps) - 1

            i_wl_prev = {0: -1, 1: -1}
            i_xl_prev = {0: -1, 1: -1}
            i_ev_bank = {0: -1, 1: -1}
            i_vv_bank = {0: -1, 1: -1}
            i_rope_buf = {0: -1, 1: -1}
            i_vstg_buf = {0: -1, 1: -1}
            i_vbf = [-1]
            ot_ctr = [0]
            v_ctr = [0]
            for g in range(KVH):
                gpar = g % 2

                def wload(e, inc, _g=g, _p=gpar):
                    for c2 in range(2):
                        for ec in range(16):
                            inc(e.dma_start(
                                out=wkst[_p][:, c2 * 2048 + ec * 128: c2 * 2048 + (ec + 1) * 128],
                                in_=wk[ec * 128:(ec + 1) * 128,
                                       _g * 256 + c2 * 128: _g * 256 + (c2 + 1) * 128]), 16)
                    for ec in range(16):
                        inc(e.dma_start(out=wvst[_p][:, ec * 128:(ec + 1) * 128],
                                        in_=wv[ec * 128:(ec + 1) * 128,
                                               _g * 128:(_g + 1) * 128]), 16)
                    for hl in range(4):
                        for c2 in range(2):
                            for ec in range(16):
                                col = (4 * _g + hl) * 256 + c2 * 128
                                inc(e.dma_start(
                                    out=wqst[_p][:, (hl * 2 + c2) * 2048 + ec * 128:
                                                 (hl * 2 + c2) * 2048 + (ec + 1) * 128],
                                    in_=wq[ec * 128:(ec + 1) * 128, col:col + 128]), 16)
                i_wl = add(Step("gpsimd", 16 * 176, wload, after=[i_wl_prev[gpar]]))

                for tc in range(4):
                    xpar = (g * 4 + tc) % 2

                    def xload(e, inc, _tc=tc, _p=xpar):
                        for f in range(16):
                            inc(e.dma_start(out=xns[_p][:, f * 512:(f + 1) * 512],
                                            in_=xnT_h[:, f * T + _tc * 512:
                                                      f * T + (_tc + 1) * 512]), 16)
                    i_xl = add(Step("sync", 16 * 16, xload, after=[i_xl_prev[xpar]]))

                    ots = [("k", c2, None) for c2 in range(2)] + [("v", None, None)] + \
                          [("q", hl, c2) for hl in range(4) for c2 in range(2)]
                    for kind, a1, a2 in ots:
                        bank = ot_ctr[0] % 2
                        ot_ctr[0] += 1
                        if kind == "v":
                            def mmv(e, inc, _p=gpar, _xp=xpar):
                                for ec in range(16):
                                    mm = e.matmul(psP[0][:, :] if False else psP[_b][:, :],
                                                  wvst[_p][:, ec * 128:(ec + 1) * 128],
                                                  xns[_xp][:, ec * 512:(ec + 1) * 512],
                                                  start=(ec == 0), stop=(ec == 15))
                                inc(mm)
                            def mmv2(e, inc, _p=gpar, _xp=xpar, _b=bank):
                                for ec in range(16):
                                    mm = e.matmul(psP[_b][:, :],
                                                  wvst[_p][:, ec * 128:(ec + 1) * 128],
                                                  xns[_xp][:, ec * 512:(ec + 1) * 512],
                                                  start=(ec == 0), stop=(ec == 15))
                                inc(mm)
                            i_mm = add(Step("tensor", 1, mmv2,
                                            after=[i_wl, i_xl, i_ev_bank[bank]]))
                            i_ev = add(Step("scalar", 1, lambda e, inc, _b=bank:
                                            inc(e.copy(out=vbfb[:, :], in_=psP[_b][:, :])),
                                            after=[i_mm, i_vbf[0]]))
                            i_ev_bank[bank] = i_ev
                            prev_tp = i_ev
                            for i in range(4):
                                vb = v_ctr[0] % 2
                                v_ctr[0] += 1
                                i_t = add(Step("tensor", 1, lambda e, inc, _i=i, _vb=vb:
                                               inc(e.transpose(psV[_vb][:, :],
                                                               vbfb[:, _i * 128:(_i + 1) * 128],
                                                               idn_s[:, :])),
                                               after=[prev_tp, i_vv_bank[vb]]))
                                i_e2 = add(Step("scalar", 1, lambda e, inc, _vb=vb:
                                               inc(e.copy(out=vstg[_vb][:, :], in_=psV[_vb][:, :])),
                                               after=[i_t, i_vstg_buf[vb]]))
                                i_vv_bank[vb] = i_e2
                                i_d = add(Step("gpsimd", 16, lambda e, inc, _g=g, _tc=tc, _i=i, _vb=vb:
                                               inc(e.dma_start(
                                                   out=v_h[:, _g * T + (_tc * 4 + _i) * 128:
                                                           _g * T + (_tc * 4 + _i + 1) * 128],
                                                   in_=vstg[_vb][:, :]), 16),
                                               after=[i_e2]))
                                i_vstg_buf[vb] = i_d
                            i_vbf[0] = i_t
                        else:
                            if kind == "k":
                                wsrc, wcol = wkst[gpar], a1 * 2048
                                comp = a1
                                dsth, dcol = kT_h, (g * 2 + a1) * T + tc * 512
                            else:
                                wsrc, wcol = wqst[gpar], (a1 * 2 + a2) * 2048
                                comp = a2
                                dsth, dcol = qT_h, ((4 * g + a1) * 2 + a2) * T + tc * 512
                            rpar = ot_ctr[0] % 2

                            def mmp(e, inc, _w=wsrc, _wc=wcol, _xp=xpar, _b=bank):
                                for ec in range(16):
                                    mm = e.matmul(psP[_b][:, :],
                                                  _w[:, _wc + ec * 128: _wc + (ec + 1) * 128],
                                                  xns[_xp][:, ec * 512:(ec + 1) * 512],
                                                  start=(ec == 0), stop=(ec == 15))
                                inc(mm)
                            i_mm = add(Step("tensor", 1, mmp,
                                            after=[i_wl, i_xl, i_ev_bank[bank]]))
                            i_ev = add(Step("scalar", 1, lambda e, inc, _b=bank, _r=rpar: (
                                e.copy(out=Xc[_r][:, :], in_=psP[_b][:, :]),
                                e.copy(out=Xs[_r][0:64, :], in_=psP[_b][64:128, :]),
                                inc(e.copy(out=Xs[_r][64:128, :], in_=psP[_b][0:64, :]))),
                                after=[i_mm, i_rope_buf[rpar]]))
                            i_ev_bank[bank] = i_ev
                            i_ro = add(Step("vector", 1, lambda e, inc, _c=comp, _tc=tc, _r=rpar: (
                                e.tensor_mul(m1b[_r][:, :], Xc[_r][:, :],
                                             ctab[_c][:, _tc * 512:(_tc + 1) * 512]),
                                vsync(e, e.tensor_mul(Xs[_r][:, :], Xs[_r][:, :],
                                                      stab[_c][:, _tc * 512:(_tc + 1) * 512])),
                                inc(e.tensor_add(qkst[_r][:, :], m1b[_r][:, :], Xs[_r][:, :]))),
                                after=[i_ev]))
                            i_d = add(Step("gpsimd", 16, lambda e, inc, _d=dsth, _dc=dcol, _r=rpar:
                                           inc(e.dma_start(out=_d[:, _dc:_dc + 512],
                                                           in_=qkst[_r][:, :]), 16),
                                           after=[i_ro]))
                            i_rope_buf[rpar] = i_d
                    i_xl_prev[xpar] = i_mm if True else i_xl
                i_wl_prev[gpar] = i_mm
            ch.run(steps)

        with ExitStack() as ph:
            sb = lambda n, sh, ty, **kw: ph.enter_context(nc.sbuf_tensor(n, sh, ty, **kw))
            attT = sb("attT", [128, H * T], BF16)
            ph4i = ph.enter_context(ExitStack())
            sb4 = sb
            sb = lambda n, sh, ty, **kw: ph4i.enter_context(nc.sbuf_tensor(n, sh, ty, **kw))
            kg = [sb(f"kg{i}", [128, 2 * T], BF16) for i in range(2)]
            vg = [sb(f"vg{i}", [128, T], BF16) for i in range(2)]
            qh = [sb(f"qh{i}", [128, 2 * T], BF16) for i in range(2)]
            e0b = sb("e0b", [128, 2 * T], BF16)
            e1b = sb("e1b", [128, 2 * T], BF16)
            Pf = sb("Pf", [128, T], F32)
            Pb = sb("Pb", [128, 4 * T], BF16)
            PT = sb("PT", [128, 16 * 512], BF16, is_transpose_dest=True)
            A0 = sb("A0", [128, 8], F32)
            A1 = sb("A1", [128, 8], F32)
            Zc = sb("Zc", [128, 8], F32)
            S0 = sb("S0", [128, 2], F32)
            S1 = sb("S1", [128, 2], F32)
            SZ = sb("SZ", [128, 2], F32)
            r0 = sb("r0", [128, 2], F32)
            lr1 = sb("lr1", [128, 2], F32)
            r2 = sb("r2", [128, 2], F32)
            sqb = sb("sqb", [128, 512], F32)
            msb = sb("msb", [128, 512], F32)
            msrow = sb("msrow", [1, 512], F32)
            taub = sb("taub", [128, 512], F32)
            taurow = sb("taurow", [1, 2 * 512], F32)
            psE0 = [ph.enter_context(nc.psum_tensor(f"psE0{i}", [128, 512], F32)) for i in range(2)]
            psE0.append(ph4i.enter_context(nc.psum_tensor("psE02", [128, 512], F32)))
            psE1 = [ph4i.enter_context(nc.psum_tensor(f"psE1{i}", [128, 512], F32)) for i in range(3)]
            psAt = ph4i.enter_context(nc.psum_tensor("psAt", [128, 512], F32))
            psRo = ph4i.enter_context(nc.psum_tensor("psRo", [128, 512], F32))
            psRp = psRo

            qc_glob = [0]
            pair_glob = [0]
            tptb_ctr = [0]
            tp_evac_b = {0: -1, 1: -1}
            steps = []
            idx_exp_bank = {0: -1, 1: -1, 2: -1}
            idx_attmm_prev = -1
            idx_attts_prev = -1
            idx_sq_prev = -1
            idx_rowmm_prev = -1
            idx_mscp_prev = -1
            idx_msdma_prev = -1
            idx_ptp_prev = {0: [], 1: []}
            i_kvload_p = {0: -1, 1: -1}
            i_qload_p = {0: -1, 1: -1}
            kv_used = {0: [], 1: []}
            q_used = {0: [], 1: []}

            import os
            _SER = os.environ.get("PH4_SER", "")
            _PIPE = os.environ.get("PH4_PIPE")

            def add(stp, tag=""):
                if os.environ.get("PH4_SERIAL") or (tag and tag in _SER):
                    stp.after = None
                if _PIPE is not None and (not tag or tag not in _PIPE):
                    stp.after = None
                steps.append(stp)
                return len(steps) - 1

            for g in range(KVH):
                gp_ = g % 2
                for hl in range(4):
                    h = 4 * g + hl
                    hp_ = h % 2
                    last_pn = {0: -1, 1: -1}
                    last_tp = {0: -1, 1: -1}

                    if hl == 0:
                        i_kvl = add(Step("sync", 48, lambda e, inc, _g=g, _p=gp_: [
                            inc(e.dma_start(out=kg[_p][:, 0:T],
                                            in_=kT_h[:, (2 * _g) * T:(2 * _g + 1) * T]), 16),
                            inc(e.dma_start(out=kg[_p][:, T:2 * T],
                                            in_=kT_h[:, (2 * _g + 1) * T:(2 * _g + 2) * T]), 16),
                            inc(e.dma_start(out=vg[_p][:, :],
                                            in_=v_h[:, _g * T:(_g + 1) * T]), 16)],
                            after=kv_used[gp_] if kv_used[gp_] else -1))
                        i_kvload_p[gp_] = i_kvl
                        kv_used[gp_] = []
                    i_ql = add(Step("sync", 32, lambda e, inc, _h=h, _p=hp_: [
                        inc(e.dma_start(out=qh[_p][:, 0:T],
                                        in_=qT_h[:, (2 * _h) * T:(2 * _h + 1) * T]), 16),
                        inc(e.dma_start(out=qh[_p][:, T:2 * T],
                                        in_=qT_h[:, (2 * _h + 1) * T:(2 * _h + 2) * T]), 16)],
                        after=q_used[hp_] if q_used[hp_] else -1))
                    i_qload_p[hp_] = i_ql
                    q_used[hp_] = []
                    i_qload = i_ql
                    i_kvload = i_kvload_p[gp_]

                    for qc in range(4):
                        ptpar = 0
                        qc_glob[0] += 1
                        PTof = 0
                        idx_ptp_cur = []
                        idx_expP_last = -1
                        for qs in range(4):
                            qt = 4 * qc + qs
                            par = qt % 2
                            nch = qt // 4 + 1
                            idx_exp_last = -1
                            for kc in range(nch):
                                cw = 512 if kc < qt // 4 else (qt % 4 + 1) * 128
                                isdiag = (kc == qt // 4)
                                bank = pair_glob[0] % 3
                                pair_glob[0] += 1

                                def mme(e, inc, _qt=qt, _kc=kc, _cw=cw, _b=bank,
                                        _gp=gp_, _hp=hp_):
                                    e.matmul(psE0[_b][:, 0:_cw],
                                             qh[_hp][:, _qt * 128:(_qt + 1) * 128],
                                             kg[_gp][:, _kc * 512:_kc * 512 + _cw],
                                             start=True, stop=True)
                                    inc(e.matmul(psE1[_b][:, 0:_cw],
                                                 qh[_hp][:, T + _qt * 128: T + (_qt + 1) * 128],
                                                 kg[_gp][:, T + _kc * 512: T + _kc * 512 + _cw],
                                                 start=True, stop=True))
                                i_mm = add(Step("tensor", 1, mme,
                                                after=[i_qload, i_kvload,
                                                       idx_exp_bank[bank]]), tag="E")
                                kv_used[gp_].append(i_mm)
                                q_used[hp_].append(i_mm)
                                i_pre = i_mm
                                if isdiag:
                                    def trif(e, inc, _cw=cw, _b=bank):
                                        e.tensor_add(psE0[_b][:, _cw - 128:_cw],
                                                     psE0[_b][:, _cw - 128:_cw], tri_s[:, :])
                                        inc(e.tensor_add(psE1[_b][:, _cw - 128:_cw],
                                                         psE1[_b][:, _cw - 128:_cw], tri_s[:, :]))
                                    i_pre = add(Step("vector", 1, trif, after=[i_mm]), tag="E")

                                def expf(e, inc, _kc=kc, _cw=cw, _b=bank, _p=par):
                                    e.activation(e0b[:, _p * T + _kc * 512: _p * T + _kc * 512 + _cw],
                                                 psE0[_b][:, 0:_cw], AF.Exp, scale=ISQ,
                                                 accum_out=A0[:, _p * 4 + _kc:_p * 4 + _kc + 1])
                                    inc(e.activation(e1b[:, _p * T + _kc * 512: _p * T + _kc * 512 + _cw],
                                                     psE1[_b][:, 0:_cw], AF.Exp, scale=ISQ,
                                                     accum_out=A1[:, _p * 4 + _kc:_p * 4 + _kc + 1]))
                                i_exp = add(Step("scalar", 1, expf,
                                                             after=[i_pre, last_pn[par]]), tag="E")
                                idx_exp_bank[bank] = i_exp
                                idx_exp_last = i_exp

                            def rstep(e, inc, _n=nch, _h=h, _p=par):
                                e.reduce_sum(S0[:, _p:_p + 1], A0[:, _p * 4:_p * 4 + _n], axis=AX)
                                vsync(e, e.reduce_sum(S1[:, _p:_p + 1], A1[:, _p * 4:_p * 4 + _n],
                                                      axis=AX))
                                e.reciprocal(r0[:, _p:_p + 1], S0[:, _p:_p + 1])
                                vsync(e, e.reciprocal(lr1[:, _p:_p + 1], S1[:, _p:_p + 1]))
                                inc(e.tensor_mul(lr1[:, _p:_p + 1], lr1[:, _p:_p + 1],
                                                 lamrep[:, _h:_h + 1]))
                            i_r = add(Step("vector", 1, rstep,
                                           after=[idx_exp_last, last_pn[par]]), tag="R")

                            kspan = (qt + 1) * 128

                            def dif(e, inc, _ks=kspan, _qt=qt, _p=par):
                                sl = slice(_p * T, _p * T + _ks)
                                e.tensor_scalar(e1b[:, sl], e1b[:, sl], lr1[:, _p:_p + 1],
                                                None, AO.mult)
                                vsync(e, e.tensor_scalar(e0b[:, sl], e0b[:, sl],
                                                         r0[:, _p:_p + 1], None, AO.mult))
                                vsync(e, e.tensor_sub(e0b[:, sl], e0b[:, sl], e1b[:, sl]))
                                dsl = slice(_p * T + _qt * 128, _p * T + (_qt + 1) * 128)
                                inc(e.tensor_add(e0b[:, dsl], e0b[:, dsl], tri_s[:, :]))
                            i_dif = add(Step("vector", 1, dif, after=[i_r]), tag="D")

                            def expP(e, inc, _ks=kspan, _p=par):
                                inc(e.activation(Pf[:, 0:_ks], e0b[:, _p * T: _p * T + _ks],
                                                 AF.Exp,
                                                 accum_out=SZ[:, _p:_p + 1]))
                            idx_expP_last = add(Step("scalar", 1, expP,
                                                     after=[i_dif, last_pn[0], last_pn[1]]),
                                                tag="D")

                            def pnorm(e, inc, _qt=qt, _p=par, _qs=qs):
                                vsync(e, e.reciprocal(r2[:, _p:_p + 1], SZ[:, _p:_p + 1]))
                                inc(e.tensor_scalar(Pb[:, _qs * T: _qs * T + (_qt + 1) * 128],
                                                    Pf[:, 0:(_qt + 1) * 128],
                                                    r2[:, _p:_p + 1], None, AO.mult))
                            i_pn = add(Step("vector", 1, pnorm,
                                            after=[idx_expP_last, idx_attmm_prev]), tag="P")
                            last_pn[par] = i_pn

                            n_even = qt + 1
                            n_odd = 0

                            def ptp_e(e, inc, _qt=qt, _qs=qs, _of=PTof):
                                for kt in range(0, _qt + 1, 1):
                                    inc(e.dma_start_transpose(
                                        PT[:, _of + kt * 512 + _qs * 128: _of + kt * 512 + (_qs + 1) * 128],
                                        Pb[:, _qs * T + kt * 128: _qs * T + (kt + 1) * 128]), 16)
                            i_tp = add(Step("sync", 16 * n_even, ptp_e,
                                            after=[i_pn] + idx_ptp_prev[ptpar]), tag="T")
                            idx_ptp_cur.append(i_tp)
                            last_tp[par] = i_tp
                        zblocks = [(kt, qs) for kt in range(4 * qc + 4) for qs in range(4)
                                   if 4 * qc + qs < kt]

                        def zset(e, inc, _zb=zblocks, _of=PTof):
                            last = None
                            for kt, qs in _zb:
                                last = e.memset(
                                    PT[:, _of + kt * 512 + qs * 128: _of + kt * 512 + (qs + 1) * 128], 0.0)
                            if last is None:
                                last = e.tensor_copy(S0[:, :], S0[:, :])
                            inc(last)
                        i_z = add(Step("vector", 1, zset,
                                       after=[steps.__len__() - 1] + idx_ptp_prev[ptpar]), tag="Z")

                        def attmm(e, inc, _qc=qc, _of=PTof, _gp=gp_):
                            for kt in range(4 * _qc + 4):
                                mm = e.matmul(psAt[:, :], vg[_gp][:, kt * 128:(kt + 1) * 128],
                                              PT[:, _of + kt * 512: _of + (kt + 1) * 512],
                                              start=(kt == 0), stop=(kt == 4 * _qc + 3))
                            inc(mm)
                        i_am = add(Step("tensor", 1, attmm,
                                        after=idx_ptp_cur + [i_z, idx_attts_prev, idx_sq_prev]),
                                   tag="A")
                        idx_ptp_prev[ptpar] = [i_am]

                        i_sq = add(Step("scalar", 1, lambda e, inc:
                                        inc(e.activation(sqb[:, :], psAt[:, :], AF.Square)),
                                        after=[i_am, idx_rowmm_prev]), tag="A")
                        i_row = add(Step("tensor", 1, lambda e, inc:
                                         inc(e.matmul(psRo[0:1, :], onc[:, :], sqb[:, :],
                                                      start=True, stop=True)),
                                         after=[i_sq, idx_mscp_prev]), tag="Q")
                        i_mscp = add(Step("scalar", 1, lambda e, inc:
                                          inc(e.copy(out=msrow[0:1, :], in_=psRo[0:1, :])),
                                          after=[i_row, idx_msdma_prev]), tag="Q")
                        i_msdma = add(Step("sync", 16, lambda e, inc, _h=h, _qc=qc:
                                           inc(e.dma_start(
                                               out=msb[_h * 4 + _qc:_h * 4 + _qc + 1, :],
                                               in_=msrow[0:1, :]), 16),
                                           after=[i_mscp]), tag="Q")
                        i_ts = add(Step("vector", 1, lambda e, inc, _h=h, _qc=qc:
                                        inc(e.tensor_scalar(
                                            attT[:, _h * T + _qc * 512: _h * T + (_qc + 1) * 512],
                                            psAt[:, :], rms_s[:, :], None, AO.mult)),
                                        after=[i_am, i_sq]), tag="S")
                        idx_attmm_prev = i_am
                        idx_attts_prev = i_ts
                        idx_sq_prev = i_sq
                        idx_rowmm_prev = i_row
                        idx_mscp_prev = i_mscp
                        idx_msdma_prev = i_msdma
                        kv_used[gp_].append(i_am)
            ch.run(steps)

            steps = [
                Step("scalar", 1, lambda e, inc:
                     inc(e.activation(taub[0:64, :], msb[0:64, :], AF.Sqrt,
                                      scale=1.0 / (128.0 * CR * CR), bias=eps2_s[0:64, :]))),
                Step("vector", 1, lambda e, inc:
                     inc(e.reciprocal(taub[0:64, :], taub[0:64, :]))),
            ]
            for sb8 in range(32):
                steps.append(Step("sync", 16, lambda e, inc, _s=sb8:
                                  inc(e.dma_start(
                                      out=taurow[0:1, :].rearrange("p (r c) -> p r c", r=2),
                                      in_=taub[2 * _s:2 * _s + 2, :]), 16)))
                for rr in range(2):
                    r = sb8 * 2 + rr
                    steps.append(Step("tensor", 1, lambda e, inc, _rr=rr:
                                      inc(e.matmul(psRp[:, :], onr[:, :],
                                                   taurow[0:1, _rr * 512:(_rr + 1) * 512],
                                                   start=True, stop=True))))
                    h, qc = r // 4, r % 4
                    steps.append(Step("vector", 1, lambda e, inc, _h=h, _qc=qc:
                                      inc(e.tensor_mul(
                                          attT[:, _h * T + _qc * 512: _h * T + (_qc + 1) * 512],
                                          attT[:, _h * T + _qc * 512: _h * T + (_qc + 1) * 512],
                                          psRp[:, :]))))
            ch.run(steps)
            if att_t is not None:
                ch.run([Step("gpsimd", 16, lambda e, inc:
                             inc(e.dma_start(out=att_t[:, :], in_=attT[:, :]), 16))])

            if upto < 6:
                return _finish(nc, ch)
            ph4i.close()
            sb = sb4
            wost = [sb("wost0", [128, 2048], BF16), sb("wost1", [128, 2048], BF16)]
            sstg = [sb("sstg0", [128, 512], BF16), sb("sstg1", [128, 512], BF16)]
            psS = [psE0[0], psE0[1]]
            steps = []

            def add5(stp):
                steps.append(stp)
                return len(steps) - 1

            i_wl_p = {0: -1, 1: -1}
            i_ev_p = {0: -1, 1: -1}
            i_st_p = {0: -1, 1: -1}
            mm_ctr = [0]
            i_mm_last_p = {0: -1, 1: -1}
            for ot in range(16):
                wpar = ot % 2

                def wl5(e, inc, _ot=ot, _p=wpar):
                    for ec in range(16):
                        inc(e.dma_start(out=wost[_p][:, ec * 128:(ec + 1) * 128],
                                        in_=wo[ec * 128:(ec + 1) * 128,
                                               _ot * 128:(_ot + 1) * 128]), 16)
                i_wl = add5(Step("gpsimd", 16 * 16, wl5, after=[i_mm_last_p[wpar]]))
                for tc in range(4):
                    b = mm_ctr[0] % 2
                    mm_ctr[0] += 1

                    def mmo(e, inc, _tc=tc, _p=wpar, _b=b):
                        for ec in range(16):
                            mm = e.matmul(psS[_b][:, :], wost[_p][:, ec * 128:(ec + 1) * 128],
                                          attT[:, ec * T + _tc * 512: ec * T + (_tc + 1) * 512],
                                          start=(ec == 0), stop=(ec == 15))
                        inc(mm)
                    i_mm = add5(Step("tensor", 1, mmo, after=[i_wl, i_ev_p[b]]))
                    i_ev = add5(Step("scalar", 1, lambda e, inc, _b=b, _sp=b:
                                     inc(e.copy(out=sstg[_sp][:, :], in_=psS[_b][:, :])),
                                     after=[i_mm, i_st_p[b]]))
                    i_ev_p[b] = i_ev
                    i_st = add5(Step("gpsimd", 16, lambda e, inc, _ot=ot, _tc=tc, _sp=b:
                                     inc(e.dma_start(out=s_hbm[:, _ot * T + _tc * 512:
                                                              _ot * T + (_tc + 1) * 512],
                                                     in_=sstg[_sp][:, :]), 16),
                                     after=[i_ev]))
                    i_st_p[b] = i_st
                i_mm_last_p[wpar] = i_mm
            ch.run(steps)

        with ExitStack() as ph:
            sb = lambda n, sh, ty: ph.enter_context(nc.sbuf_tensor(n, sh, ty))
            xqs = sb("xqs", [128, NE * T], BF16)
            wstg = [sb(f"wstg{i}", [128, 2048], F32) for i in range(2)]
            i8sl = [sb(f"i8sl{i}", [128, 2048], I8) for i in range(2)]
            tsl = [sb(f"tsl{i}", [128, 2048], BF16) for i in range(2)]
            rlb = [sb(f"rlb{i}", [128, 512], F32) for i in range(2)]
            h2b = [sb(f"h2b{i}", [128, 512], F32) for i in range(2)]
            psU = [ph.enter_context(nc.psum_tensor(f"psU{i}", [128, 512], F32)) for i in range(2)]
            ch.run([Step("sync", 16 * 16, lambda e, inc: [
                inc(e.dma_start(out=xqs[:, f * T:(f + 1) * T], in_=xq_h[:, f * T:(f + 1) * T]), 16)
                for f in range(16)])])
            steps = []

            def add6(stp):
                steps.append(stp)
                return len(steps) - 1

            i_i8_p = {0: -1, 1: -1}
            i_cl_p = {0: -1, 1: -1}
            i_mml_p = {0: -1, 1: -1}
            i_tt_b = {0: -1, 1: -1}
            i_dma_b = {0: -1, 1: -1}
            mm_ctr = [0]
            for o in range(NF2):
                p = o % 2

                def wl6(e, inc, _o=o, _p=p):
                    for ec in range(16):
                        inc(e.dma_start(out=wstg[_p][:, ec * 128:(ec + 1) * 128],
                                        in_=wupT[ec * 128:(ec + 1) * 128,
                                                 _o * 128:(_o + 1) * 128]), 16)
                i_wl = add6(Step("sync", 16 * 16, wl6, after=[i_i8_p[p]]))
                i_i8 = add6(Step("scalar", 1, lambda e, inc, _p=p:
                                 inc(e.activation(i8sl[_p][:, :], wstg[_p][:, :], AF.Copy,
                                                  scale=rwsu[:, :])),
                                 after=[i_wl, i_cl_p[p]]))
                i_i8_p[p] = i_i8
                i_cl = add6(Step("vector", 1, lambda e, inc, _p=p:
                                 inc(e.tensor_scalar(tsl[_p][:, :], i8sl[_p][:, :], 1.0, -1.0,
                                                     AO.min, AO.max)),
                                 after=[i_i8, i_mml_p[p]]))
                i_cl_p[p] = i_cl
                for tc in range(4):
                    b = mm_ctr[0] % 2
                    mm_ctr[0] += 1

                    def mmu(e, inc, _tc=tc, _p=p, _b=b):
                        for ec in range(16):
                            mm = e.matmul(psU[_b][:, :], tsl[_p][:, ec * 128:(ec + 1) * 128],
                                          xqs[:, ec * T + _tc * 512: ec * T + (_tc + 1) * 512],
                                          start=(ec == 0), stop=(ec == 15))
                        inc(mm)
                    i_mm = add6(Step("tensor", 1, mmu, after=[i_cl, i_tt_b[b]]))
                    i_rl = add6(Step("scalar", 1, lambda e, inc, _b=b:
                                     inc(e.activation(rlb[_b][:, :], psU[_b][:, :], AF.Relu)),
                                     after=[i_mm]))
                    i_tt = add6(Step("vector", 1, lambda e, inc, _b=b:
                                     inc(e.tensor_mul(h2b[_b][:, :], rlb[_b][:, :], psU[_b][:, :])),
                                     after=[i_rl, i_dma_b[b]]))
                    i_tt_b[b] = i_tt
                    i_dma = add6(Step("gpsimd", 16, lambda e, inc, _o=o, _tc=tc, _b=b:
                                      inc(e.dma_start(out=h2_h[:, _o * T + _tc * 512:
                                                               _o * T + (_tc + 1) * 512],
                                                      in_=h2b[_b][:, :]), 16),
                                      after=[i_tt]))
                    i_dma_b[b] = i_dma
                i_mml_p[p] = i_mm
            ch.run(steps)

        with ExitStack() as ph:
            sb = lambda n, sh, ty: ph.enter_context(nc.sbuf_tensor(n, sh, ty))
            K2 = sb("K2", [128, NF2 * T], BF16)
            ph7 = ph.enter_context(ExitStack())
            sb7 = lambda n, sh, ty: ph7.enter_context(nc.sbuf_tensor(n, sh, ty))
            W = [sb7(f"W2f{i}", [128, 256], F32) for i in range(48)]
            mx = sb7("mx2", [128, 256], F32)
            mn = sb7("mn2", [128, 256], F32)
            ptmp = sb7("ptmp2", [64, 256], F32)
            Sb2 = sb7("Sb22", [128, 256], F32)
            qi8 = sb7("qi82", [128, 256], I8)
            mrow2 = sb7("mrow2", [1, 256], F32)
            psH = ph7.enter_context(nc.psum_tensor("psH2", [128, 256], F32))
            psR = ph7.enter_context(nc.psum_tensor("psR2", [128, 256], F32))


            def butterflies2(e, inc, nt=32):
                slot = list(range(nt))
                free = list(range(nt, nt + nt // 2))
                for p in range(5):
                    b = 1 << p
                    pairs = [(i, i | b) for i in range(nt) if not (i & b)]
                    last = None
                    for idx, (i, j) in enumerate(pairs):
                        last = e.tensor_sub(W[free[idx]][:, :], W[slot[i]][:, :],
                                            W[slot[j]][:, :])
                    vsync(e, last)
                    last = None
                    for i, j in pairs:
                        last = e.tensor_add(W[slot[i]][:, :], W[slot[i]][:, :],
                                            W[slot[j]][:, :])
                    vsync(e, last)
                    for idx, (i, j) in enumerate(pairs):
                        slot[j], free[idx] = free[idx], slot[j]
                for op, dstbuf, neg in ((AO.max, mx, False), (AO.min, mn, True)):
                    last = None
                    for t in range(nt // 2):
                        last = e.tensor_tensor(W[free[t]][:, :], W[slot[2 * t]][:, :],
                                               W[slot[2 * t + 1]][:, :], op)
                    vsync(e, last)
                    gap = 1
                    while gap < nt // 2:
                        last = None
                        for t in range(0, nt // 2, 2 * gap):
                            last = e.tensor_tensor(W[free[t]][:, :], W[free[t]][:, :],
                                                   W[free[t + gap]][:, :], op)
                        vsync(e, last)
                        gap *= 2
                    if neg:
                        vsync(e, e.tensor_scalar_mul(dstbuf[:, :], W[free[0]][:, :], -1.0))
                    else:
                        vsync(e, e.tensor_copy(dstbuf[:, :], W[free[0]][:, :]))
                inc(e.tensor_max(mx[:, :], mx[:, :], mn[:, :]))
                return slot

            ld_prev = [-1]
            mm_prev_b = {0: -1, 1: -1}
            ev_prev = [-1]
            qt_prev = [-1]
            srep_prev = [-1]
            steps = []

            def add7(stp):
                steps.append(stp)
                return len(steps) - 1

            for cc in range(8):
                def ld7(e, inc, _cc=cc):
                    for o in range(NF2):
                        inc(e.dma_start(out=W[o][:, :],
                                        in_=h2_h[:, o * T + _cc * 256:
                                                 o * T + (_cc + 1) * 256]), 16)
                i_ld = add7(Step("sync", 16 * 32, ld7, after=[qt_prev[0]]))
                i_ev_last = -1
                for o in range(NF2):
                    b = o % 2
                    i_mm = add7(Step("tensor", 1, lambda e, inc, _o=o, _b=b:
                                     inc(e.matmul(psH[:, 0:256] if False else
                                                  (psH if _b == 0 else psR)[:, :],
                                                  h128f_s[:, :], W[_o][:, :],
                                                  start=True, stop=True)),
                                     after=[i_ld, mm_prev_b[b]]))
                    i_ev = add7(Step("scalar", 1, lambda e, inc, _o=o, _b=b:
                                     inc(e.copy(out=W[_o][:, :],
                                                in_=(psH if _b == 0 else psR)[:, :])),
                                     after=[i_mm]))
                    mm_prev_b[b] = i_ev
                    i_ev_last = i_ev
                sstore = {}
                i_bf = add7(Step("vector", 1,
                                 lambda e, inc: sstore.__setitem__("s", butterflies2(e, inc)),
                                 after=[i_ev_last]))
                i_prev = i_bf
                for r in range(7):
                    p = 64 >> r
                    i_d = add7(Step("sync", 16, lambda e, inc, _p=p:
                                    inc(e.dma_start(out=ptmp[0:_p, :], in_=mx[_p:2 * _p, :]), 16),
                                    after=[i_prev]))
                    i_prev = add7(Step("vector", 1, lambda e, inc, _p=p:
                                       inc(e.tensor_max(mx[0:_p, :], mx[0:_p, :], ptmp[0:_p, :])),
                                       after=[i_d]))
                i_bt = add7(Step("vector", 1, lambda e, inc, _cc=cc: (
                    vsync(e, e.tensor_scalar_max(mrow2[0:1, :], mx[0:1, :], 1e-5)),
                    vsync(e, e.tensor_mul(betar[0:1, _cc * 256:(_cc + 1) * 256],
                                          mrow1[0:1, _cc * 256:(_cc + 1) * 256],
                                          mrow1[0:1, _cc * 256:(_cc + 1) * 256])),
                    vsync(e, e.tensor_mul(betar[0:1, _cc * 256:(_cc + 1) * 256],
                                          betar[0:1, _cc * 256:(_cc + 1) * 256], mrow2[0:1, :])),
                    inc(e.tensor_scalar(betar[0:1, _cc * 256:(_cc + 1) * 256],
                                        betar[0:1, _cc * 256:(_cc + 1) * 256],
                                        combo[0:1, :], None, AO.mult))),
                    after=[i_prev, srep_prev[0]]))
                i_rp = add7(Step("tensor", 1, lambda e, inc:
                                 inc(e.matmul(psR[:, 0:256], onr[:, :], mrow2[0:1, :],
                                              start=True, stop=True)),
                                 after=[i_bt]))
                i_sr = add7(Step("vector", 1, lambda e, inc: (
                    vsync(e, e.tensor_scalar_mul(Sb2[:, :], psR[:, 0:256], 1.0 / 127.0)),
                    inc(e.reciprocal(Sb2[:, :], Sb2[:, :]))),
                    after=[i_rp]))
                srep_prev[0] = i_sr
                i_q_last = -1
                for o in range(NF2):
                    i_q = add7(Step("vector", 1, lambda e, inc, _o=o, _cc=cc: (
                        vsync(e, e.tensor_mul(qi8[:, :], W[sstore["s"][_o]][:, :], Sb2[:, :])),
                        inc(e.tensor_copy(K2[:, _o * T + _cc * 256: _o * T + (_cc + 1) * 256],
                                          qi8[:, :]))),
                        after=[i_sr] if o == 0 else None))
                    i_q_last = i_q
                qt_prev[0] = i_q_last
            ch.run(steps)
            if k2_t is not None:
                ch.run([Step("gpsimd", 16 * 8, lambda e, inc: [
                    inc(e.dma_start(out=k2_t[:, i * 8192:(i + 1) * 8192],
                                    in_=K2[:, i * 8192:(i + 1) * 8192]), 16) for i in range(8)])])

            ph7.close()

            if upto < 9:
                return _finish(nc, ch)
            brep = sb("brep", [128, T], F32)
            brepp = ph.enter_context(nc.psum_tensor("psB0", [128, 512], F32))
            steps = []
            for tc in range(4):
                steps.append(Step("tensor", 1, lambda e, inc, _tc=tc:
                                  inc(e.matmul(brepp[:, :], onr[:, :],
                                               betar[0:1, _tc * 512:(_tc + 1) * 512],
                                               start=True, stop=True))))
                steps.append(Step("scalar", 1, lambda e, inc, _tc=tc:
                                  inc(e.copy(out=brep[:, _tc * 512:(_tc + 1) * 512],
                                             in_=brepp[:, :]))))
            ch.run(steps)

            wst2 = sb("wst2", [128, 4096], F32)
            i8s2 = sb("i8s2", [128, 4096], I8)
            tsl2 = [sb(f"tsl2{i}", [128, 4096], BF16) for i in range(2)]
            ub = [sb(f"ub{i}", [128, 512], F32) for i in range(2)]
            gb = ub
            stile = [sb(f"stile{i}", [128, 512], BF16) for i in range(2)]
            sgb = [sb(f"sgb{i}", [128, 512], BF16) for i in range(2)]
            xblk4 = [sb(f"xblk4{i}", [128, 512], F32) for i in range(2)]
            obuf4 = [sb(f"obuf4{i}", [128, 512], F32) for i in range(2)]
            psD = [ph.enter_context(nc.psum_tensor(f"psD{i}", [128, 512], F32)) for i in range(2)]
            psX = [ph.enter_context(nc.psum_tensor(f"psX{i}", [128, 128], BF16)) for i in range(2)]
            steps = []

            def add8(stp):
                steps.append(stp)
                return len(steps) - 1

            i_i8_l = [-1]
            i_cl_p = {0: -1, 1: -1}
            i_mml_p = {0: -1, 1: -1}
            i_u_b = {0: -1, 1: -1}
            i_sg_p = {0: -1, 1: -1}
            i_sl_p = {0: -1, 1: -1}
            i_xl_p = {0: -1, 1: -1}
            i_tp_b = {0: -1, 1: -1}
            i_xa_b = {0: -1, 1: -1}
            i_ob_p = {0: -1, 1: -1}
            i_od_p = {0: -1, 1: -1}
            mm_ctr = [0]
            tp_ctr = [0]
            for o2 in range(16):
                p = o2 % 2

                def wl8(e, inc, _o2=o2):
                    for ec in range(32):
                        inc(e.dma_start(out=wst2[:, ec * 128:(ec + 1) * 128],
                                        in_=wdnT[ec * 128:(ec + 1) * 128,
                                                 _o2 * 128:(_o2 + 1) * 128]), 16)
                i_wl = add8(Step("sync", 16 * 32, wl8, after=[i_i8_l[0]]))
                i_i8 = add8(Step("scalar", 1, lambda e, inc:
                                 inc(e.activation(i8s2[:, :], wst2[:, :], AF.Copy,
                                                  scale=rwsd[:, :])),
                                 after=[i_wl, i_cl_p[1 - p]]))
                i_i8_l[0] = i_i8
                i_cl = add8(Step("vector", 1, lambda e, inc, _p=p:
                                 inc(e.tensor_scalar(tsl2[_p][:, :], i8s2[:, :], 1.0, -1.0,
                                                     AO.min, AO.max)),
                                 after=[i_i8, i_mml_p[p]]))
                i_cl_p[p] = i_cl
                for tc in range(4):
                    b = mm_ctr[0] % 2
                    mm_ctr[0] += 1
                    i_sl = add8(Step("gpsimd", 16, lambda e, inc, _o2=o2, _tc=tc, _b=b:
                                     inc(e.dma_start(out=stile[_b][:, :],
                                                     in_=s_hbm[:, _o2 * T + _tc * 512:
                                                              _o2 * T + (_tc + 1) * 512]), 16),
                                     after=[i_sg_p[b]]))
                    i_xl = add8(Step("gpsimd", 16, lambda e, inc, _o2=o2, _tc=tc, _b=b:
                                     inc(e.dma_start(
                                         out=xblk4[_b][:, :].rearrange("p (i c) -> p i c", i=4),
                                         in_=x[_tc * 512:(_tc + 1) * 512,
                                               _o2 * 128:(_o2 + 1) * 128]
                                         .rearrange("(i p) c -> p i c", i=4)), 16),
                                     after=[i_xa_b[b]]))

                    def mmd(e, inc, _tc=tc, _p=p, _b=b):
                        for ec in range(32):
                            mm = e.matmul(psD[_b][:, :], tsl2[_p][:, ec * 128:(ec + 1) * 128],
                                          K2[:, ec * T + _tc * 512: ec * T + (_tc + 1) * 512],
                                          start=(ec == 0), stop=(ec == 31))
                        inc(mm)
                    i_mm = add8(Step("tensor", 1, mmd, after=[i_cl, i_u_b[b]]))
                    i_u = add8(Step("vector", 1, lambda e, inc, _tc=tc, _b=b:
                                    inc(e.tensor_mul(ub[_b][:, :], psD[_b][:, :],
                                                     brep[:, _tc * 512:(_tc + 1) * 512])),
                                    after=[i_mm, i_sg_p[b]]))
                    i_u_b[b] = i_u
                    i_g = add8(Step("scalar", 1, lambda e, inc, _b=b:
                                    inc(e.activation(gb[_b][:, :], ub[_b][:, :], AF.Silu)),
                                    after=[i_u]))
                    i_sg = add8(Step("vector", 1, lambda e, inc, _b=b:
                                     inc(e.tensor_mul(sgb[_b][:, :], gb[_b][:, :],
                                                      stile[_b][:, :])),
                                     after=[i_g, i_sl]))
                    i_sg_p[b] = i_sg
                    i_xa_last = -1
                    for i in range(4):
                        xb = tp_ctr[0] % 2
                        tp_ctr[0] += 1
                        i_t = add8(Step("tensor", 1, lambda e, inc, _i=i, _xb=xb, _b=b:
                                        inc(e.transpose(psX[_xb][:, :],
                                                        sgb[_b][:, _i * 128:(_i + 1) * 128],
                                                        idn_s[:, :])),
                                        after=[i_sg, i_tp_b[xb]]))
                        i_xa = add8(Step("vector", 1, lambda e, inc, _i=i, _xb=xb, _b=b:
                                         inc(e.tensor_add(obuf4[_b][:, _i * 128:(_i + 1) * 128],
                                                          psX[_xb][:, :],
                                                          xblk4[_b][:, _i * 128:(_i + 1) * 128])),
                                         after=[i_t, i_xl, i_od_p[b]]))
                        i_tp_b[xb] = i_xa
                        i_xa_last = i_xa
                    i_xa_b[b] = i_xa_last
                    i_od = add8(Step("gpsimd", 16, lambda e, inc, _o2=o2, _tc=tc, _b=b:
                                     inc(e.dma_start(
                                         out=out[_tc * 512:(_tc + 1) * 512,
                                                 _o2 * 128:(_o2 + 1) * 128]
                                         .rearrange("(i p) c -> p i c", i=4),
                                         in_=obuf4[_b][:, :].rearrange("p (i c) -> p i c", i=4)), 16),
                                     after=[i_xa_last]))
                    i_od_p[b] = i_od
                i_mml_p[p] = i_mm
            ch.run(steps)

        return _finish(nc, ch)


def _finish(nc, ch):
    fin = dict(ch.cnt)
    sems = dict(ch.sems)
    with nc.Block() as blk:
        @blk.gpsimd
        def _(gp):
            for ln, c in fin.items():
                if c > 0:
                    gp.wait_ge(sems[ln], c)
    return nc


def _prep_inputs(L0_attn_x, L0_attn_Wq, L0_attn_Wk, L0_attn_Wv, L0_attn_Wo,
                 L0_attn_lq1, L0_attn_lq2, L0_attn_lk1, L0_attn_lk2,
                 L0_attn_rms_w, L0_ffn_w_up, L0_ffn_w_down):
    cons = _host_consts()
    f32 = np.float32
    wq = np.ascontiguousarray(np.asarray(L0_attn_Wq, dtype=f32)[:, _perm_cols(H)])
    wk = np.ascontiguousarray(np.asarray(L0_attn_Wk, dtype=f32)[:, _perm_cols(KVH)])
    wv = np.ascontiguousarray(np.asarray(L0_attn_Wv, dtype=f32))
    wo = np.ascontiguousarray(np.asarray(L0_attn_Wo, dtype=f32))
    wupT = np.ascontiguousarray(np.asarray(L0_ffn_w_up, dtype=f32).T)
    wdnT = np.ascontiguousarray(np.asarray(L0_ffn_w_down, dtype=f32).T)
    base = {
        "wq": wq, "wk": wk, "wv": wv, "wo": wo, "wupT": wupT, "wdnT": wdnT,
        "lq1": np.ascontiguousarray(np.asarray(L0_attn_lq1, dtype=f32).T),
        "lq2": np.ascontiguousarray(np.asarray(L0_attn_lq2, dtype=f32).T),
        "lk1": np.ascontiguousarray(np.asarray(L0_attn_lk1, dtype=f32).T),
        "lk2": np.ascontiguousarray(np.asarray(L0_attn_lk2, dtype=f32).T),
        "rmsw": np.ascontiguousarray(np.asarray(L0_attn_rms_w, dtype=f32).reshape(128, 1)),
        "c_lo": cons["c_lo"], "s_lo": cons["s_lo"],
        "c_hi": cons["c_hi"], "s_hi": cons["s_hi"],
        "trid": cons["tri"], "h128bd": cons["h128b"], "h128fd": cons["h128f"],
        "idnd": cons["idn"], "ones_rd": cons["ones_r"], "ones_cd": cons["ones_c"],
        "eps1d": np.full((128, 1), 1e-5, dtype=f32),
        "eps2d": np.full((128, 1), 1e-8 / (CR * CR), dtype=f32),
        "ohd": (np.arange(127)[None, :] == 63).astype(f32) * np.ones((128, 1), f32),
    }
    xs = np.asarray(L0_attn_x, dtype=f32)
    in_maps = []
    for b in range(8):
        m = dict(base)
        m["x"] = np.ascontiguousarray(xs[b])
        in_maps.append(m)
    return in_maps


def kernel(L0_attn_x, L0_attn_Wq, L0_attn_Wk, L0_attn_Wv, L0_attn_Wo,
           L0_attn_lq1, L0_attn_lq2, L0_attn_lk1, L0_attn_lk2,
           L0_attn_rms_w, L0_ffn_w_up, L0_ffn_w_down):
    in_maps = _prep_inputs(L0_attn_x, L0_attn_Wq, L0_attn_Wk, L0_attn_Wv, L0_attn_Wo,
                           L0_attn_lq1, L0_attn_lq2, L0_attn_lk1, L0_attn_lk2,
                           L0_attn_rms_w, L0_ffn_w_up, L0_ffn_w_down)
    nc = build_program()
    res = run_bass_kernel_spmd(nc, in_maps, core_ids=list(range(8)))
    return np.stack([res.results[b]["out"] for b in range(8)], axis=0)


# version 16
# speedup vs baseline: 1.0242x; 1.0227x over previous
import numpy as np
import ml_dtypes
import concourse.bass as bass
import concourse.mybir as mybir
from concourse.bass_utils import run_bass_kernel_spmd
from concourse.alu_op_type import AluOpType as AO
from contextlib import ExitStack

F32 = mybir.dt.float32
BF16 = mybir.dt.bfloat16
I8 = mybir.dt.int8
AF = mybir.ActivationFunctionType
AX = mybir.AxisListType.X

T = 2048
E = 2048
H = 16
KVH = 4
NEG = -1e30
LAM = 0.8 - 0.6 * float(np.exp(-0.3 * 0))
CR = 1.0 - LAM
IFF = 4096
NE = E // 128
NF2 = IFF // 128
NW_ELEM = float(E * IFF)
C0 = 1.0 / (127.0 ** 3 * 2048.0 * 64.0)
ISQ = float(1.0 / np.sqrt(128.0))


def _host_consts():
    th = 10000.0 ** (-np.arange(128, dtype=np.float64) / 128.0)
    ang = th[:, None] * np.arange(T, dtype=np.float64)[None, :]
    cos, sin = np.cos(ang), np.sin(ang)
    c_lo = np.concatenate([cos[0:64], cos[0:64]], 0).astype(np.float32)
    s_lo = np.concatenate([-sin[0:64], sin[0:64]], 0).astype(np.float32)
    c_hi = np.concatenate([cos[64:128], cos[64:128]], 0).astype(np.float32)
    s_hi = np.concatenate([-sin[64:128], sin[64:128]], 0).astype(np.float32)
    tri = np.where(np.arange(128)[None, :] > np.arange(128)[:, None],
                   np.float32(NEG), np.float32(0.0)).astype(np.float32)
    ii, jj = np.meshgrid(np.arange(128), np.arange(128), indexing="ij")
    v = (ii & jj).astype(np.int64)
    par = np.zeros((128, 128), dtype=np.int64)
    while v.max() > 0:
        par ^= (v & 1)
        v >>= 1
    h128 = np.where(par == 0, 1.0, -1.0)
    return {"c_lo": c_lo, "s_lo": s_lo, "c_hi": c_hi, "s_hi": s_hi, "tri": tri,
            "h128b": h128.astype(ml_dtypes.bfloat16),
            "h128f": h128.astype(np.float32),
            "idn": np.eye(128).astype(ml_dtypes.bfloat16),
            "ones_r": np.ones((1, 128), dtype=np.float32),
            "ones_c": np.ones((128, 1), dtype=np.float32)}


def _perm_cols(nheads):
    cols = []
    for h in range(nheads):
        b = h * 256
        cols += [b + 2 * j for j in range(64)] + [b + 2 * j + 1 for j in range(64)]
        cols += [b + 2 * j for j in range(64, 128)] + [b + 2 * j + 1 for j in range(64, 128)]
    return np.array(cols)


class Step:
    __slots__ = ("eng", "inc", "fn", "after", "lane", "_lane")

    def __init__(self, eng, inc, fn, after=None, lane=None):
        self.eng, self.inc, self.fn, self.after, self.lane = eng, inc, fn, after, lane


class Ch:
    ENG_LANE = {"tensor": "te", "vector": "ve", "scalar": "sc",
                "sync": "sy", "gpsimd": "gd"}

    def __init__(self, nc, mksem):
        self.nc = nc
        self.mksem = mksem
        self.sems = {}
        self.cnt = {}

    def lane(self, name):
        if name not in self.sems:
            self.sems[name] = self.mksem(name)
            self.cnt[name] = 0
        return self.sems[name]

    def run(self, steps):
        nc = self.nc
        lanes = []
        for s in steps:
            s._lane = s.lane if getattr(s, "lane", None) else self.ENG_LANE[s.eng]
            self.lane(s._lane)
            lanes.append(s._lane)
        base = dict(self.cnt)
        run_cnt = dict(self.cnt)
        post = []
        for s in steps:
            run_cnt[s._lane] += s.inc
            post.append((s._lane, run_cnt[s._lane]))
        waited = set()
        def _preds(i, s):
            a = s.after
            if a is None:
                return [i - 1] if i > 0 else []
            if isinstance(a, int):
                return [a] if a >= 0 else []
            return [j for j in a if j >= 0]
        for i, s in enumerate(steps):
            pr = _preds(i, s)
            if not pr:
                continue
            best = {}
            for j in pr:
                ln, c = post[j]
                if c > best.get(ln, (-1, -1))[0]:
                    best[ln] = (c, j)
            for ln, (c, j) in best.items():
                waited.add(j)
        engs = []
        for e in ("sync", "tensor", "vector", "scalar", "gpsimd"):
            if any(s.eng == e for s in steps):
                engs.append(e)
        with nc.Block() as blk:
            for ename in engs:
                def body(eng, _e=ename):
                    for i, s in enumerate(steps):
                        if s.eng != _e:
                            continue
                        preds = _preds(i, s)
                        if preds:
                            tg = {}
                            for j in preds:
                                ln, c = post[j]
                                tg[ln] = max(tg.get(ln, 0), c)
                            for ln, c in tg.items():
                                eng.wait_ge(self.sems[ln], c)
                        else:
                            for ln, c in base.items():
                                if c > 0:
                                    eng.wait_ge(self.sems[ln], c)
                        mysem = self.sems[s._lane]
                        got = {"n": 0}

                        def inc(instr, k=1, _g=got, _s=mysem):
                            _g["n"] += k
                            return instr.then_inc(_s, k)
                        s.fn(eng, inc)
                        assert got["n"] == s.inc, \
                            f"step {i} ({s.eng}): declared inc {s.inc} != emitted {got['n']}"
                        if s.eng in ("sync", "gpsimd") and i in waited:
                            eng.wait_ge(mysem, post[i][1])
                getattr(blk, ename)(body)
        self.cnt = run_cnt


def build_program(taps=(), upto=99):
    nc = bass.Bass("TRN2", target_bir_lowering=False)
    dt = lambda n, sh, ty, kind: nc.dram_tensor(n, sh, ty, kind=kind).ap()
    tk = lambda n: "ExternalOutput" if n in taps else "Internal"

    x = dt("x", [T, E], F32, "ExternalInput")
    wq = dt("wq", [E, H * 256], F32, "ExternalInput")
    wk = dt("wk", [E, KVH * 256], F32, "ExternalInput")
    wv = dt("wv", [E, KVH * 128], F32, "ExternalInput")
    wo = dt("wo", [E, E], F32, "ExternalInput")
    wupT = dt("wupT", [E, IFF], F32, "ExternalInput")
    wdnT = dt("wdnT", [IFF, E], F32, "ExternalInput")
    lq1d = dt("lq1", [128, 16], F32, "ExternalInput")
    lq2d = dt("lq2", [128, 16], F32, "ExternalInput")
    lk1d = dt("lk1", [128, 16], F32, "ExternalInput")
    lk2d = dt("lk2", [128, 16], F32, "ExternalInput")
    rmswd = dt("rmsw", [128, 1], F32, "ExternalInput")
    c_lod = dt("c_lo", [128, T], F32, "ExternalInput")
    s_lod = dt("s_lo", [128, T], F32, "ExternalInput")
    c_hid = dt("c_hi", [128, T], F32, "ExternalInput")
    s_hid = dt("s_hi", [128, T], F32, "ExternalInput")
    trid = dt("trid", [128, 128], F32, "ExternalInput")
    h128bd = dt("h128bd", [128, 128], BF16, "ExternalInput")
    h128fd = dt("h128fd", [128, 128], F32, "ExternalInput")
    idnd = dt("idnd", [128, 128], BF16, "ExternalInput")
    ones_rd = dt("ones_rd", [1, 128], F32, "ExternalInput")
    ones_cd = dt("ones_cd", [128, 1], F32, "ExternalInput")
    eps1d = dt("eps1d", [128, 1], F32, "ExternalInput")
    eps2d = dt("eps2d", [128, 1], F32, "ExternalInput")
    ohd = dt("ohd", [128, 127], F32, "ExternalInput")
    out = dt("out", [T, E], F32, "ExternalOutput")

    xnT_h = dt("xnT_h", [128, NE * T], BF16, tk("xnT_h"))
    xq_h = dt("xq_h", [128, NE * T], BF16, tk("xq_h"))
    qT_h = dt("qT_h", [128, H * 2 * T], BF16, tk("qT_h"))
    kT_h = dt("kT_h", [128, KVH * 2 * T], BF16, tk("kT_h"))
    v_h = dt("v_h", [128, KVH * T], BF16, tk("v_h"))
    s_hbm = dt("s_hbm", [128, NE * T], BF16, tk("s_hbm"))
    h2_h = dt("h2_h", [128, NF2 * T], F32, tk("h2_h"))
    att_t = dt("att_t", [128, H * T], BF16, "ExternalOutput") if "att_t" in taps else None
    k2_t = dt("k2_t", [128, NF2 * T], BF16, "ExternalOutput") if "k2_t" in taps else None
    aux_t = dt("aux_t", [8, 4096], F32, "ExternalOutput") if "aux_t" in taps else None

    st = ExitStack()
    with st:
        sbuf = lambda n, sh, ty, **kw: st.enter_context(nc.sbuf_tensor(n, sh, ty, **kw))
        ch = Ch(nc, lambda n: st.enter_context(nc.semaphore("L_" + n)))
        vssem = st.enter_context(nc.semaphore("vs"))
        vs_n = [0]

        def vsync(e, instr):
            instr.then_inc(vssem, 1)
            vs_n[0] += 1
            e.wait_ge(vssem, vs_n[0])
            return instr

        lamrep = sbuf("lamrep", [128, 16], F32)
        rms_s = sbuf("rms_s", [128, 1], F32)
        rwsu = sbuf("rwsu", [128, 1], F32)
        rwsd = sbuf("rwsd", [128, 1], F32)
        wsu_s = sbuf("wsu_s", [1, 1], F32)
        wsd_s = sbuf("wsd_s", [1, 1], F32)
        combo = sbuf("combo", [1, 1], F32)
        mrow1 = sbuf("mrow1", [1, T], F32)
        betar = sbuf("betar", [1, T], F32)
        tri_s = sbuf("tri_s", [128, 128], F32)
        h128b_s = sbuf("h128b_s", [128, 128], BF16)
        h128f_s = sbuf("h128f_s", [128, 128], F32)
        idn_s = sbuf("idn_s", [128, 128], BF16)
        onr = sbuf("onr", [1, 128], F32)
        onc = sbuf("onc", [128, 1], F32)
        eps1_s = sbuf("eps1_s", [128, 1], F32)
        eps2_s = sbuf("eps2_s", [128, 1], F32)
        oh_s = sbuf("oh_s", [128, 127], F32)

        with ExitStack() as ph:
            sb = lambda n, sh, ty: ph.enter_context(nc.sbuf_tensor(n, sh, ty))
            pt = lambda n, sh, ty=F32: ph.enter_context(nc.psum_tensor(n, sh, ty))
            l1, l2 = sb("l1", [128, 16], F32), sb("l2", [128, 16], F32)
            l3, l4 = sb("l3", [128, 16], F32), sb("l4", [128, 16], F32)
            p0, p1, p2 = pt("p0", [128, 16]), pt("p1", [128, 16]), pt("p2", [128, 16])
            row1, row2 = sb("row1", [1, 16], F32), sb("row2", [1, 16], F32)
            lamrow = sb("lamrow", [1, 16], F32)

            loads = [(tri_s, trid), (h128b_s, h128bd), (h128f_s, h128fd),
                     (idn_s, idnd), (onr, ones_rd), (onc, ones_cd),
                     (rms_s, rmswd), (l1, lq1d), (l2, lq2d), (l3, lk1d), (l4, lk2d),
                     (eps1_s, eps1d), (eps2_s, eps2d), (oh_s, ohd)]
            steps = [Step("sync", 16 * len(loads),
                          lambda e, inc, _l=loads: [inc(e.dma_start(out=d[:, :], in_=s[:, :]), 16)
                                                    for d, s in _l])]
            steps += [
                Step("vector", 1, lambda e, inc: (
                    e.tensor_mul(l1[:, :], l1[:, :], l3[:, :]),
                    inc(e.tensor_mul(l2[:, :], l2[:, :], l4[:, :])))),
                Step("tensor", 1, lambda e, inc: (
                    e.matmul(p0[0:1, :], onc[:, :], l1[:, :], start=True, stop=True),
                    inc(e.matmul(p1[0:1, :], onc[:, :], l2[:, :], start=True, stop=True)))),
                Step("scalar", 1, lambda e, inc: (
                    e.activation(row1[:, :], p0[0:1, :], AF.Exp),
                    inc(e.activation(row2[:, :], p1[0:1, :], AF.Exp)))),
                Step("vector", 1, lambda e, inc: (
                    vsync(e, e.tensor_sub(lamrow[:, :], row1[:, :], row2[:, :])),
                    inc(e.tensor_scalar_add(lamrow[:, :], lamrow[:, :], float(LAM))))),
                Step("tensor", 1, lambda e, inc:
                     inc(e.matmul(p2[:, :], onr[:, :], lamrow[:, :], start=True, stop=True))),
                Step("scalar", 1, lambda e, inc:
                     inc(e.copy(out=lamrep[:, :], in_=p2[:, :]))),
            ]
            ch.run(steps)

        if upto < 1:
            return _finish(nc, ch)
        with ExitStack() as ph:
            sb = lambda n, sh, ty: ph.enter_context(nc.sbuf_tensor(n, sh, ty))
            pt = lambda n, sh: ph.enter_context(nc.psum_tensor(n, sh, F32))
            wst = [sb(f"wst{i}", [128, 2048], F32) for i in range(2)]
            acc = sb("acc", [128, 64], F32)
            ucol, dcol = sb("ucol", [128, 1], F32), sb("dcol", [128, 1], F32)
            rtmp = sb("rtmp", [1, 1], F32)
            pA, pC = pt("pA", [128, 1]), pt("pC", [128, 1])
            tiles = [(wupT, r, c) for r in range(16) for c in range(2)] + \
                    [(wdnT, r, 0) for r in range(32)]
            steps = []
            ld_idx = {}
    
            for i, (w, r, c) in enumerate(tiles):
                eng = "sync" if i % 2 == 0 else "gpsimd"
                prev_red = 2 * (i - 2) + 1 if i >= 2 else -1
                ld_idx[i] = len(steps)
                steps.append(Step(eng, 16,
                                  lambda e, inc, _w=w, _r=r, _c=c, _i=i:
                                  inc(e.dma_start(out=wst[_i % 2][:, :],
                                                  in_=_w[_r * 128:(_r + 1) * 128,
                                                         _c * 2048:(_c + 1) * 2048]), 16),
                                  after=[prev_red] if prev_red >= 0 else -1))
                steps.append(Step("vector", 1,
                                  lambda e, inc, _i=i:
                                  inc(e.reduce_sum(acc[:, _i:_i + 1], wst[_i % 2][:, :],
                                                   axis=AX, apply_absolute_value=True)),
                                  after=[ld_idx[i]]))
            for name, lo, hi, col, wsx, rws in [("u", 0, 32, ucol, wsu_s, rwsu),
                                                ("d", 32, 64, dcol, wsd_s, rwsd)]:
                steps += [
                    Step("vector", 1, lambda e, inc, _lo=lo, _hi=hi, _col=col:
                         inc(e.reduce_sum(_col[:, :], acc[:, _lo:_hi], axis=AX))),
                    Step("tensor", 1, lambda e, inc, _col=col:
                         inc(e.matmul(pA[0:1, :], onc[:, :], _col[:, :], start=True, stop=True))),
                    Step("vector", 1, lambda e, inc, _w=wsx:
                         inc(e.tensor_scalar(_w[:, :], pA[0:1, :], 1.0 / NW_ELEM, 1e-5,
                                             AO.mult, AO.max))),
                    Step("vector", 1, lambda e, inc, _w=wsx:
                         inc(e.reciprocal(rtmp[:, :], _w[:, :]))),
                    Step("tensor", 1, lambda e, inc:
                         inc(e.matmul(pC[:, :], onr[:, :], rtmp[:, :], start=True, stop=True))),
                    Step("scalar", 1, lambda e, inc, _r=rws:
                         inc(e.copy(out=_r[:, :], in_=pC[:, :]))),
                ]
            steps.append(Step("vector", 1, lambda e, inc: (
                vsync(e, e.tensor_mul(combo[:, :], wsu_s[:, :], wsu_s[:, :])),
                vsync(e, e.tensor_mul(combo[:, :], combo[:, :], wsd_s[:, :])),
                inc(e.tensor_scalar_mul(combo[:, :], combo[:, :], float(C0))))))
            ch.run(steps)

        if upto < 2:
            return _finish(nc, ch)
        with ExitStack() as ph:
            sb = lambda n, sh, ty: ph.enter_context(nc.sbuf_tensor(n, sh, ty))
            xt = sb("xt", [128, 2048], F32)
            dump = sb("dump", [128, 2048], F32)
            sqc = sb("sqc", [128, 1], F32)
            sumc = sb("sumc", [128, 1], F32)
            mean = sb("mean", [128, 1], F32)
            msq = sb("msq", [128, 1], F32)
            var = sb("var", [128, 1], F32)
            rstd = sb("rstd", [128, 1], F32)
            negmr = sb("negmr", [128, 1], F32)
            xnbf = sb("xnbf", [128, 2048], BF16)
            xstg = sb("xstg", [128, 2048], BF16)
            psT = [ph.enter_context(nc.psum_tensor(f"psT{i}", [128, 128], BF16)) for i in range(2)]
            for tt in range(16):
                steps = [
                    Step("sync", 16, lambda e, inc, _tt=tt:
                         inc(e.dma_start(out=xt[:, :], in_=x[_tt * 128:(_tt + 1) * 128, :]), 16),
                         after=-1),
                    Step("scalar", 1, lambda e, inc:
                         inc(e.activation(dump[:, :], xt[:, :], AF.Square, accum_out=sqc[:, :]))),
                    Step("vector", 1, lambda e, inc: (
                        vsync(e, e.reduce_sum(sumc[:, :], xt[:, :], axis=AX)),
                        vsync(e, e.tensor_scalar_mul(mean[:, :], sumc[:, :], 1.0 / 2048.0)),
                        vsync(e, e.tensor_mul(msq[:, :], mean[:, :], mean[:, :])),
                        vsync(e, e.tensor_scalar_mul(var[:, :], sqc[:, :], 1.0 / 2048.0)),
                        inc(e.tensor_sub(var[:, :], var[:, :], msq[:, :])))),
                    Step("scalar", 1, lambda e, inc:
                         inc(e.activation(rstd[:, :], var[:, :], AF.Sqrt, bias=eps1_s[:, :]))),
                    Step("vector", 1, lambda e, inc: (
                        vsync(e, e.reciprocal(rstd[:, :], rstd[:, :])),
                        vsync(e, e.tensor_mul(negmr[:, :], mean[:, :], rstd[:, :])),
                        vsync(e, e.tensor_scalar_mul(negmr[:, :], negmr[:, :], -1.0)),
                        inc(e.tensor_scalar(xnbf[:, :], xt[:, :], rstd[:, :], negmr[:, :],
                                            AO.mult, AO.add)))),
                ]
                for fp in range(8):
                    steps.append(Step("tensor", 1, lambda e, inc, _f=2 * fp: (
                        e.transpose(psT[0][:, :], xnbf[:, _f * 128:(_f + 1) * 128], idn_s[:, :]),
                        inc(e.transpose(psT[1][:, :], xnbf[:, (_f + 1) * 128:(_f + 2) * 128],
                                        idn_s[:, :])))))
                    steps.append(Step("scalar", 1, lambda e, inc, _f=2 * fp: (
                        e.copy(out=xstg[:, _f * 128:(_f + 1) * 128], in_=psT[0][:, :]),
                        inc(e.copy(out=xstg[:, (_f + 1) * 128:(_f + 2) * 128], in_=psT[1][:, :])))))
                steps.append(Step("gpsimd", 16, lambda e, inc, _tt=tt:
                                  inc(e.dma_start(
                                      out=xnT_h.rearrange("p (f t) -> p f t", f=16)[
                                          :, :, _tt * 128:(_tt + 1) * 128],
                                      in_=xstg[:, :].rearrange("p (f c) -> p f c", f=16)), 16)))
                ch.run(steps)

        if upto < 3:
            return _finish(nc, ch)
        with ExitStack() as ph:
            sb = lambda n, sh, ty: ph.enter_context(nc.sbuf_tensor(n, sh, ty))
            W = [sb(f"Wf{i}", [128, 512], F32) for i in range(24)]
            xsl = sb("xsl", [128, 512], BF16)
            mx = sb("mx", [128, 512], F32)
            mn = sb("mn", [128, 512], F32)
            ptmp = sb("ptmp", [64, 512], F32)
            Sb2 = sb("Sb2", [128, 512], F32)
            qi8 = sb("qi8", [128, 512], I8)
            qbf = sb("qbf", [128, 512], BF16)
            psH = ph.enter_context(nc.psum_tensor("psH", [128, 512], F32))
            psR = ph.enter_context(nc.psum_tensor("psR", [128, 512], F32))
            for tc in range(4):
                steps = []
                for f in range(16):
                    steps.append(Step("sync", 16, lambda e, inc, _f=f, _tc=tc:
                                      inc(e.dma_start(out=xsl[:, :],
                                                      in_=xnT_h[:, _f * T + _tc * 512:
                                                                _f * T + (_tc + 1) * 512]), 16)))
                    steps.append(Step("tensor", 1, lambda e, inc:
                                      inc(e.matmul(psH[:, :], h128b_s[:, :], xsl[:, :],
                                                   start=True, stop=True))))
                    steps.append(Step("scalar", 1, lambda e, inc, _f=f:
                                      inc(e.copy(out=W[_f][:, :], in_=psH[:, :]))))

                def butterflies(e, inc, nt=16):
                    slot = list(range(nt))
                    free = list(range(nt, nt + nt // 2))
                    nbits = int(np.log2(nt))
                    for p in range(nbits):
                        b = 1 << p
                        pairs = [(i, i | b) for i in range(nt) if not (i & b)]
                        last = None
                        for idx, (i, j) in enumerate(pairs):
                            last = e.tensor_sub(W[free[idx]][:, :], W[slot[i]][:, :],
                                                W[slot[j]][:, :])
                        vsync(e, last)
                        last = None
                        for i, j in pairs:
                            last = e.tensor_add(W[slot[i]][:, :], W[slot[i]][:, :],
                                                W[slot[j]][:, :])
                        vsync(e, last)
                        for idx, (i, j) in enumerate(pairs):
                            slot[j], free[idx] = free[idx], slot[j]
                    for op, dstbuf, neg in ((AO.max, mx, False), (AO.min, mn, True)):
                        last = None
                        for t in range(nt // 2):
                            last = e.tensor_tensor(W[free[t]][:, :], W[slot[2 * t]][:, :],
                                                   W[slot[2 * t + 1]][:, :], op)
                        vsync(e, last)
                        gap = 1
                        while gap < nt // 2:
                            last = None
                            for t in range(0, nt // 2, 2 * gap):
                                last = e.tensor_tensor(W[free[t]][:, :], W[free[t]][:, :],
                                                       W[free[t + gap]][:, :], op)
                            vsync(e, last)
                            gap *= 2
                        if neg:
                            vsync(e, e.tensor_scalar_mul(dstbuf[:, :], W[free[0]][:, :], -1.0))
                        else:
                            vsync(e, e.tensor_copy(dstbuf[:, :], W[free[0]][:, :]))
                    inc(e.tensor_max(mx[:, :], mx[:, :], mn[:, :]))
                    return slot
                slot_store = {}
                steps.append(Step("vector", 1,
                                  lambda e, inc: slot_store.__setitem__("s", butterflies(e, inc))))
                for r in range(7):
                    p = 64 >> r
                    steps.append(Step("sync", 16, lambda e, inc, _p=p:
                                      inc(e.dma_start(out=ptmp[0:_p, :], in_=mx[_p:2 * _p, :]), 16)))
                    steps.append(Step("vector", 1, lambda e, inc, _p=p:
                                      inc(e.tensor_max(mx[0:_p, :], mx[0:_p, :], ptmp[0:_p, :]))))
                steps.append(Step("vector", 1, lambda e, inc, _tc=tc:
                                  inc(e.tensor_scalar_max(mrow1[0:1, _tc * 512:(_tc + 1) * 512],
                                                          mx[0:1, :], 1e-5))))
                steps.append(Step("tensor", 1, lambda e, inc, _tc=tc:
                                  inc(e.matmul(psR[:, :], onr[:, :],
                                               mrow1[0:1, _tc * 512:(_tc + 1) * 512],
                                               start=True, stop=True))))
                steps.append(Step("vector", 1, lambda e, inc: (
                    vsync(e, e.tensor_scalar_mul(Sb2[:, :], psR[:, :], 1.0 / 127.0)),
                    inc(e.reciprocal(Sb2[:, :], Sb2[:, :])))))
                for f in range(16):
                    steps.append(Step("vector", 1, lambda e, inc, _f=f: (
                        vsync(e, e.tensor_mul(qi8[:, :], W[slot_store["s"][_f]][:, :], Sb2[:, :])),
                        inc(e.tensor_copy(qbf[:, :], qi8[:, :])))))
                    steps.append(Step("gpsimd", 16, lambda e, inc, _f=f, _tc=tc:
                                      inc(e.dma_start(out=xq_h[:, _f * T + _tc * 512:
                                                               _f * T + (_tc + 1) * 512],
                                                      in_=qbf[:, :]), 16)))
                ch.run(steps)

        if upto < 4:
            return _finish(nc, ch)
        with ExitStack() as ph:
            sb = lambda n, sh, ty: ph.enter_context(nc.sbuf_tensor(n, sh, ty))
            ctab = [sb("ctab0", [128, T], F32), sb("ctab1", [128, T], F32)]
            stab = [sb("stab0", [128, T], F32), sb("stab1", [128, T], F32)]
            wqst = [sb(f"wqst{i}", [128, 8 * 2048], BF16) for i in range(2)]
            wkst = [sb(f"wkst{i}", [128, 2 * 2048], BF16) for i in range(2)]
            wvst = [sb(f"wvst{i}", [128, 2048], BF16) for i in range(2)]
            xns = [sb(f"xns{i}", [128, 16 * 512], BF16) for i in range(2)]
            Xc = [sb(f"Xc{i}", [128, 512], F32) for i in range(2)]
            Xs = [sb(f"Xs{i}", [128, 512], F32) for i in range(2)]
            m1b = [sb(f"m1b{i}", [128, 512], F32) for i in range(2)]
            qkst = [sb(f"qkst{i}", [128, 512], BF16) for i in range(2)]
            vbfb = sb("vbfb", [128, 512], BF16)
            vstg = [sb(f"vstg{i}", [128, 128], BF16) for i in range(2)]
            psP = [ph.enter_context(nc.psum_tensor(f"psP{i}", [128, 512], F32)) for i in range(2)]
            psV = [ph.enter_context(nc.psum_tensor(f"psV{i}", [128, 128], BF16)) for i in range(2)]

            ch.run([Step("sync", 16 * 4, lambda e, inc: [
                inc(e.dma_start(out=ctab[0][:, :], in_=c_lod[:, :]), 16),
                inc(e.dma_start(out=stab[0][:, :], in_=s_lod[:, :]), 16),
                inc(e.dma_start(out=ctab[1][:, :], in_=c_hid[:, :]), 16),
                inc(e.dma_start(out=stab[1][:, :], in_=s_hid[:, :]), 16)])])

            steps = []

            def add(stp):
                steps.append(stp)
                return len(steps) - 1

            i_wl_prev = {0: -1, 1: -1}
            i_xl_prev = {0: -1, 1: -1}
            i_ev_bank = {0: -1, 1: -1}
            i_vv_bank = {0: -1, 1: -1}
            i_rope_buf = {0: -1, 1: -1}
            i_vstg_buf = {0: -1, 1: -1}
            i_vbf = [-1]
            ot_ctr = [0]
            v_ctr = [0]
            for g in range(KVH):
                gpar = g % 2

                def wload(e, inc, _g=g, _p=gpar):
                    for c2 in range(2):
                        for ec in range(16):
                            inc(e.dma_start(
                                out=wkst[_p][:, c2 * 2048 + ec * 128: c2 * 2048 + (ec + 1) * 128],
                                in_=wk[ec * 128:(ec + 1) * 128,
                                       _g * 256 + c2 * 128: _g * 256 + (c2 + 1) * 128]), 16)
                    for ec in range(16):
                        inc(e.dma_start(out=wvst[_p][:, ec * 128:(ec + 1) * 128],
                                        in_=wv[ec * 128:(ec + 1) * 128,
                                               _g * 128:(_g + 1) * 128]), 16)
                    for hl in range(4):
                        for c2 in range(2):
                            for ec in range(16):
                                col = (4 * _g + hl) * 256 + c2 * 128
                                inc(e.dma_start(
                                    out=wqst[_p][:, (hl * 2 + c2) * 2048 + ec * 128:
                                                 (hl * 2 + c2) * 2048 + (ec + 1) * 128],
                                    in_=wq[ec * 128:(ec + 1) * 128, col:col + 128]), 16)
                i_wl = add(Step("gpsimd", 16 * 176, wload, after=[i_wl_prev[gpar]]))

                for tc in range(4):
                    xpar = (g * 4 + tc) % 2

                    def xload(e, inc, _tc=tc, _p=xpar):
                        for f in range(16):
                            inc(e.dma_start(out=xns[_p][:, f * 512:(f + 1) * 512],
                                            in_=xnT_h[:, f * T + _tc * 512:
                                                      f * T + (_tc + 1) * 512]), 16)
                    i_xl = add(Step("sync", 16 * 16, xload, after=[i_xl_prev[xpar]]))

                    ots = [("k", c2, None) for c2 in range(2)] + [("v", None, None)] + \
                          [("q", hl, c2) for hl in range(4) for c2 in range(2)]
                    for kind, a1, a2 in ots:
                        bank = ot_ctr[0] % 2
                        ot_ctr[0] += 1
                        if kind == "v":
                            def mmv(e, inc, _p=gpar, _xp=xpar):
                                for ec in range(16):
                                    mm = e.matmul(psP[0][:, :] if False else psP[_b][:, :],
                                                  wvst[_p][:, ec * 128:(ec + 1) * 128],
                                                  xns[_xp][:, ec * 512:(ec + 1) * 512],
                                                  start=(ec == 0), stop=(ec == 15))
                                inc(mm)
                            def mmv2(e, inc, _p=gpar, _xp=xpar, _b=bank):
                                for ec in range(16):
                                    mm = e.matmul(psP[_b][:, :],
                                                  wvst[_p][:, ec * 128:(ec + 1) * 128],
                                                  xns[_xp][:, ec * 512:(ec + 1) * 512],
                                                  start=(ec == 0), stop=(ec == 15))
                                inc(mm)
                            i_mm = add(Step("tensor", 1, mmv2,
                                            after=[i_wl, i_xl, i_ev_bank[bank]]))
                            i_ev = add(Step("scalar", 1, lambda e, inc, _b=bank:
                                            inc(e.copy(out=vbfb[:, :], in_=psP[_b][:, :])),
                                            after=[i_mm, i_vbf[0]]))
                            i_ev_bank[bank] = i_ev
                            prev_tp = i_ev
                            for i in range(4):
                                vb = v_ctr[0] % 2
                                v_ctr[0] += 1
                                i_t = add(Step("tensor", 1, lambda e, inc, _i=i, _vb=vb:
                                               inc(e.transpose(psV[_vb][:, :],
                                                               vbfb[:, _i * 128:(_i + 1) * 128],
                                                               idn_s[:, :])),
                                               after=[prev_tp, i_vv_bank[vb]]))
                                i_e2 = add(Step("scalar", 1, lambda e, inc, _vb=vb:
                                               inc(e.copy(out=vstg[_vb][:, :], in_=psV[_vb][:, :])),
                                               after=[i_t, i_vstg_buf[vb]]))
                                i_vv_bank[vb] = i_e2
                                i_d = add(Step("gpsimd", 16, lambda e, inc, _g=g, _tc=tc, _i=i, _vb=vb:
                                               inc(e.dma_start(
                                                   out=v_h[:, _g * T + (_tc * 4 + _i) * 128:
                                                           _g * T + (_tc * 4 + _i + 1) * 128],
                                                   in_=vstg[_vb][:, :]), 16),
                                               after=[i_e2]))
                                i_vstg_buf[vb] = i_d
                            i_vbf[0] = i_t
                        else:
                            if kind == "k":
                                wsrc, wcol = wkst[gpar], a1 * 2048
                                comp = a1
                                dsth, dcol = kT_h, (g * 2 + a1) * T + tc * 512
                            else:
                                wsrc, wcol = wqst[gpar], (a1 * 2 + a2) * 2048
                                comp = a2
                                dsth, dcol = qT_h, ((4 * g + a1) * 2 + a2) * T + tc * 512
                            rpar = ot_ctr[0] % 2

                            def mmp(e, inc, _w=wsrc, _wc=wcol, _xp=xpar, _b=bank):
                                for ec in range(16):
                                    mm = e.matmul(psP[_b][:, :],
                                                  _w[:, _wc + ec * 128: _wc + (ec + 1) * 128],
                                                  xns[_xp][:, ec * 512:(ec + 1) * 512],
                                                  start=(ec == 0), stop=(ec == 15))
                                inc(mm)
                            i_mm = add(Step("tensor", 1, mmp,
                                            after=[i_wl, i_xl, i_ev_bank[bank]]))
                            i_ev = add(Step("scalar", 1, lambda e, inc, _b=bank, _r=rpar: (
                                e.copy(out=Xc[_r][:, :], in_=psP[_b][:, :]),
                                e.copy(out=Xs[_r][0:64, :], in_=psP[_b][64:128, :]),
                                inc(e.copy(out=Xs[_r][64:128, :], in_=psP[_b][0:64, :]))),
                                after=[i_mm, i_rope_buf[rpar]]))
                            i_ev_bank[bank] = i_ev
                            i_ro = add(Step("vector", 1, lambda e, inc, _c=comp, _tc=tc, _r=rpar: (
                                e.tensor_mul(m1b[_r][:, :], Xc[_r][:, :],
                                             ctab[_c][:, _tc * 512:(_tc + 1) * 512]),
                                vsync(e, e.tensor_mul(Xs[_r][:, :], Xs[_r][:, :],
                                                      stab[_c][:, _tc * 512:(_tc + 1) * 512])),
                                inc(e.tensor_add(qkst[_r][:, :], m1b[_r][:, :], Xs[_r][:, :]))),
                                after=[i_ev]))
                            i_d = add(Step("gpsimd", 16, lambda e, inc, _d=dsth, _dc=dcol, _r=rpar:
                                           inc(e.dma_start(out=_d[:, _dc:_dc + 512],
                                                           in_=qkst[_r][:, :]), 16),
                                           after=[i_ro]))
                            i_rope_buf[rpar] = i_d
                    i_xl_prev[xpar] = i_mm if True else i_xl
                i_wl_prev[gpar] = i_mm
            ch.run(steps)

        with ExitStack() as ph:
            sb = lambda n, sh, ty, **kw: ph.enter_context(nc.sbuf_tensor(n, sh, ty, **kw))
            attT = sb("attT", [128, H * T], BF16)
            ph4i = ph.enter_context(ExitStack())
            sb4 = sb
            sb = lambda n, sh, ty, **kw: ph4i.enter_context(nc.sbuf_tensor(n, sh, ty, **kw))
            kg = [sb(f"kg{i}", [128, 2 * T], BF16) for i in range(2)]
            vg = [sb(f"vg{i}", [128, T], BF16) for i in range(2)]
            qh = [sb(f"qh{i}", [128, 2 * T], BF16) for i in range(2)]
            e0b = sb("e0b", [128, 2 * T], BF16)
            e1b = sb("e1b", [128, 2 * T], BF16)
            Pf = sb("Pf", [128, T], BF16)
            Pb = sb("Pb", [128, 8 * T], BF16)
            PT = sb("PT", [128, 16 * 512], BF16, is_transpose_dest=True)
            A0 = sb("A0", [128, 8], F32)
            A1 = sb("A1", [128, 8], F32)
            Zc = sb("Zc", [128, 8], F32)
            S0 = sb("S0", [128, 2], F32)
            S1 = sb("S1", [128, 2], F32)
            SZ = sb("SZ", [128, 2], F32)
            r0 = sb("r0", [128, 2], F32)
            lr1 = sb("lr1", [128, 2], F32)
            r2 = sb("r2", [128, 2], F32)
            sqb = sb("sqb", [128, 512], F32)
            msb = sb("msb", [128, 512], F32)
            msrow = sb("msrow", [1, 512], F32)
            taub = sb("taub", [128, 512], F32)
            taurow = sb("taurow", [1, 2 * 512], F32)
            psE0 = [ph.enter_context(nc.psum_tensor(f"psE0{i}", [128, 512], F32)) for i in range(2)]
            psE1 = [ph4i.enter_context(nc.psum_tensor(f"psE1{i}", [128, 512], F32)) for i in range(2)]
            psT4 = [ph4i.enter_context(nc.psum_tensor(f"psT4{i}", [128, 128], BF16))
                    for i in range(2)]
            psAt = ph4i.enter_context(nc.psum_tensor("psAt", [128, 512], F32))
            psRo = ph4i.enter_context(nc.psum_tensor("psRo", [128, 512], F32))
            psRp = psRo

            qc_glob = [0]
            pair_glob = [0]
            tptb_ctr = [0]
            tp_evac_b = {0: -1, 1: -1}
            steps = []
            idx_exp_bank = {0: -1, 1: -1}
            idx_attmm_prev = -1
            idx_attmm_prev2 = -1
            idx_attts_prev = -1
            idx_sq_prev = -1
            idx_rowmm_prev = -1
            idx_mscp_prev = -1
            idx_msdma_prev = -1
            idx_ptp_prev = {0: [], 1: []}
            i_kvload_p = {0: -1, 1: -1}
            i_qload_p = {0: -1, 1: -1}
            kv_used = {0: [], 1: []}
            q_used = {0: [], 1: []}

            import os
            _SER = os.environ.get("PH4_SER", "")
            _PIPE = os.environ.get("PH4_PIPE")

            def add(stp, tag=""):
                if os.environ.get("PH4_SERIAL") or (tag and tag in _SER):
                    stp.after = None
                if _PIPE is not None and (not tag or tag not in _PIPE):
                    stp.after = None
                steps.append(stp)
                return len(steps) - 1

            for g in range(KVH):
                gp_ = g % 2
                for hl in range(4):
                    h = 4 * g + hl
                    hp_ = h % 2
                    last_pn = {0: -1, 1: -1}
                    last_tp = {0: -1, 1: -1}

                    if hl == 0:
                        i_kvl = add(Step("sync", 48, lambda e, inc, _g=g, _p=gp_: [
                            inc(e.dma_start(out=kg[_p][:, 0:T],
                                            in_=kT_h[:, (2 * _g) * T:(2 * _g + 1) * T]), 16),
                            inc(e.dma_start(out=kg[_p][:, T:2 * T],
                                            in_=kT_h[:, (2 * _g + 1) * T:(2 * _g + 2) * T]), 16),
                            inc(e.dma_start(out=vg[_p][:, :],
                                            in_=v_h[:, _g * T:(_g + 1) * T]), 16)],
                            after=kv_used[gp_] if kv_used[gp_] else -1))
                        i_kvload_p[gp_] = i_kvl
                        kv_used[gp_] = []
                    i_ql = add(Step("sync", 32, lambda e, inc, _h=h, _p=hp_: [
                        inc(e.dma_start(out=qh[_p][:, 0:T],
                                        in_=qT_h[:, (2 * _h) * T:(2 * _h + 1) * T]), 16),
                        inc(e.dma_start(out=qh[_p][:, T:2 * T],
                                        in_=qT_h[:, (2 * _h + 1) * T:(2 * _h + 2) * T]), 16)],
                        after=q_used[hp_] if q_used[hp_] else -1))
                    i_qload_p[hp_] = i_ql
                    q_used[hp_] = []
                    i_qload = i_ql
                    i_kvload = i_kvload_p[gp_]

                    for qc in range(4):
                        ptpar = 0
                        qc_glob[0] += 1
                        PTof = 0
                        idx_ptp_cur = []
                        idx_expP_last = -1
                        for qs in range(4):
                            qt = 4 * qc + qs
                            par = qt % 2
                            nch = qt // 4 + 1
                            idx_exp_last = -1
                            for kc in range(nch):
                                cw = 512 if kc < qt // 4 else (qt % 4 + 1) * 128
                                isdiag = (kc == qt // 4)
                                bank = pair_glob[0] % 2
                                pair_glob[0] += 1

                                def mme(e, inc, _qt=qt, _kc=kc, _cw=cw, _b=bank,
                                        _gp=gp_, _hp=hp_):
                                    e.matmul(psE0[_b][:, 0:_cw],
                                             qh[_hp][:, _qt * 128:(_qt + 1) * 128],
                                             kg[_gp][:, _kc * 512:_kc * 512 + _cw],
                                             start=True, stop=True)
                                    inc(e.matmul(psE1[_b][:, 0:_cw],
                                                 qh[_hp][:, T + _qt * 128: T + (_qt + 1) * 128],
                                                 kg[_gp][:, T + _kc * 512: T + _kc * 512 + _cw],
                                                 start=True, stop=True))
                                i_mm = add(Step("tensor", 1, mme,
                                                after=[i_qload, i_kvload,
                                                       idx_exp_bank[bank]]), tag="E")
                                kv_used[gp_].append(i_mm)
                                q_used[hp_].append(i_mm)
                                i_pre = i_mm
                                if isdiag:
                                    def trif(e, inc, _cw=cw, _b=bank):
                                        e.tensor_add(psE0[_b][:, _cw - 128:_cw],
                                                     psE0[_b][:, _cw - 128:_cw], tri_s[:, :])
                                        inc(e.tensor_add(psE1[_b][:, _cw - 128:_cw],
                                                         psE1[_b][:, _cw - 128:_cw], tri_s[:, :]))
                                    i_pre = add(Step("vector", 1, trif, after=[i_mm]), tag="E")

                                def expf(e, inc, _kc=kc, _cw=cw, _b=bank, _p=par):
                                    e.activation(e0b[:, _p * T + _kc * 512: _p * T + _kc * 512 + _cw],
                                                 psE0[_b][:, 0:_cw], AF.Exp, scale=ISQ,
                                                 accum_out=A0[:, _p * 4 + _kc:_p * 4 + _kc + 1])
                                    inc(e.activation(e1b[:, _p * T + _kc * 512: _p * T + _kc * 512 + _cw],
                                                     psE1[_b][:, 0:_cw], AF.Exp, scale=ISQ,
                                                     accum_out=A1[:, _p * 4 + _kc:_p * 4 + _kc + 1]))
                                i_exp = add(Step("scalar", 1, expf,
                                                             after=[i_pre, last_pn[par]]), tag="E")
                                idx_exp_bank[bank] = i_exp
                                idx_exp_last = i_exp

                            def rstep(e, inc, _n=nch, _h=h, _p=par):
                                e.reduce_sum(S0[:, _p:_p + 1], A0[:, _p * 4:_p * 4 + _n], axis=AX)
                                vsync(e, e.reduce_sum(S1[:, _p:_p + 1], A1[:, _p * 4:_p * 4 + _n],
                                                      axis=AX))
                                e.reciprocal(r0[:, _p:_p + 1], S0[:, _p:_p + 1])
                                vsync(e, e.reciprocal(lr1[:, _p:_p + 1], S1[:, _p:_p + 1]))
                                inc(e.tensor_mul(lr1[:, _p:_p + 1], lr1[:, _p:_p + 1],
                                                 lamrep[:, _h:_h + 1]))
                            i_r = add(Step("vector", 1, rstep,
                                           after=[idx_exp_last, last_pn[par]]), tag="R")

                            kspan = (qt + 1) * 128

                            def dif(e, inc, _ks=kspan, _qt=qt, _p=par):
                                sl = slice(_p * T, _p * T + _ks)
                                e.tensor_scalar(e1b[:, sl], e1b[:, sl], lr1[:, _p:_p + 1],
                                                None, AO.mult)
                                vsync(e, e.tensor_scalar(e0b[:, sl], e0b[:, sl],
                                                         r0[:, _p:_p + 1], None, AO.mult))
                                vsync(e, e.tensor_sub(e0b[:, sl], e0b[:, sl], e1b[:, sl]))
                                dsl = slice(_p * T + _qt * 128, _p * T + (_qt + 1) * 128)
                                inc(e.tensor_add(e0b[:, dsl], e0b[:, dsl], tri_s[:, :]))
                            i_dif = add(Step("vector", 1, dif, after=[i_r]), tag="D")

                            def expP(e, inc, _ks=kspan, _p=par):
                                inc(e.activation(Pf[:, 0:_ks], e0b[:, _p * T: _p * T + _ks],
                                                 AF.Exp,
                                                 accum_out=SZ[:, _p:_p + 1]))
                            idx_expP_last = add(Step("scalar", 1, expP,
                                                     after=[i_dif, last_pn[0], last_pn[1]]),
                                                tag="D")

                            pbslot = (qc % 2) * 4 + qs

                            def pnorm(e, inc, _qt=qt, _p=par, _sl=pbslot):
                                vsync(e, e.reciprocal(r2[:, _p:_p + 1], SZ[:, _p:_p + 1]))
                                inc(e.tensor_scalar(Pb[:, _sl * T: _sl * T + (_qt + 1) * 128],
                                                    Pf[:, 0:(_qt + 1) * 128],
                                                    r2[:, _p:_p + 1], None, AO.mult))
                            i_pn = add(Step("vector", 1, pnorm,
                                            after=[idx_expP_last, idx_attmm_prev2]), tag="P")
                            last_pn[par] = i_pn

                            n_even = (qt + 2) // 2

                            def ptp_e(e, inc, _qt=qt, _qs=qs, _of=PTof, _sl=pbslot):
                                for kt in range(0, _qt + 1, 2):
                                    inc(e.dma_start_transpose(
                                        PT[:, _of + kt * 512 + _qs * 128: _of + kt * 512 + (_qs + 1) * 128],
                                        Pb[:, _sl * T + kt * 128: _sl * T + (kt + 1) * 128]), 16)
                            i_tp = add(Step("sync", 16 * n_even, ptp_e,
                                            after=[i_pn] + idx_ptp_prev[ptpar]), tag="T")
                            idx_ptp_cur.append(i_tp)
                            last_tp[par] = i_tp
                            for kt in range(1, qt + 1, 2):
                                tb = tptb_ctr[0] % 2
                                tptb_ctr[0] += 1
                                i_tt = add(Step("tensor", 1,
                                               lambda e, inc, _kt=kt, _sl=pbslot, _tb=tb:
                                               inc(e.transpose(psT4[_tb][:, :],
                                                               Pb[:, _sl * T + _kt * 128:
                                                                  _sl * T + (_kt + 1) * 128],
                                                               idn_s[:, :])),
                                               after=[i_pn, tp_evac_b[tb]] + idx_ptp_prev[ptpar]),
                                           tag="T")
                                ee = "scalar" if (tptb_ctr[0] % 2 == 0) else "vector"

                                def tevac_s(e, inc, _kt=kt, _qs=qs, _of=PTof, _tb=tb):
                                    inc(e.copy(out=PT[:, _of + _kt * 512 + _qs * 128:
                                                      _of + _kt * 512 + (_qs + 1) * 128],
                                               in_=psT4[_tb][:, :]))

                                def tevac_v(e, inc, _kt=kt, _qs=qs, _of=PTof, _tb=tb):
                                    inc(e.tensor_copy(
                                        PT[:, _of + _kt * 512 + _qs * 128:
                                           _of + _kt * 512 + (_qs + 1) * 128],
                                        psT4[_tb][:, :]))
                                i_ev4 = add(Step(ee, 1, tevac_s if ee == "scalar" else tevac_v,
                                                 after=[i_tt]), tag="T")
                                tp_evac_b[tb] = i_ev4
                                idx_ptp_cur.append(i_ev4)
                        zblocks = [(kt, qs) for kt in range(4 * qc + 4) for qs in range(4)
                                   if 4 * qc + qs < kt]

                        def zset(e, inc, _zb=zblocks, _of=PTof):
                            last = None
                            for kt, qs in _zb:
                                last = e.memset(
                                    PT[:, _of + kt * 512 + qs * 128: _of + kt * 512 + (qs + 1) * 128], 0.0)
                            if last is None:
                                last = e.tensor_copy(S0[:, :], S0[:, :])
                            inc(last)
                        i_z = add(Step("vector", 1, zset,
                                       after=[steps.__len__() - 1] + idx_ptp_prev[ptpar]), tag="Z")

                        def attmm(e, inc, _qc=qc, _of=PTof, _gp=gp_):
                            for kt in range(4 * _qc + 4):
                                mm = e.matmul(psAt[:, :], vg[_gp][:, kt * 128:(kt + 1) * 128],
                                              PT[:, _of + kt * 512: _of + (kt + 1) * 512],
                                              start=(kt == 0), stop=(kt == 4 * _qc + 3))
                            inc(mm)
                        i_am = add(Step("tensor", 1, attmm,
                                        after=idx_ptp_cur + [i_z, idx_attts_prev, idx_sq_prev]),
                                   tag="A")
                        idx_ptp_prev[ptpar] = [i_am]

                        i_sq = add(Step("scalar", 1, lambda e, inc:
                                        inc(e.activation(sqb[:, :], psAt[:, :], AF.Square)),
                                        after=[i_am, idx_rowmm_prev]), tag="A")
                        i_row = add(Step("tensor", 1, lambda e, inc:
                                         inc(e.matmul(psRo[0:1, :], onc[:, :], sqb[:, :],
                                                      start=True, stop=True)),
                                         after=[i_sq, idx_mscp_prev]), tag="Q")
                        i_mscp = add(Step("scalar", 1, lambda e, inc:
                                          inc(e.copy(out=msrow[0:1, :], in_=psRo[0:1, :])),
                                          after=[i_row, idx_msdma_prev]), tag="Q")
                        i_msdma = add(Step("sync", 16, lambda e, inc, _h=h, _qc=qc:
                                           inc(e.dma_start(
                                               out=msb[_h * 4 + _qc:_h * 4 + _qc + 1, :],
                                               in_=msrow[0:1, :]), 16),
                                           after=[i_mscp]), tag="Q")
                        i_ts = add(Step("vector", 1, lambda e, inc, _h=h, _qc=qc:
                                        inc(e.tensor_scalar(
                                            attT[:, _h * T + _qc * 512: _h * T + (_qc + 1) * 512],
                                            psAt[:, :], rms_s[:, :], None, AO.mult)),
                                        after=[i_am, i_sq]), tag="S")
                        idx_attmm_prev2 = idx_attmm_prev
                        idx_attmm_prev = i_am
                        idx_attts_prev = i_ts
                        idx_sq_prev = i_sq
                        idx_rowmm_prev = i_row
                        idx_mscp_prev = i_mscp
                        idx_msdma_prev = i_msdma
                        kv_used[gp_].append(i_am)
            ch.run(steps)

            steps = [
                Step("scalar", 1, lambda e, inc:
                     inc(e.activation(taub[0:64, :], msb[0:64, :], AF.Sqrt,
                                      scale=1.0 / (128.0 * CR * CR), bias=eps2_s[0:64, :]))),
                Step("vector", 1, lambda e, inc:
                     inc(e.reciprocal(taub[0:64, :], taub[0:64, :]))),
            ]
            for sb8 in range(32):
                steps.append(Step("sync", 16, lambda e, inc, _s=sb8:
                                  inc(e.dma_start(
                                      out=taurow[0:1, :].rearrange("p (r c) -> p r c", r=2),
                                      in_=taub[2 * _s:2 * _s + 2, :]), 16)))
                for rr in range(2):
                    r = sb8 * 2 + rr
                    steps.append(Step("tensor", 1, lambda e, inc, _rr=rr:
                                      inc(e.matmul(psRp[:, :], onr[:, :],
                                                   taurow[0:1, _rr * 512:(_rr + 1) * 512],
                                                   start=True, stop=True))))
                    h, qc = r // 4, r % 4
                    steps.append(Step("vector", 1, lambda e, inc, _h=h, _qc=qc:
                                      inc(e.tensor_mul(
                                          attT[:, _h * T + _qc * 512: _h * T + (_qc + 1) * 512],
                                          attT[:, _h * T + _qc * 512: _h * T + (_qc + 1) * 512],
                                          psRp[:, :]))))
            ch.run(steps)
            if att_t is not None:
                ch.run([Step("gpsimd", 16, lambda e, inc:
                             inc(e.dma_start(out=att_t[:, :], in_=attT[:, :]), 16))])

            if upto < 6:
                return _finish(nc, ch)
            ph4i.close()
            sb = sb4
            wost = [sb("wost0", [128, 2048], BF16), sb("wost1", [128, 2048], BF16)]
            sstg = [sb("sstg0", [128, 512], BF16), sb("sstg1", [128, 512], BF16)]
            psS = [psE0[0], psE0[1]]
            steps = []

            def add5(stp):
                steps.append(stp)
                return len(steps) - 1

            i_wl_p = {0: -1, 1: -1}
            i_ev_p = {0: -1, 1: -1}
            i_st_p = {0: -1, 1: -1}
            mm_ctr = [0]
            i_mm_last_p = {0: -1, 1: -1}
            for ot in range(16):
                wpar = ot % 2

                def wl5(e, inc, _ot=ot, _p=wpar):
                    for ec in range(16):
                        inc(e.dma_start(out=wost[_p][:, ec * 128:(ec + 1) * 128],
                                        in_=wo[ec * 128:(ec + 1) * 128,
                                               _ot * 128:(_ot + 1) * 128]), 16)
                i_wl = add5(Step("gpsimd", 16 * 16, wl5, after=[i_mm_last_p[wpar]]))
                for tc in range(4):
                    b = mm_ctr[0] % 2
                    mm_ctr[0] += 1

                    def mmo(e, inc, _tc=tc, _p=wpar, _b=b):
                        for ec in range(16):
                            mm = e.matmul(psS[_b][:, :], wost[_p][:, ec * 128:(ec + 1) * 128],
                                          attT[:, ec * T + _tc * 512: ec * T + (_tc + 1) * 512],
                                          start=(ec == 0), stop=(ec == 15))
                        inc(mm)
                    i_mm = add5(Step("tensor", 1, mmo, after=[i_wl, i_ev_p[b]]))
                    i_ev = add5(Step("scalar", 1, lambda e, inc, _b=b, _sp=b:
                                     inc(e.copy(out=sstg[_sp][:, :], in_=psS[_b][:, :])),
                                     after=[i_mm, i_st_p[b]]))
                    i_ev_p[b] = i_ev
                    i_st = add5(Step("gpsimd", 16, lambda e, inc, _ot=ot, _tc=tc, _sp=b:
                                     inc(e.dma_start(out=s_hbm[:, _ot * T + _tc * 512:
                                                              _ot * T + (_tc + 1) * 512],
                                                     in_=sstg[_sp][:, :]), 16),
                                     after=[i_ev]))
                    i_st_p[b] = i_st
                i_mm_last_p[wpar] = i_mm
            ch.run(steps)

        with ExitStack() as ph:
            sb = lambda n, sh, ty: ph.enter_context(nc.sbuf_tensor(n, sh, ty))
            xqs = sb("xqs", [128, NE * T], BF16)
            wstg = [sb(f"wstg{i}", [128, 2048], F32) for i in range(2)]
            i8sl = [sb(f"i8sl{i}", [128, 2048], I8) for i in range(2)]
            tsl = [sb(f"tsl{i}", [128, 2048], BF16) for i in range(2)]
            rlb = [sb(f"rlb{i}", [128, 512], F32) for i in range(2)]
            h2b = [sb(f"h2b{i}", [128, 512], F32) for i in range(2)]
            psU = [ph.enter_context(nc.psum_tensor(f"psU{i}", [128, 512], F32)) for i in range(2)]
            ch.run([Step("sync", 16 * 16, lambda e, inc: [
                inc(e.dma_start(out=xqs[:, f * T:(f + 1) * T], in_=xq_h[:, f * T:(f + 1) * T]), 16)
                for f in range(16)])])
            steps = []

            def add6(stp):
                steps.append(stp)
                return len(steps) - 1

            i_i8_p = {0: -1, 1: -1}
            i_cl_p = {0: -1, 1: -1}
            i_mml_p = {0: -1, 1: -1}
            i_tt_b = {0: -1, 1: -1}
            i_dma_b = {0: -1, 1: -1}
            mm_ctr = [0]
            for o in range(NF2):
                p = o % 2

                def wl6(e, inc, _o=o, _p=p):
                    for ec in range(16):
                        inc(e.dma_start(out=wstg[_p][:, ec * 128:(ec + 1) * 128],
                                        in_=wupT[ec * 128:(ec + 1) * 128,
                                                 _o * 128:(_o + 1) * 128]), 16)
                i_wl = add6(Step("sync", 16 * 16, wl6, after=[i_i8_p[p]]))
                i_i8 = add6(Step("scalar", 1, lambda e, inc, _p=p:
                                 inc(e.activation(i8sl[_p][:, :], wstg[_p][:, :], AF.Copy,
                                                  scale=rwsu[:, :])),
                                 after=[i_wl, i_cl_p[p]]))
                i_i8_p[p] = i_i8
                i_cl = add6(Step("vector", 1, lambda e, inc, _p=p:
                                 inc(e.tensor_scalar(tsl[_p][:, :], i8sl[_p][:, :], 1.0, -1.0,
                                                     AO.min, AO.max)),
                                 after=[i_i8, i_mml_p[p]]))
                i_cl_p[p] = i_cl
                for tc in range(4):
                    b = mm_ctr[0] % 2
                    mm_ctr[0] += 1

                    def mmu(e, inc, _tc=tc, _p=p, _b=b):
                        for ec in range(16):
                            mm = e.matmul(psU[_b][:, :], tsl[_p][:, ec * 128:(ec + 1) * 128],
                                          xqs[:, ec * T + _tc * 512: ec * T + (_tc + 1) * 512],
                                          start=(ec == 0), stop=(ec == 15))
                        inc(mm)
                    i_mm = add6(Step("tensor", 1, mmu, after=[i_cl, i_tt_b[b]]))
                    i_rl = add6(Step("scalar", 1, lambda e, inc, _b=b:
                                     inc(e.activation(rlb[_b][:, :], psU[_b][:, :], AF.Relu)),
                                     after=[i_mm]))
                    i_tt = add6(Step("vector", 1, lambda e, inc, _b=b:
                                     inc(e.tensor_mul(h2b[_b][:, :], rlb[_b][:, :], psU[_b][:, :])),
                                     after=[i_rl, i_dma_b[b]]))
                    i_tt_b[b] = i_tt
                    i_dma = add6(Step("gpsimd", 16, lambda e, inc, _o=o, _tc=tc, _b=b:
                                      inc(e.dma_start(out=h2_h[:, _o * T + _tc * 512:
                                                               _o * T + (_tc + 1) * 512],
                                                      in_=h2b[_b][:, :]), 16),
                                      after=[i_tt]))
                    i_dma_b[b] = i_dma
                i_mml_p[p] = i_mm
            ch.run(steps)

        with ExitStack() as ph:
            sb = lambda n, sh, ty: ph.enter_context(nc.sbuf_tensor(n, sh, ty))
            K2 = sb("K2", [128, NF2 * T], BF16)
            ph7 = ph.enter_context(ExitStack())
            sb7 = lambda n, sh, ty: ph7.enter_context(nc.sbuf_tensor(n, sh, ty))
            W = [sb7(f"W2f{i}", [128, 256], F32) for i in range(48)]
            mx = sb7("mx2", [128, 256], F32)
            mn = sb7("mn2", [128, 256], F32)
            ptmp = sb7("ptmp2", [64, 256], F32)
            Sb2 = sb7("Sb22", [128, 256], F32)
            qi8 = sb7("qi82", [128, 256], I8)
            mrow2 = sb7("mrow2", [1, 256], F32)
            psH = ph7.enter_context(nc.psum_tensor("psH2", [128, 256], F32))
            psR = ph7.enter_context(nc.psum_tensor("psR2", [128, 256], F32))


            def butterflies2(e, inc, nt=32):
                slot = list(range(nt))
                free = list(range(nt, nt + nt // 2))
                for p in range(5):
                    b = 1 << p
                    pairs = [(i, i | b) for i in range(nt) if not (i & b)]
                    last = None
                    for idx, (i, j) in enumerate(pairs):
                        last = e.tensor_sub(W[free[idx]][:, :], W[slot[i]][:, :],
                                            W[slot[j]][:, :])
                    vsync(e, last)
                    last = None
                    for i, j in pairs:
                        last = e.tensor_add(W[slot[i]][:, :], W[slot[i]][:, :],
                                            W[slot[j]][:, :])
                    vsync(e, last)
                    for idx, (i, j) in enumerate(pairs):
                        slot[j], free[idx] = free[idx], slot[j]
                for op, dstbuf, neg in ((AO.max, mx, False), (AO.min, mn, True)):
                    last = None
                    for t in range(nt // 2):
                        last = e.tensor_tensor(W[free[t]][:, :], W[slot[2 * t]][:, :],
                                               W[slot[2 * t + 1]][:, :], op)
                    vsync(e, last)
                    gap = 1
                    while gap < nt // 2:
                        last = None
                        for t in range(0, nt // 2, 2 * gap):
                            last = e.tensor_tensor(W[free[t]][:, :], W[free[t]][:, :],
                                                   W[free[t + gap]][:, :], op)
                        vsync(e, last)
                        gap *= 2
                    if neg:
                        vsync(e, e.tensor_scalar_mul(dstbuf[:, :], W[free[0]][:, :], -1.0))
                    else:
                        vsync(e, e.tensor_copy(dstbuf[:, :], W[free[0]][:, :]))
                inc(e.tensor_max(mx[:, :], mx[:, :], mn[:, :]))
                return slot

            ld_prev = [-1]
            mm_prev_b = {0: -1, 1: -1}
            ev_prev = [-1]
            qt_prev = [-1]
            srep_prev = [-1]
            steps = []

            def add7(stp):
                steps.append(stp)
                return len(steps) - 1

            for cc in range(8):
                def ld7(e, inc, _cc=cc):
                    for o in range(NF2):
                        inc(e.dma_start(out=W[o][:, :],
                                        in_=h2_h[:, o * T + _cc * 256:
                                                 o * T + (_cc + 1) * 256]), 16)
                i_ld = add7(Step("sync", 16 * 32, ld7, after=[qt_prev[0]]))
                i_ev_last = -1
                for o in range(NF2):
                    b = o % 2
                    i_mm = add7(Step("tensor", 1, lambda e, inc, _o=o, _b=b:
                                     inc(e.matmul(psH[:, 0:256] if False else
                                                  (psH if _b == 0 else psR)[:, :],
                                                  h128f_s[:, :], W[_o][:, :],
                                                  start=True, stop=True)),
                                     after=[i_ld, mm_prev_b[b]]))
                    i_ev = add7(Step("scalar", 1, lambda e, inc, _o=o, _b=b:
                                     inc(e.copy(out=W[_o][:, :],
                                                in_=(psH if _b == 0 else psR)[:, :])),
                                     after=[i_mm]))
                    mm_prev_b[b] = i_ev
                    i_ev_last = i_ev
                sstore = {}
                i_bf = add7(Step("vector", 1,
                                 lambda e, inc: sstore.__setitem__("s", butterflies2(e, inc)),
                                 after=[i_ev_last]))
                i_prev = i_bf
                for r in range(7):
                    p = 64 >> r
                    i_d = add7(Step("sync", 16, lambda e, inc, _p=p:
                                    inc(e.dma_start(out=ptmp[0:_p, :], in_=mx[_p:2 * _p, :]), 16),
                                    after=[i_prev]))
                    i_prev = add7(Step("vector", 1, lambda e, inc, _p=p:
                                       inc(e.tensor_max(mx[0:_p, :], mx[0:_p, :], ptmp[0:_p, :])),
                                       after=[i_d]))
                i_bt = add7(Step("vector", 1, lambda e, inc, _cc=cc: (
                    vsync(e, e.tensor_scalar_max(mrow2[0:1, :], mx[0:1, :], 1e-5)),
                    vsync(e, e.tensor_mul(betar[0:1, _cc * 256:(_cc + 1) * 256],
                                          mrow1[0:1, _cc * 256:(_cc + 1) * 256],
                                          mrow1[0:1, _cc * 256:(_cc + 1) * 256])),
                    vsync(e, e.tensor_mul(betar[0:1, _cc * 256:(_cc + 1) * 256],
                                          betar[0:1, _cc * 256:(_cc + 1) * 256], mrow2[0:1, :])),
                    inc(e.tensor_scalar(betar[0:1, _cc * 256:(_cc + 1) * 256],
                                        betar[0:1, _cc * 256:(_cc + 1) * 256],
                                        combo[0:1, :], None, AO.mult))),
                    after=[i_prev, srep_prev[0]]))
                i_rp = add7(Step("tensor", 1, lambda e, inc:
                                 inc(e.matmul(psR[:, 0:256], onr[:, :], mrow2[0:1, :],
                                              start=True, stop=True)),
                                 after=[i_bt]))
                i_sr = add7(Step("vector", 1, lambda e, inc: (
                    vsync(e, e.tensor_scalar_mul(Sb2[:, :], psR[:, 0:256], 1.0 / 127.0)),
                    inc(e.reciprocal(Sb2[:, :], Sb2[:, :]))),
                    after=[i_rp]))
                srep_prev[0] = i_sr
                i_q_last = -1
                for o in range(NF2):
                    i_q = add7(Step("vector", 1, lambda e, inc, _o=o, _cc=cc: (
                        vsync(e, e.tensor_mul(qi8[:, :], W[sstore["s"][_o]][:, :], Sb2[:, :])),
                        inc(e.tensor_copy(K2[:, _o * T + _cc * 256: _o * T + (_cc + 1) * 256],
                                          qi8[:, :]))),
                        after=[i_sr] if o == 0 else None))
                    i_q_last = i_q
                qt_prev[0] = i_q_last
            ch.run(steps)
            if k2_t is not None:
                ch.run([Step("gpsimd", 16 * 8, lambda e, inc: [
                    inc(e.dma_start(out=k2_t[:, i * 8192:(i + 1) * 8192],
                                    in_=K2[:, i * 8192:(i + 1) * 8192]), 16) for i in range(8)])])

            ph7.close()

            if upto < 9:
                return _finish(nc, ch)
            brep = sb("brep", [128, T], F32)
            brepp = ph.enter_context(nc.psum_tensor("psB0", [128, 512], F32))
            steps = []
            for tc in range(4):
                steps.append(Step("tensor", 1, lambda e, inc, _tc=tc:
                                  inc(e.matmul(brepp[:, :], onr[:, :],
                                               betar[0:1, _tc * 512:(_tc + 1) * 512],
                                               start=True, stop=True))))
                steps.append(Step("scalar", 1, lambda e, inc, _tc=tc:
                                  inc(e.copy(out=brep[:, _tc * 512:(_tc + 1) * 512],
                                             in_=brepp[:, :]))))
            ch.run(steps)

            wst2 = sb("wst2", [128, 4096], F32)
            i8s2 = sb("i8s2", [128, 4096], I8)
            tsl2 = [sb(f"tsl2{i}", [128, 4096], BF16) for i in range(2)]
            ub = [sb(f"ub{i}", [128, 512], F32) for i in range(2)]
            gb = ub
            stile = [sb(f"stile{i}", [128, 512], BF16) for i in range(2)]
            sgb = [sb(f"sgb{i}", [128, 512], BF16) for i in range(2)]
            xblk4 = [sb(f"xblk4{i}", [128, 512], F32) for i in range(2)]
            obuf4 = [sb(f"obuf4{i}", [128, 512], F32) for i in range(2)]
            psD = [ph.enter_context(nc.psum_tensor(f"psD{i}", [128, 512], F32)) for i in range(2)]
            psX = [ph.enter_context(nc.psum_tensor(f"psX{i}", [128, 128], BF16)) for i in range(2)]
            steps = []

            def add8(stp):
                steps.append(stp)
                return len(steps) - 1

            i_i8_l = [-1]
            i_cl_p = {0: -1, 1: -1}
            i_mml_p = {0: -1, 1: -1}
            i_u_b = {0: -1, 1: -1}
            i_sg_p = {0: -1, 1: -1}
            i_sl_p = {0: -1, 1: -1}
            i_xl_p = {0: -1, 1: -1}
            i_tp_b = {0: -1, 1: -1}
            i_xa_b = {0: -1, 1: -1}
            i_ob_p = {0: -1, 1: -1}
            i_od_p = {0: -1, 1: -1}
            mm_ctr = [0]
            tp_ctr = [0]
            for o2 in range(16):
                p = o2 % 2

                def wl8(e, inc, _o2=o2):
                    for ec in range(32):
                        inc(e.dma_start(out=wst2[:, ec * 128:(ec + 1) * 128],
                                        in_=wdnT[ec * 128:(ec + 1) * 128,
                                                 _o2 * 128:(_o2 + 1) * 128]), 16)
                i_wl = add8(Step("sync", 16 * 32, wl8, after=[i_i8_l[0]]))
                i_i8 = add8(Step("scalar", 1, lambda e, inc:
                                 inc(e.activation(i8s2[:, :], wst2[:, :], AF.Copy,
                                                  scale=rwsd[:, :])),
                                 after=[i_wl, i_cl_p[1 - p]]))
                i_i8_l[0] = i_i8
                i_cl = add8(Step("vector", 1, lambda e, inc, _p=p:
                                 inc(e.tensor_scalar(tsl2[_p][:, :], i8s2[:, :], 1.0, -1.0,
                                                     AO.min, AO.max)),
                                 after=[i_i8, i_mml_p[p]]))
                i_cl_p[p] = i_cl
                for tc in range(4):
                    b = mm_ctr[0] % 2
                    mm_ctr[0] += 1
                    i_sl = add8(Step("gpsimd", 16, lambda e, inc, _o2=o2, _tc=tc, _b=b:
                                     inc(e.dma_start(out=stile[_b][:, :],
                                                     in_=s_hbm[:, _o2 * T + _tc * 512:
                                                              _o2 * T + (_tc + 1) * 512]), 16),
                                     after=[i_sg_p[b]]))
                    i_xl = add8(Step("gpsimd", 16, lambda e, inc, _o2=o2, _tc=tc, _b=b:
                                     inc(e.dma_start(
                                         out=xblk4[_b][:, :].rearrange("p (i c) -> p i c", i=4),
                                         in_=x[_tc * 512:(_tc + 1) * 512,
                                               _o2 * 128:(_o2 + 1) * 128]
                                         .rearrange("(i p) c -> p i c", i=4)), 16),
                                     after=[i_xa_b[b]]))

                    def mmd(e, inc, _tc=tc, _p=p, _b=b):
                        for ec in range(32):
                            mm = e.matmul(psD[_b][:, :], tsl2[_p][:, ec * 128:(ec + 1) * 128],
                                          K2[:, ec * T + _tc * 512: ec * T + (_tc + 1) * 512],
                                          start=(ec == 0), stop=(ec == 31))
                        inc(mm)
                    i_mm = add8(Step("tensor", 1, mmd, after=[i_cl, i_u_b[b]]))
                    i_u = add8(Step("vector", 1, lambda e, inc, _tc=tc, _b=b:
                                    inc(e.tensor_mul(ub[_b][:, :], psD[_b][:, :],
                                                     brep[:, _tc * 512:(_tc + 1) * 512])),
                                    after=[i_mm, i_sg_p[b]]))
                    i_u_b[b] = i_u
                    i_g = add8(Step("scalar", 1, lambda e, inc, _b=b:
                                    inc(e.activation(gb[_b][:, :], ub[_b][:, :], AF.Silu)),
                                    after=[i_u]))
                    i_sg = add8(Step("vector", 1, lambda e, inc, _b=b:
                                     inc(e.tensor_mul(sgb[_b][:, :], gb[_b][:, :],
                                                      stile[_b][:, :])),
                                     after=[i_g, i_sl]))
                    i_sg_p[b] = i_sg
                    i_xa_last = -1
                    for i in range(4):
                        xb = tp_ctr[0] % 2
                        tp_ctr[0] += 1
                        i_t = add8(Step("tensor", 1, lambda e, inc, _i=i, _xb=xb, _b=b:
                                        inc(e.transpose(psX[_xb][:, :],
                                                        sgb[_b][:, _i * 128:(_i + 1) * 128],
                                                        idn_s[:, :])),
                                        after=[i_sg, i_tp_b[xb]]))
                        i_xa = add8(Step("vector", 1, lambda e, inc, _i=i, _xb=xb, _b=b:
                                         inc(e.tensor_add(obuf4[_b][:, _i * 128:(_i + 1) * 128],
                                                          psX[_xb][:, :],
                                                          xblk4[_b][:, _i * 128:(_i + 1) * 128])),
                                         after=[i_t, i_xl, i_od_p[b]]))
                        i_tp_b[xb] = i_xa
                        i_xa_last = i_xa
                    i_xa_b[b] = i_xa_last
                    i_od = add8(Step("gpsimd", 16, lambda e, inc, _o2=o2, _tc=tc, _b=b:
                                     inc(e.dma_start(
                                         out=out[_tc * 512:(_tc + 1) * 512,
                                                 _o2 * 128:(_o2 + 1) * 128]
                                         .rearrange("(i p) c -> p i c", i=4),
                                         in_=obuf4[_b][:, :].rearrange("p (i c) -> p i c", i=4)), 16),
                                     after=[i_xa_last]))
                    i_od_p[b] = i_od
                i_mml_p[p] = i_mm
            ch.run(steps)

        return _finish(nc, ch)


def _finish(nc, ch):
    fin = dict(ch.cnt)
    sems = dict(ch.sems)
    with nc.Block() as blk:
        @blk.gpsimd
        def _(gp):
            for ln, c in fin.items():
                if c > 0:
                    gp.wait_ge(sems[ln], c)
    return nc


def _prep_inputs(L0_attn_x, L0_attn_Wq, L0_attn_Wk, L0_attn_Wv, L0_attn_Wo,
                 L0_attn_lq1, L0_attn_lq2, L0_attn_lk1, L0_attn_lk2,
                 L0_attn_rms_w, L0_ffn_w_up, L0_ffn_w_down):
    cons = _host_consts()
    f32 = np.float32
    wq = np.ascontiguousarray(np.asarray(L0_attn_Wq, dtype=f32)[:, _perm_cols(H)])
    wk = np.ascontiguousarray(np.asarray(L0_attn_Wk, dtype=f32)[:, _perm_cols(KVH)])
    wv = np.ascontiguousarray(np.asarray(L0_attn_Wv, dtype=f32))
    wo = np.ascontiguousarray(np.asarray(L0_attn_Wo, dtype=f32))
    wupT = np.ascontiguousarray(np.asarray(L0_ffn_w_up, dtype=f32).T)
    wdnT = np.ascontiguousarray(np.asarray(L0_ffn_w_down, dtype=f32).T)
    base = {
        "wq": wq, "wk": wk, "wv": wv, "wo": wo, "wupT": wupT, "wdnT": wdnT,
        "lq1": np.ascontiguousarray(np.asarray(L0_attn_lq1, dtype=f32).T),
        "lq2": np.ascontiguousarray(np.asarray(L0_attn_lq2, dtype=f32).T),
        "lk1": np.ascontiguousarray(np.asarray(L0_attn_lk1, dtype=f32).T),
        "lk2": np.ascontiguousarray(np.asarray(L0_attn_lk2, dtype=f32).T),
        "rmsw": np.ascontiguousarray(np.asarray(L0_attn_rms_w, dtype=f32).reshape(128, 1)),
        "c_lo": cons["c_lo"], "s_lo": cons["s_lo"],
        "c_hi": cons["c_hi"], "s_hi": cons["s_hi"],
        "trid": cons["tri"], "h128bd": cons["h128b"], "h128fd": cons["h128f"],
        "idnd": cons["idn"], "ones_rd": cons["ones_r"], "ones_cd": cons["ones_c"],
        "eps1d": np.full((128, 1), 1e-5, dtype=f32),
        "eps2d": np.full((128, 1), 1e-8 / (CR * CR), dtype=f32),
        "ohd": (np.arange(127)[None, :] == 63).astype(f32) * np.ones((128, 1), f32),
    }
    xs = np.asarray(L0_attn_x, dtype=f32)
    in_maps = []
    for b in range(8):
        m = dict(base)
        m["x"] = np.ascontiguousarray(xs[b])
        in_maps.append(m)
    return in_maps


def kernel(L0_attn_x, L0_attn_Wq, L0_attn_Wk, L0_attn_Wv, L0_attn_Wo,
           L0_attn_lq1, L0_attn_lq2, L0_attn_lk1, L0_attn_lk2,
           L0_attn_rms_w, L0_ffn_w_up, L0_ffn_w_down):
    in_maps = _prep_inputs(L0_attn_x, L0_attn_Wq, L0_attn_Wk, L0_attn_Wv, L0_attn_Wo,
                           L0_attn_lq1, L0_attn_lq2, L0_attn_lk1, L0_attn_lk2,
                           L0_attn_rms_w, L0_ffn_w_up, L0_ffn_w_down)
    nc = build_program()
    res = run_bass_kernel_spmd(nc, in_maps, core_ids=list(range(8)))
    return np.stack([res.results[b]["out"] for b in range(8)], axis=0)


# version 19
# speedup vs baseline: 1.0424x; 1.0178x over previous
import numpy as np
import ml_dtypes
import concourse.bass as bass
import concourse.mybir as mybir
from concourse.bass_utils import run_bass_kernel_spmd
from concourse.alu_op_type import AluOpType as AO
from contextlib import ExitStack

F32 = mybir.dt.float32
BF16 = mybir.dt.bfloat16
I8 = mybir.dt.int8
AF = mybir.ActivationFunctionType
AX = mybir.AxisListType.X

T = 2048
E = 2048
H = 16
KVH = 4
NEG = -1e30
LAM = 0.8 - 0.6 * float(np.exp(-0.3 * 0))
CR = 1.0 - LAM
IFF = 4096
NE = E // 128
NF2 = IFF // 128
NW_ELEM = float(E * IFF)
C0 = 1.0 / (127.0 ** 3 * 2048.0 * 64.0)
ISQ = float(1.0 / np.sqrt(128.0))


def _host_consts():
    th = 10000.0 ** (-np.arange(128, dtype=np.float64) / 128.0)
    ang = th[:, None] * np.arange(T, dtype=np.float64)[None, :]
    cos, sin = np.cos(ang), np.sin(ang)
    c_lo = np.concatenate([cos[0:64], cos[0:64]], 0).astype(np.float32)
    s_lo = np.concatenate([-sin[0:64], sin[0:64]], 0).astype(np.float32)
    c_hi = np.concatenate([cos[64:128], cos[64:128]], 0).astype(np.float32)
    s_hi = np.concatenate([-sin[64:128], sin[64:128]], 0).astype(np.float32)
    tri = np.where(np.arange(128)[None, :] > np.arange(128)[:, None],
                   np.float32(NEG), np.float32(0.0)).astype(np.float32)
    ii, jj = np.meshgrid(np.arange(128), np.arange(128), indexing="ij")
    v = (ii & jj).astype(np.int64)
    par = np.zeros((128, 128), dtype=np.int64)
    while v.max() > 0:
        par ^= (v & 1)
        v >>= 1
    h128 = np.where(par == 0, 1.0, -1.0)
    return {"c_lo": c_lo, "s_lo": s_lo, "c_hi": c_hi, "s_hi": s_hi, "tri": tri,
            "h128b": h128.astype(ml_dtypes.bfloat16),
            "h128f": h128.astype(np.float32),
            "idn": np.eye(128).astype(ml_dtypes.bfloat16),
            "ones_r": np.ones((1, 128), dtype=np.float32),
            "ones_c": np.ones((128, 1), dtype=np.float32)}


def _perm_cols(nheads):
    cols = []
    for h in range(nheads):
        b = h * 256
        cols += [b + 2 * j for j in range(64)] + [b + 2 * j + 1 for j in range(64)]
        cols += [b + 2 * j for j in range(64, 128)] + [b + 2 * j + 1 for j in range(64, 128)]
    return np.array(cols)


class Step:
    __slots__ = ("eng", "inc", "fn", "after", "lane", "_lane")

    def __init__(self, eng, inc, fn, after=None, lane=None):
        self.eng, self.inc, self.fn, self.after, self.lane = eng, inc, fn, after, lane


class Ch:
    ENG_LANE = {"tensor": "te", "vector": "ve", "scalar": "sc",
                "sync": "sy", "gpsimd": "gd"}

    def __init__(self, nc, mksem):
        self.nc = nc
        self.mksem = mksem
        self.sems = {}
        self.cnt = {}

    def lane(self, name):
        if name not in self.sems:
            self.sems[name] = self.mksem(name)
            self.cnt[name] = 0
        return self.sems[name]

    def run(self, steps):
        nc = self.nc
        lanes = []
        for s in steps:
            s._lane = s.lane if getattr(s, "lane", None) else self.ENG_LANE[s.eng]
            self.lane(s._lane)
            lanes.append(s._lane)
        base = dict(self.cnt)
        run_cnt = dict(self.cnt)
        post = []
        for s in steps:
            run_cnt[s._lane] += s.inc
            post.append((s._lane, run_cnt[s._lane]))
        waited = set()
        def _preds(i, s):
            a = s.after
            if a is None:
                return [i - 1] if i > 0 else []
            if isinstance(a, int):
                return [a] if a >= 0 else []
            return [j for j in a if j >= 0]
        for i, s in enumerate(steps):
            pr = _preds(i, s)
            if not pr:
                continue
            best = {}
            for j in pr:
                ln, c = post[j]
                if c > best.get(ln, (-1, -1))[0]:
                    best[ln] = (c, j)
            for ln, (c, j) in best.items():
                waited.add(j)
        engs = []
        for e in ("sync", "tensor", "vector", "scalar", "gpsimd"):
            if any(s.eng == e for s in steps):
                engs.append(e)
        with nc.Block() as blk:
            for ename in engs:
                def body(eng, _e=ename):
                    for i, s in enumerate(steps):
                        if s.eng != _e:
                            continue
                        preds = _preds(i, s)
                        if preds:
                            tg = {}
                            for j in preds:
                                ln, c = post[j]
                                tg[ln] = max(tg.get(ln, 0), c)
                            for ln, c in tg.items():
                                eng.wait_ge(self.sems[ln], c)
                        else:
                            for ln, c in base.items():
                                if c > 0:
                                    eng.wait_ge(self.sems[ln], c)
                        mysem = self.sems[s._lane]
                        got = {"n": 0}

                        def inc(instr, k=1, _g=got, _s=mysem):
                            _g["n"] += k
                            return instr.then_inc(_s, k)
                        s.fn(eng, inc)
                        assert got["n"] == s.inc, \
                            f"step {i} ({s.eng}): declared inc {s.inc} != emitted {got['n']}"
                        if s.eng in ("sync", "gpsimd") and i in waited:
                            eng.wait_ge(mysem, post[i][1])
                getattr(blk, ename)(body)
        self.cnt = run_cnt


def build_program(taps=(), upto=99):
    nc = bass.Bass("TRN2", target_bir_lowering=False)
    dt = lambda n, sh, ty, kind: nc.dram_tensor(n, sh, ty, kind=kind).ap()
    tk = lambda n: "ExternalOutput" if n in taps else "Internal"

    x = dt("x", [T, E], F32, "ExternalInput")
    wq = dt("wq", [E, H * 256], F32, "ExternalInput")
    wk = dt("wk", [E, KVH * 256], F32, "ExternalInput")
    wv = dt("wv", [E, KVH * 128], F32, "ExternalInput")
    wo = dt("wo", [E, E], F32, "ExternalInput")
    wupT = dt("wupT", [E, IFF], F32, "ExternalInput")
    wdnT = dt("wdnT", [IFF, E], F32, "ExternalInput")
    lq1d = dt("lq1", [128, 16], F32, "ExternalInput")
    lq2d = dt("lq2", [128, 16], F32, "ExternalInput")
    lk1d = dt("lk1", [128, 16], F32, "ExternalInput")
    lk2d = dt("lk2", [128, 16], F32, "ExternalInput")
    rmswd = dt("rmsw", [128, 1], F32, "ExternalInput")
    c_lod = dt("c_lo", [128, T], F32, "ExternalInput")
    s_lod = dt("s_lo", [128, T], F32, "ExternalInput")
    c_hid = dt("c_hi", [128, T], F32, "ExternalInput")
    s_hid = dt("s_hi", [128, T], F32, "ExternalInput")
    trid = dt("trid", [128, 128], F32, "ExternalInput")
    h128bd = dt("h128bd", [128, 128], BF16, "ExternalInput")
    h128fd = dt("h128fd", [128, 128], F32, "ExternalInput")
    idnd = dt("idnd", [128, 128], BF16, "ExternalInput")
    ones_rd = dt("ones_rd", [1, 128], F32, "ExternalInput")
    ones_cd = dt("ones_cd", [128, 1], F32, "ExternalInput")
    eps1d = dt("eps1d", [128, 1], F32, "ExternalInput")
    eps2d = dt("eps2d", [128, 1], F32, "ExternalInput")
    ohd = dt("ohd", [128, 127], F32, "ExternalInput")
    out = dt("out", [T, E], F32, "ExternalOutput")

    xnT_h = dt("xnT_h", [128, NE * T], BF16, tk("xnT_h"))
    xq_h = dt("xq_h", [128, NE * T], BF16, tk("xq_h"))
    qT_h = dt("qT_h", [128, H * 2 * T], BF16, tk("qT_h"))
    kT_h = dt("kT_h", [128, KVH * 2 * T], BF16, tk("kT_h"))
    v_h = dt("v_h", [128, KVH * T], BF16, tk("v_h"))
    s_hbm = dt("s_hbm", [128, NE * T], BF16, tk("s_hbm"))
    h2_h = dt("h2_h", [128, NF2 * T], F32, tk("h2_h"))
    att_t = dt("att_t", [128, H * T], BF16, "ExternalOutput") if "att_t" in taps else None
    k2_t = dt("k2_t", [128, NF2 * T], BF16, "ExternalOutput") if "k2_t" in taps else None
    aux_t = dt("aux_t", [8, 4096], F32, "ExternalOutput") if "aux_t" in taps else None

    st = ExitStack()
    with st:
        sbuf = lambda n, sh, ty, **kw: st.enter_context(nc.sbuf_tensor(n, sh, ty, **kw))
        ch = Ch(nc, lambda n: st.enter_context(nc.semaphore("L_" + n)))
        vssem = st.enter_context(nc.semaphore("vs"))
        vs_n = [0]

        def vsync(e, instr):
            instr.then_inc(vssem, 1)
            vs_n[0] += 1
            e.wait_ge(vssem, vs_n[0])
            return instr

        lamrep = sbuf("lamrep", [128, 16], F32)
        rms_s = sbuf("rms_s", [128, 1], F32)
        rwsu = sbuf("rwsu", [128, 1], F32)
        rwsd = sbuf("rwsd", [128, 1], F32)
        wsu_s = sbuf("wsu_s", [1, 1], F32)
        wsd_s = sbuf("wsd_s", [1, 1], F32)
        combo = sbuf("combo", [1, 1], F32)
        mrow1 = sbuf("mrow1", [1, T], F32)
        betar = sbuf("betar", [1, T], F32)
        tri_s = sbuf("tri_s", [128, 128], F32)
        h128b_s = sbuf("h128b_s", [128, 128], BF16)
        h128f_s = sbuf("h128f_s", [128, 128], F32)
        idn_s = sbuf("idn_s", [128, 128], BF16)
        onr = sbuf("onr", [1, 128], F32)
        onc = sbuf("onc", [128, 1], F32)
        eps1_s = sbuf("eps1_s", [128, 1], F32)
        eps2_s = sbuf("eps2_s", [128, 1], F32)
        oh_s = sbuf("oh_s", [128, 127], F32)

        with ExitStack() as ph:
            sb = lambda n, sh, ty: ph.enter_context(nc.sbuf_tensor(n, sh, ty))
            pt = lambda n, sh, ty=F32: ph.enter_context(nc.psum_tensor(n, sh, ty))
            l1, l2 = sb("l1", [128, 16], F32), sb("l2", [128, 16], F32)
            l3, l4 = sb("l3", [128, 16], F32), sb("l4", [128, 16], F32)
            p0, p1, p2 = pt("p0", [128, 16]), pt("p1", [128, 16]), pt("p2", [128, 16])
            row1, row2 = sb("row1", [1, 16], F32), sb("row2", [1, 16], F32)
            lamrow = sb("lamrow", [1, 16], F32)

            loads = [(tri_s, trid), (h128b_s, h128bd), (h128f_s, h128fd),
                     (idn_s, idnd), (onr, ones_rd), (onc, ones_cd),
                     (rms_s, rmswd), (l1, lq1d), (l2, lq2d), (l3, lk1d), (l4, lk2d),
                     (eps1_s, eps1d), (eps2_s, eps2d), (oh_s, ohd)]
            steps = [Step("sync", 16 * len(loads),
                          lambda e, inc, _l=loads: [inc(e.dma_start(out=d[:, :], in_=s[:, :]), 16)
                                                    for d, s in _l])]
            steps += [
                Step("vector", 1, lambda e, inc: (
                    e.tensor_mul(l1[:, :], l1[:, :], l3[:, :]),
                    inc(e.tensor_mul(l2[:, :], l2[:, :], l4[:, :])))),
                Step("tensor", 1, lambda e, inc: (
                    e.matmul(p0[0:1, :], onc[:, :], l1[:, :], start=True, stop=True),
                    inc(e.matmul(p1[0:1, :], onc[:, :], l2[:, :], start=True, stop=True)))),
                Step("scalar", 1, lambda e, inc: (
                    e.activation(row1[:, :], p0[0:1, :], AF.Exp),
                    inc(e.activation(row2[:, :], p1[0:1, :], AF.Exp)))),
                Step("vector", 1, lambda e, inc: (
                    vsync(e, e.tensor_sub(lamrow[:, :], row1[:, :], row2[:, :])),
                    inc(e.tensor_scalar_add(lamrow[:, :], lamrow[:, :], float(LAM))))),
                Step("tensor", 1, lambda e, inc:
                     inc(e.matmul(p2[:, :], onr[:, :], lamrow[:, :], start=True, stop=True))),
                Step("scalar", 1, lambda e, inc:
                     inc(e.copy(out=lamrep[:, :], in_=p2[:, :]))),
            ]
            ch.run(steps)

        if upto < 1:
            return _finish(nc, ch)
        with ExitStack() as ph:
            sb = lambda n, sh, ty: ph.enter_context(nc.sbuf_tensor(n, sh, ty))
            pt = lambda n, sh: ph.enter_context(nc.psum_tensor(n, sh, F32))
            wst = [sb(f"wst{i}", [128, 2048], F32) for i in range(2)]
            acc = sb("acc", [128, 64], F32)
            ucol, dcol = sb("ucol", [128, 1], F32), sb("dcol", [128, 1], F32)
            rtmp = sb("rtmp", [1, 1], F32)
            pA, pC = pt("pA", [128, 1]), pt("pC", [128, 1])
            tiles = [(wupT, r, c) for r in range(16) for c in range(2)] + \
                    [(wdnT, r, 0) for r in range(32)]
            steps = []
            ld_idx = {}
    
            for i, (w, r, c) in enumerate(tiles):
                eng = "sync" if i % 2 == 0 else "gpsimd"
                prev_red = 2 * (i - 2) + 1 if i >= 2 else -1
                ld_idx[i] = len(steps)
                steps.append(Step(eng, 16,
                                  lambda e, inc, _w=w, _r=r, _c=c, _i=i:
                                  inc(e.dma_start(out=wst[_i % 2][:, :],
                                                  in_=_w[_r * 128:(_r + 1) * 128,
                                                         _c * 2048:(_c + 1) * 2048]), 16),
                                  after=[prev_red] if prev_red >= 0 else -1))
                steps.append(Step("vector", 1,
                                  lambda e, inc, _i=i:
                                  inc(e.reduce_sum(acc[:, _i:_i + 1], wst[_i % 2][:, :],
                                                   axis=AX, apply_absolute_value=True)),
                                  after=[ld_idx[i]]))
            for name, lo, hi, col, wsx, rws in [("u", 0, 32, ucol, wsu_s, rwsu),
                                                ("d", 32, 64, dcol, wsd_s, rwsd)]:
                steps += [
                    Step("vector", 1, lambda e, inc, _lo=lo, _hi=hi, _col=col:
                         inc(e.reduce_sum(_col[:, :], acc[:, _lo:_hi], axis=AX))),
                    Step("tensor", 1, lambda e, inc, _col=col:
                         inc(e.matmul(pA[0:1, :], onc[:, :], _col[:, :], start=True, stop=True))),
                    Step("vector", 1, lambda e, inc, _w=wsx:
                         inc(e.tensor_scalar(_w[:, :], pA[0:1, :], 1.0 / NW_ELEM, 1e-5,
                                             AO.mult, AO.max))),
                    Step("vector", 1, lambda e, inc, _w=wsx:
                         inc(e.reciprocal(rtmp[:, :], _w[:, :]))),
                    Step("tensor", 1, lambda e, inc:
                         inc(e.matmul(pC[:, :], onr[:, :], rtmp[:, :], start=True, stop=True))),
                    Step("scalar", 1, lambda e, inc, _r=rws:
                         inc(e.copy(out=_r[:, :], in_=pC[:, :]))),
                ]
            steps.append(Step("vector", 1, lambda e, inc: (
                vsync(e, e.tensor_mul(combo[:, :], wsu_s[:, :], wsu_s[:, :])),
                vsync(e, e.tensor_mul(combo[:, :], combo[:, :], wsd_s[:, :])),
                inc(e.tensor_scalar_mul(combo[:, :], combo[:, :], float(C0))))))
            ch.run(steps)

        if upto < 2:
            return _finish(nc, ch)
        with ExitStack() as ph:
            sb = lambda n, sh, ty: ph.enter_context(nc.sbuf_tensor(n, sh, ty))
            xt = sb("xt", [128, 2048], F32)
            dump = sb("dump", [128, 2048], F32)
            sqc = sb("sqc", [128, 1], F32)
            sumc = sb("sumc", [128, 1], F32)
            mean = sb("mean", [128, 1], F32)
            msq = sb("msq", [128, 1], F32)
            var = sb("var", [128, 1], F32)
            rstd = sb("rstd", [128, 1], F32)
            negmr = sb("negmr", [128, 1], F32)
            xnbf = sb("xnbf", [128, 2048], BF16)
            xstg = sb("xstg", [128, 2048], BF16)
            psT = [ph.enter_context(nc.psum_tensor(f"psT{i}", [128, 128], BF16)) for i in range(2)]
            for tt in range(16):
                steps = [
                    Step("sync", 16, lambda e, inc, _tt=tt:
                         inc(e.dma_start(out=xt[:, :], in_=x[_tt * 128:(_tt + 1) * 128, :]), 16),
                         after=-1),
                    Step("scalar", 1, lambda e, inc:
                         inc(e.activation(dump[:, :], xt[:, :], AF.Square, accum_out=sqc[:, :]))),
                    Step("vector", 1, lambda e, inc: (
                        vsync(e, e.reduce_sum(sumc[:, :], xt[:, :], axis=AX)),
                        vsync(e, e.tensor_scalar_mul(mean[:, :], sumc[:, :], 1.0 / 2048.0)),
                        vsync(e, e.tensor_mul(msq[:, :], mean[:, :], mean[:, :])),
                        vsync(e, e.tensor_scalar_mul(var[:, :], sqc[:, :], 1.0 / 2048.0)),
                        inc(e.tensor_sub(var[:, :], var[:, :], msq[:, :])))),
                    Step("scalar", 1, lambda e, inc:
                         inc(e.activation(rstd[:, :], var[:, :], AF.Sqrt, bias=eps1_s[:, :]))),
                    Step("vector", 1, lambda e, inc: (
                        vsync(e, e.reciprocal(rstd[:, :], rstd[:, :])),
                        vsync(e, e.tensor_mul(negmr[:, :], mean[:, :], rstd[:, :])),
                        vsync(e, e.tensor_scalar_mul(negmr[:, :], negmr[:, :], -1.0)),
                        inc(e.tensor_scalar(xnbf[:, :], xt[:, :], rstd[:, :], negmr[:, :],
                                            AO.mult, AO.add)))),
                ]
                for fp in range(8):
                    steps.append(Step("tensor", 1, lambda e, inc, _f=2 * fp: (
                        e.transpose(psT[0][:, :], xnbf[:, _f * 128:(_f + 1) * 128], idn_s[:, :]),
                        inc(e.transpose(psT[1][:, :], xnbf[:, (_f + 1) * 128:(_f + 2) * 128],
                                        idn_s[:, :])))))
                    steps.append(Step("scalar", 1, lambda e, inc, _f=2 * fp: (
                        e.copy(out=xstg[:, _f * 128:(_f + 1) * 128], in_=psT[0][:, :]),
                        inc(e.copy(out=xstg[:, (_f + 1) * 128:(_f + 2) * 128], in_=psT[1][:, :])))))
                steps.append(Step("gpsimd", 16, lambda e, inc, _tt=tt:
                                  inc(e.dma_start(
                                      out=xnT_h.rearrange("p (f t) -> p f t", f=16)[
                                          :, :, _tt * 128:(_tt + 1) * 128],
                                      in_=xstg[:, :].rearrange("p (f c) -> p f c", f=16)), 16)))
                ch.run(steps)

        if upto < 3:
            return _finish(nc, ch)
        with ExitStack() as ph:
            sb = lambda n, sh, ty: ph.enter_context(nc.sbuf_tensor(n, sh, ty))
            W = [sb(f"Wf{i}", [128, 512], F32) for i in range(24)]
            xsl = sb("xsl", [128, 512], BF16)
            mx = sb("mx", [128, 512], F32)
            mn = sb("mn", [128, 512], F32)
            ptmp = sb("ptmp", [64, 512], F32)
            Sb2 = sb("Sb2", [128, 512], F32)
            qi8 = sb("qi8", [128, 512], I8)
            qbf = sb("qbf", [128, 512], BF16)
            psH = ph.enter_context(nc.psum_tensor("psH", [128, 512], F32))
            psR = ph.enter_context(nc.psum_tensor("psR", [128, 512], F32))
            for tc in range(4):
                steps = []
                for f in range(16):
                    steps.append(Step("sync", 16, lambda e, inc, _f=f, _tc=tc:
                                      inc(e.dma_start(out=xsl[:, :],
                                                      in_=xnT_h[:, _f * T + _tc * 512:
                                                                _f * T + (_tc + 1) * 512]), 16)))
                    steps.append(Step("tensor", 1, lambda e, inc:
                                      inc(e.matmul(psH[:, :], h128b_s[:, :], xsl[:, :],
                                                   start=True, stop=True))))
                    steps.append(Step("scalar", 1, lambda e, inc, _f=f:
                                      inc(e.copy(out=W[_f][:, :], in_=psH[:, :]))))

                def butterflies(e, inc, nt=16):
                    slot = list(range(nt))
                    free = list(range(nt, nt + nt // 2))
                    nbits = int(np.log2(nt))
                    for p in range(nbits):
                        b = 1 << p
                        pairs = [(i, i | b) for i in range(nt) if not (i & b)]
                        last = None
                        for idx, (i, j) in enumerate(pairs):
                            last = e.tensor_sub(W[free[idx]][:, :], W[slot[i]][:, :],
                                                W[slot[j]][:, :])
                        vsync(e, last)
                        last = None
                        for i, j in pairs:
                            last = e.tensor_add(W[slot[i]][:, :], W[slot[i]][:, :],
                                                W[slot[j]][:, :])
                        vsync(e, last)
                        for idx, (i, j) in enumerate(pairs):
                            slot[j], free[idx] = free[idx], slot[j]
                    for op, dstbuf, neg in ((AO.max, mx, False), (AO.min, mn, True)):
                        last = None
                        for t in range(nt // 2):
                            last = e.tensor_tensor(W[free[t]][:, :], W[slot[2 * t]][:, :],
                                                   W[slot[2 * t + 1]][:, :], op)
                        vsync(e, last)
                        gap = 1
                        while gap < nt // 2:
                            last = None
                            for t in range(0, nt // 2, 2 * gap):
                                last = e.tensor_tensor(W[free[t]][:, :], W[free[t]][:, :],
                                                       W[free[t + gap]][:, :], op)
                            vsync(e, last)
                            gap *= 2
                        if neg:
                            vsync(e, e.tensor_scalar_mul(dstbuf[:, :], W[free[0]][:, :], -1.0))
                        else:
                            vsync(e, e.tensor_copy(dstbuf[:, :], W[free[0]][:, :]))
                    inc(e.tensor_max(mx[:, :], mx[:, :], mn[:, :]))
                    return slot
                slot_store = {}
                steps.append(Step("vector", 1,
                                  lambda e, inc: slot_store.__setitem__("s", butterflies(e, inc))))
                for r in range(7):
                    p = 64 >> r
                    steps.append(Step("sync", 16, lambda e, inc, _p=p:
                                      inc(e.dma_start(out=ptmp[0:_p, :], in_=mx[_p:2 * _p, :]), 16)))
                    steps.append(Step("vector", 1, lambda e, inc, _p=p:
                                      inc(e.tensor_max(mx[0:_p, :], mx[0:_p, :], ptmp[0:_p, :]))))
                steps.append(Step("vector", 1, lambda e, inc, _tc=tc:
                                  inc(e.tensor_scalar_max(mrow1[0:1, _tc * 512:(_tc + 1) * 512],
                                                          mx[0:1, :], 1e-5))))
                steps.append(Step("tensor", 1, lambda e, inc, _tc=tc:
                                  inc(e.matmul(psR[:, :], onr[:, :],
                                               mrow1[0:1, _tc * 512:(_tc + 1) * 512],
                                               start=True, stop=True))))
                steps.append(Step("vector", 1, lambda e, inc: (
                    vsync(e, e.tensor_scalar_mul(Sb2[:, :], psR[:, :], 1.0 / 127.0)),
                    inc(e.reciprocal(Sb2[:, :], Sb2[:, :])))))
                for f in range(16):
                    steps.append(Step("vector", 1, lambda e, inc, _f=f: (
                        vsync(e, e.tensor_mul(qi8[:, :], W[slot_store["s"][_f]][:, :], Sb2[:, :])),
                        inc(e.tensor_copy(qbf[:, :], qi8[:, :])))))
                    steps.append(Step("gpsimd", 16, lambda e, inc, _f=f, _tc=tc:
                                      inc(e.dma_start(out=xq_h[:, _f * T + _tc * 512:
                                                               _f * T + (_tc + 1) * 512],
                                                      in_=qbf[:, :]), 16)))
                ch.run(steps)

        if upto < 4:
            return _finish(nc, ch)
        with ExitStack() as ph:
            sb = lambda n, sh, ty: ph.enter_context(nc.sbuf_tensor(n, sh, ty))
            ctab = [sb("ctab0", [128, T], F32), sb("ctab1", [128, T], F32)]
            stab = [sb("stab0", [128, T], F32), sb("stab1", [128, T], F32)]
            wqst = [sb(f"wqst{i}", [128, 8 * 2048], BF16) for i in range(2)]
            wkst = [sb(f"wkst{i}", [128, 2 * 2048], BF16) for i in range(2)]
            wvst = [sb(f"wvst{i}", [128, 2048], BF16) for i in range(2)]
            xns = [sb(f"xns{i}", [128, 16 * 512], BF16) for i in range(2)]
            Xc = [sb(f"Xc{i}", [128, 512], F32) for i in range(2)]
            Xs = [sb(f"Xs{i}", [128, 512], F32) for i in range(2)]
            m1b = [sb(f"m1b{i}", [128, 512], F32) for i in range(2)]
            qkst = [sb(f"qkst{i}", [128, 512], BF16) for i in range(2)]
            vbfb = sb("vbfb", [128, 512], BF16)
            vstg = [sb(f"vstg{i}", [128, 128], BF16) for i in range(2)]
            psP = [ph.enter_context(nc.psum_tensor(f"psP{i}", [128, 512], F32)) for i in range(2)]
            psV = [ph.enter_context(nc.psum_tensor(f"psV{i}", [128, 128], BF16)) for i in range(2)]

            ch.run([Step("sync", 16 * 4, lambda e, inc: [
                inc(e.dma_start(out=ctab[0][:, :], in_=c_lod[:, :]), 16),
                inc(e.dma_start(out=stab[0][:, :], in_=s_lod[:, :]), 16),
                inc(e.dma_start(out=ctab[1][:, :], in_=c_hid[:, :]), 16),
                inc(e.dma_start(out=stab[1][:, :], in_=s_hid[:, :]), 16)])])

            steps = []

            def add(stp):
                steps.append(stp)
                return len(steps) - 1

            i_wl_prev = {0: -1, 1: -1}
            i_xl_prev = {0: -1, 1: -1}
            i_ev_bank = {0: -1, 1: -1}
            i_vv_bank = {0: -1, 1: -1}
            i_rope_buf = {0: -1, 1: -1}
            i_vstg_buf = {0: -1, 1: -1}
            i_vbf = [-1]
            ot_ctr = [0]
            v_ctr = [0]
            for g in range(KVH):
                gpar = g % 2

                def wload(e, inc, _g=g, _p=gpar):
                    for c2 in range(2):
                        for ec in range(16):
                            inc(e.dma_start(
                                out=wkst[_p][:, c2 * 2048 + ec * 128: c2 * 2048 + (ec + 1) * 128],
                                in_=wk[ec * 128:(ec + 1) * 128,
                                       _g * 256 + c2 * 128: _g * 256 + (c2 + 1) * 128]), 16)
                    for ec in range(16):
                        inc(e.dma_start(out=wvst[_p][:, ec * 128:(ec + 1) * 128],
                                        in_=wv[ec * 128:(ec + 1) * 128,
                                               _g * 128:(_g + 1) * 128]), 16)
                    for hl in range(4):
                        for c2 in range(2):
                            for ec in range(16):
                                col = (4 * _g + hl) * 256 + c2 * 128
                                inc(e.dma_start(
                                    out=wqst[_p][:, (hl * 2 + c2) * 2048 + ec * 128:
                                                 (hl * 2 + c2) * 2048 + (ec + 1) * 128],
                                    in_=wq[ec * 128:(ec + 1) * 128, col:col + 128]), 16)
                i_wl = add(Step("gpsimd", 16 * 176, wload, after=[i_wl_prev[gpar]]))

                for tc in range(4):
                    xpar = (g * 4 + tc) % 2

                    def xload(e, inc, _tc=tc, _p=xpar):
                        for f in range(16):
                            inc(e.dma_start(out=xns[_p][:, f * 512:(f + 1) * 512],
                                            in_=xnT_h[:, f * T + _tc * 512:
                                                      f * T + (_tc + 1) * 512]), 16)
                    i_xl = add(Step("sync", 16 * 16, xload, after=[i_xl_prev[xpar]]))

                    ots = [("k", c2, None) for c2 in range(2)] + [("v", None, None)] + \
                          [("q", hl, c2) for hl in range(4) for c2 in range(2)]
                    for kind, a1, a2 in ots:
                        bank = ot_ctr[0] % 2
                        ot_ctr[0] += 1
                        if kind == "v":
                            def mmv(e, inc, _p=gpar, _xp=xpar):
                                for ec in range(16):
                                    mm = e.matmul(psP[0][:, :] if False else psP[_b][:, :],
                                                  wvst[_p][:, ec * 128:(ec + 1) * 128],
                                                  xns[_xp][:, ec * 512:(ec + 1) * 512],
                                                  start=(ec == 0), stop=(ec == 15))
                                inc(mm)
                            def mmv2(e, inc, _p=gpar, _xp=xpar, _b=bank):
                                for ec in range(16):
                                    mm = e.matmul(psP[_b][:, :],
                                                  wvst[_p][:, ec * 128:(ec + 1) * 128],
                                                  xns[_xp][:, ec * 512:(ec + 1) * 512],
                                                  start=(ec == 0), stop=(ec == 15))
                                inc(mm)
                            i_mm = add(Step("tensor", 1, mmv2,
                                            after=[i_wl, i_xl, i_ev_bank[bank]]))
                            i_ev = add(Step("scalar", 1, lambda e, inc, _b=bank:
                                            inc(e.copy(out=vbfb[:, :], in_=psP[_b][:, :])),
                                            after=[i_mm, i_vbf[0]]))
                            i_ev_bank[bank] = i_ev
                            prev_tp = i_ev
                            for i in range(4):
                                vb = v_ctr[0] % 2
                                v_ctr[0] += 1
                                i_t = add(Step("tensor", 1, lambda e, inc, _i=i, _vb=vb:
                                               inc(e.transpose(psV[_vb][:, :],
                                                               vbfb[:, _i * 128:(_i + 1) * 128],
                                                               idn_s[:, :])),
                                               after=[prev_tp, i_vv_bank[vb]]))
                                i_e2 = add(Step("scalar", 1, lambda e, inc, _vb=vb:
                                               inc(e.copy(out=vstg[_vb][:, :], in_=psV[_vb][:, :])),
                                               after=[i_t, i_vstg_buf[vb]]))
                                i_vv_bank[vb] = i_e2
                                i_d = add(Step("gpsimd", 16, lambda e, inc, _g=g, _tc=tc, _i=i, _vb=vb:
                                               inc(e.dma_start(
                                                   out=v_h[:, _g * T + (_tc * 4 + _i) * 128:
                                                           _g * T + (_tc * 4 + _i + 1) * 128],
                                                   in_=vstg[_vb][:, :]), 16),
                                               after=[i_e2]))
                                i_vstg_buf[vb] = i_d
                            i_vbf[0] = i_t
                        else:
                            if kind == "k":
                                wsrc, wcol = wkst[gpar], a1 * 2048
                                comp = a1
                                dsth, dcol = kT_h, (g * 2 + a1) * T + tc * 512
                            else:
                                wsrc, wcol = wqst[gpar], (a1 * 2 + a2) * 2048
                                comp = a2
                                dsth, dcol = qT_h, ((4 * g + a1) * 2 + a2) * T + tc * 512
                            rpar = ot_ctr[0] % 2

                            def mmp(e, inc, _w=wsrc, _wc=wcol, _xp=xpar, _b=bank):
                                for ec in range(16):
                                    mm = e.matmul(psP[_b][:, :],
                                                  _w[:, _wc + ec * 128: _wc + (ec + 1) * 128],
                                                  xns[_xp][:, ec * 512:(ec + 1) * 512],
                                                  start=(ec == 0), stop=(ec == 15))
                                inc(mm)
                            i_mm = add(Step("tensor", 1, mmp,
                                            after=[i_wl, i_xl, i_ev_bank[bank]]))
                            i_ev = add(Step("scalar", 1, lambda e, inc, _b=bank, _r=rpar: (
                                e.copy(out=Xc[_r][:, :], in_=psP[_b][:, :]),
                                e.copy(out=Xs[_r][0:64, :], in_=psP[_b][64:128, :]),
                                inc(e.copy(out=Xs[_r][64:128, :], in_=psP[_b][0:64, :]))),
                                after=[i_mm, i_rope_buf[rpar]]))
                            i_ev_bank[bank] = i_ev
                            i_ro = add(Step("vector", 1, lambda e, inc, _c=comp, _tc=tc, _r=rpar: (
                                e.tensor_mul(m1b[_r][:, :], Xc[_r][:, :],
                                             ctab[_c][:, _tc * 512:(_tc + 1) * 512]),
                                vsync(e, e.tensor_mul(Xs[_r][:, :], Xs[_r][:, :],
                                                      stab[_c][:, _tc * 512:(_tc + 1) * 512])),
                                inc(e.tensor_add(qkst[_r][:, :], m1b[_r][:, :], Xs[_r][:, :]))),
                                after=[i_ev]))
                            i_d = add(Step("gpsimd", 16, lambda e, inc, _d=dsth, _dc=dcol, _r=rpar:
                                           inc(e.dma_start(out=_d[:, _dc:_dc + 512],
                                                           in_=qkst[_r][:, :]), 16),
                                           after=[i_ro]))
                            i_rope_buf[rpar] = i_d
                    i_xl_prev[xpar] = i_mm if True else i_xl
                i_wl_prev[gpar] = i_mm
            ch.run(steps)

        with ExitStack() as ph:
            sb = lambda n, sh, ty, **kw: ph.enter_context(nc.sbuf_tensor(n, sh, ty, **kw))
            attT = sb("attT", [128, H * T], BF16)
            ph4i = ph.enter_context(ExitStack())
            sb4 = sb
            sb = lambda n, sh, ty, **kw: ph4i.enter_context(nc.sbuf_tensor(n, sh, ty, **kw))
            kg = [sb(f"kg{i}", [128, 2 * T], BF16) for i in range(2)]
            vg = [sb(f"vg{i}", [128, T], BF16) for i in range(2)]
            qh = [sb(f"qh{i}", [128, 2 * T], BF16) for i in range(2)]
            e0b = sb("e0b", [128, 2 * T], BF16)
            e1b = sb("e1b", [128, 2 * T], BF16)
            Pf = sb("Pf", [128, T], BF16)
            Pb = sb("Pb", [128, 8 * T], BF16)
            PT = sb("PT", [128, 16 * 512], BF16, is_transpose_dest=True)
            A0 = sb("A0", [128, 8], F32)
            A1 = sb("A1", [128, 8], F32)
            Zc = sb("Zc", [128, 8], F32)
            S0 = sb("S0", [128, 2], F32)
            S1 = sb("S1", [128, 2], F32)
            SZ = sb("SZ", [128, 2], F32)
            r0 = sb("r0", [128, 2], F32)
            lr1 = sb("lr1", [128, 2], F32)
            r2 = sb("r2", [128, 2], F32)
            sqb = sb("sqb", [128, 512], F32)
            msb = sb("msb", [128, 512], F32)
            msrow = sb("msrow", [1, 512], F32)
            taub = sb("taub", [128, 512], F32)
            taurow = sb("taurow", [1, 2 * 512], F32)
            psE0 = [ph.enter_context(nc.psum_tensor(f"psE0{i}", [128, 512], F32)) for i in range(2)]
            psE1 = [ph4i.enter_context(nc.psum_tensor(f"psE1{i}", [128, 512], F32)) for i in range(2)]
            psT4 = [ph4i.enter_context(nc.psum_tensor(f"psT4{i}", [128, 128], BF16))
                    for i in range(2)]
            psAt = ph4i.enter_context(nc.psum_tensor("psAt", [128, 512], F32))
            psRo = ph4i.enter_context(nc.psum_tensor("psRo", [128, 512], F32))
            psRp = psRo

            qc_glob = [0]
            pair_glob = [0]
            tptb_ctr = [0]
            tp_evac_b = {0: -1, 1: -1}
            steps = []
            idx_exp_bank = {0: -1, 1: -1}
            idx_attmm_prev = -1
            idx_attmm_prev2 = -1
            idx_attts_prev = -1
            idx_sq_prev = -1
            idx_rowmm_prev = -1
            idx_mscp_prev = -1
            idx_msdma_prev = -1
            idx_ptp_prev = {0: [], 1: []}
            i_kvload_p = {0: -1, 1: -1}
            i_qload_p = {0: -1, 1: -1}
            kv_used = {0: [], 1: []}
            q_used = {0: [], 1: []}

            import os
            _SER = os.environ.get("PH4_SER", "")
            _PIPE = os.environ.get("PH4_PIPE")

            def add(stp, tag=""):
                if os.environ.get("PH4_SERIAL") or (tag and tag in _SER):
                    stp.after = None
                if _PIPE is not None and (not tag or tag not in _PIPE):
                    stp.after = None
                steps.append(stp)
                return len(steps) - 1

            for g in range(KVH):
                gp_ = g % 2
                for hl in range(4):
                    h = 4 * g + hl
                    hp_ = h % 2
                    last_pn = {0: -1, 1: -1}
                    last_tp = {0: -1, 1: -1}

                    if hl == 0:
                        i_kvl = add(Step("sync", 48, lambda e, inc, _g=g, _p=gp_: [
                            inc(e.dma_start(out=kg[_p][:, 0:T],
                                            in_=kT_h[:, (2 * _g) * T:(2 * _g + 1) * T]), 16),
                            inc(e.dma_start(out=kg[_p][:, T:2 * T],
                                            in_=kT_h[:, (2 * _g + 1) * T:(2 * _g + 2) * T]), 16),
                            inc(e.dma_start(out=vg[_p][:, :],
                                            in_=v_h[:, _g * T:(_g + 1) * T]), 16)],
                            after=kv_used[gp_] if kv_used[gp_] else -1))
                        i_kvload_p[gp_] = i_kvl
                        kv_used[gp_] = []
                    i_ql = add(Step("sync", 32, lambda e, inc, _h=h, _p=hp_: [
                        inc(e.dma_start(out=qh[_p][:, 0:T],
                                        in_=qT_h[:, (2 * _h) * T:(2 * _h + 1) * T]), 16),
                        inc(e.dma_start(out=qh[_p][:, T:2 * T],
                                        in_=qT_h[:, (2 * _h + 1) * T:(2 * _h + 2) * T]), 16)],
                        after=q_used[hp_] if q_used[hp_] else -1))
                    i_qload_p[hp_] = i_ql
                    q_used[hp_] = []
                    i_qload = i_ql
                    i_kvload = i_kvload_p[gp_]

                    for qc in range(4):
                        ptpar = 0
                        qc_glob[0] += 1
                        PTof = 0
                        idx_ptp_cur = []
                        idx_expP_last = -1
                        for qs in range(4):
                            qt = 4 * qc + qs
                            par = qt % 2
                            nch = qt // 4 + 1
                            idx_exp_last = -1
                            for kc in range(nch):
                                cw = 512 if kc < qt // 4 else (qt % 4 + 1) * 128
                                isdiag = (kc == qt // 4)
                                bank = pair_glob[0] % 2
                                pair_glob[0] += 1

                                def mme(e, inc, _qt=qt, _kc=kc, _cw=cw, _b=bank,
                                        _gp=gp_, _hp=hp_):
                                    e.matmul(psE0[_b][:, 0:_cw],
                                             qh[_hp][:, _qt * 128:(_qt + 1) * 128],
                                             kg[_gp][:, _kc * 512:_kc * 512 + _cw],
                                             start=True, stop=True)
                                    inc(e.matmul(psE1[_b][:, 0:_cw],
                                                 qh[_hp][:, T + _qt * 128: T + (_qt + 1) * 128],
                                                 kg[_gp][:, T + _kc * 512: T + _kc * 512 + _cw],
                                                 start=True, stop=True))
                                i_mm = add(Step("tensor", 1, mme,
                                                after=[i_qload, i_kvload,
                                                       idx_exp_bank[bank]]), tag="E")
                                kv_used[gp_].append(i_mm)
                                q_used[hp_].append(i_mm)
                                i_pre = i_mm
                                if isdiag:
                                    def trif(e, inc, _cw=cw, _b=bank):
                                        e.tensor_add(psE0[_b][:, _cw - 128:_cw],
                                                     psE0[_b][:, _cw - 128:_cw], tri_s[:, :])
                                        inc(e.tensor_add(psE1[_b][:, _cw - 128:_cw],
                                                         psE1[_b][:, _cw - 128:_cw], tri_s[:, :]))
                                    i_pre = add(Step("vector", 1, trif, after=[i_mm]), tag="E")

                                def expf(e, inc, _kc=kc, _cw=cw, _b=bank, _p=par):
                                    e.activation(e0b[:, _p * T + _kc * 512: _p * T + _kc * 512 + _cw],
                                                 psE0[_b][:, 0:_cw], AF.Exp, scale=ISQ,
                                                 accum_out=A0[:, _p * 4 + _kc:_p * 4 + _kc + 1])
                                    inc(e.activation(e1b[:, _p * T + _kc * 512: _p * T + _kc * 512 + _cw],
                                                     psE1[_b][:, 0:_cw], AF.Exp, scale=ISQ,
                                                     accum_out=A1[:, _p * 4 + _kc:_p * 4 + _kc + 1]))
                                i_exp = add(Step("scalar", 1, expf,
                                                             after=[i_pre, last_pn[par]]), tag="E")
                                idx_exp_bank[bank] = i_exp
                                idx_exp_last = i_exp

                            def rstep(e, inc, _n=nch, _h=h, _p=par):
                                e.reduce_sum(S0[:, _p:_p + 1], A0[:, _p * 4:_p * 4 + _n], axis=AX)
                                vsync(e, e.reduce_sum(S1[:, _p:_p + 1], A1[:, _p * 4:_p * 4 + _n],
                                                      axis=AX))
                                e.reciprocal(r0[:, _p:_p + 1], S0[:, _p:_p + 1])
                                vsync(e, e.reciprocal(lr1[:, _p:_p + 1], S1[:, _p:_p + 1]))
                                inc(e.tensor_mul(lr1[:, _p:_p + 1], lr1[:, _p:_p + 1],
                                                 lamrep[:, _h:_h + 1]))
                            i_r = add(Step("vector", 1, rstep,
                                           after=[idx_exp_last, last_pn[par]]), tag="R")

                            kspan = (qt + 1) * 128

                            def dif(e, inc, _ks=kspan, _qt=qt, _p=par):
                                sl = slice(_p * T, _p * T + _ks)
                                e.tensor_scalar(e1b[:, sl], e1b[:, sl], lr1[:, _p:_p + 1],
                                                None, AO.mult)
                                vsync(e, e.tensor_scalar(e0b[:, sl], e0b[:, sl],
                                                         r0[:, _p:_p + 1], None, AO.mult))
                                vsync(e, e.tensor_sub(e0b[:, sl], e0b[:, sl], e1b[:, sl]))
                                dsl = slice(_p * T + _qt * 128, _p * T + (_qt + 1) * 128)
                                inc(e.tensor_add(e0b[:, dsl], e0b[:, dsl], tri_s[:, :]))
                            i_dif = add(Step("vector", 1, dif, after=[i_r]), tag="D")

                            def expP(e, inc, _ks=kspan, _p=par):
                                inc(e.activation(Pf[:, 0:_ks], e0b[:, _p * T: _p * T + _ks],
                                                 AF.Exp,
                                                 accum_out=SZ[:, _p:_p + 1]))
                            idx_expP_last = add(Step("scalar", 1, expP,
                                                     after=[i_dif, last_pn[0], last_pn[1]]),
                                                tag="D")

                            pbslot = (qc % 2) * 4 + qs

                            def pnorm(e, inc, _qt=qt, _p=par, _sl=pbslot):
                                vsync(e, e.reciprocal(r2[:, _p:_p + 1], SZ[:, _p:_p + 1]))
                                inc(e.tensor_scalar(Pb[:, _sl * T: _sl * T + (_qt + 1) * 128],
                                                    Pf[:, 0:(_qt + 1) * 128],
                                                    r2[:, _p:_p + 1], None, AO.mult))
                            i_pn = add(Step("vector", 1, pnorm,
                                            after=[idx_expP_last, idx_attmm_prev2]), tag="P")
                            last_pn[par] = i_pn

                            n_even = (qt + 2) // 2

                            def ptp_e(e, inc, _qt=qt, _qs=qs, _of=PTof, _sl=pbslot):
                                for kt in range(0, _qt + 1, 2):
                                    inc(e.dma_start_transpose(
                                        PT[:, _of + kt * 512 + _qs * 128: _of + kt * 512 + (_qs + 1) * 128],
                                        Pb[:, _sl * T + kt * 128: _sl * T + (kt + 1) * 128]), 16)
                            i_tp = add(Step("sync", 16 * n_even, ptp_e,
                                            after=[i_pn] + idx_ptp_prev[ptpar]), tag="T")
                            idx_ptp_cur.append(i_tp)
                            last_tp[par] = i_tp
                            for kt in range(1, qt + 1, 2):
                                tb = tptb_ctr[0] % 2
                                tptb_ctr[0] += 1
                                i_tt = add(Step("tensor", 1,
                                               lambda e, inc, _kt=kt, _sl=pbslot, _tb=tb:
                                               inc(e.transpose(psT4[_tb][:, :],
                                                               Pb[:, _sl * T + _kt * 128:
                                                                  _sl * T + (_kt + 1) * 128],
                                                               idn_s[:, :])),
                                               after=[i_pn, tp_evac_b[tb]] + idx_ptp_prev[ptpar]),
                                           tag="T")
                                ee = "scalar" if (tptb_ctr[0] % 2 == 0) else "vector"

                                def tevac_s(e, inc, _kt=kt, _qs=qs, _of=PTof, _tb=tb):
                                    inc(e.copy(out=PT[:, _of + _kt * 512 + _qs * 128:
                                                      _of + _kt * 512 + (_qs + 1) * 128],
                                               in_=psT4[_tb][:, :]))

                                def tevac_v(e, inc, _kt=kt, _qs=qs, _of=PTof, _tb=tb):
                                    inc(e.tensor_copy(
                                        PT[:, _of + _kt * 512 + _qs * 128:
                                           _of + _kt * 512 + (_qs + 1) * 128],
                                        psT4[_tb][:, :]))
                                i_ev4 = add(Step(ee, 1, tevac_s if ee == "scalar" else tevac_v,
                                                 after=[i_tt]), tag="T")
                                tp_evac_b[tb] = i_ev4
                                idx_ptp_cur.append(i_ev4)
                        zblocks = [(kt, qs) for kt in range(4 * qc + 4) for qs in range(4)
                                   if 4 * qc + qs < kt]

                        def zset(e, inc, _zb=zblocks, _of=PTof):
                            last = None
                            for kt, qs in _zb:
                                last = e.memset(
                                    PT[:, _of + kt * 512 + qs * 128: _of + kt * 512 + (qs + 1) * 128], 0.0)
                            if last is None:
                                last = e.tensor_copy(S0[:, :], S0[:, :])
                            inc(last)
                        i_z = add(Step("vector", 1, zset,
                                       after=[steps.__len__() - 1] + idx_ptp_prev[ptpar]), tag="Z")

                        def attmm(e, inc, _qc=qc, _of=PTof, _gp=gp_):
                            for kt in range(4 * _qc + 4):
                                mm = e.matmul(psAt[:, :], vg[_gp][:, kt * 128:(kt + 1) * 128],
                                              PT[:, _of + kt * 512: _of + (kt + 1) * 512],
                                              start=(kt == 0), stop=(kt == 4 * _qc + 3))
                            inc(mm)
                        i_am = add(Step("tensor", 1, attmm,
                                        after=idx_ptp_cur + [i_z, idx_attts_prev, idx_sq_prev]),
                                   tag="A")
                        idx_ptp_prev[ptpar] = [i_am]

                        i_sq = add(Step("scalar", 1, lambda e, inc:
                                        inc(e.activation(sqb[:, :], psAt[:, :], AF.Square)),
                                        after=[i_am, idx_rowmm_prev]), tag="A")
                        i_row = add(Step("tensor", 1, lambda e, inc:
                                         inc(e.matmul(psRo[0:1, :], onc[:, :], sqb[:, :],
                                                      start=True, stop=True)),
                                         after=[i_sq, idx_mscp_prev]), tag="Q")
                        i_mscp = add(Step("scalar", 1, lambda e, inc:
                                          inc(e.copy(out=msrow[0:1, :], in_=psRo[0:1, :])),
                                          after=[i_row, idx_msdma_prev]), tag="Q")
                        i_msdma = add(Step("sync", 16, lambda e, inc, _h=h, _qc=qc:
                                           inc(e.dma_start(
                                               out=msb[_h * 4 + _qc:_h * 4 + _qc + 1, :],
                                               in_=msrow[0:1, :]), 16),
                                           after=[i_mscp]), tag="Q")
                        i_ts = add(Step("vector", 1, lambda e, inc, _h=h, _qc=qc:
                                        inc(e.tensor_scalar(
                                            attT[:, _h * T + _qc * 512: _h * T + (_qc + 1) * 512],
                                            psAt[:, :], rms_s[:, :], None, AO.mult)),
                                        after=[i_am, i_sq]), tag="S")
                        idx_attmm_prev2 = idx_attmm_prev
                        idx_attmm_prev = i_am
                        idx_attts_prev = i_ts
                        idx_sq_prev = i_sq
                        idx_rowmm_prev = i_row
                        idx_mscp_prev = i_mscp
                        idx_msdma_prev = i_msdma
                        kv_used[gp_].append(i_am)
            ch.run(steps)

            steps = [
                Step("scalar", 1, lambda e, inc:
                     inc(e.activation(taub[0:64, :], msb[0:64, :], AF.Sqrt,
                                      scale=1.0 / (128.0 * CR * CR), bias=eps2_s[0:64, :]))),
                Step("vector", 1, lambda e, inc:
                     inc(e.reciprocal(taub[0:64, :], taub[0:64, :]))),
            ]
            for sb8 in range(32):
                steps.append(Step("sync", 16, lambda e, inc, _s=sb8:
                                  inc(e.dma_start(
                                      out=taurow[0:1, :].rearrange("p (r c) -> p r c", r=2),
                                      in_=taub[2 * _s:2 * _s + 2, :]), 16)))
                for rr in range(2):
                    r = sb8 * 2 + rr
                    steps.append(Step("tensor", 1, lambda e, inc, _rr=rr:
                                      inc(e.matmul(psRp[:, :], onr[:, :],
                                                   taurow[0:1, _rr * 512:(_rr + 1) * 512],
                                                   start=True, stop=True))))
                    h, qc = r // 4, r % 4
                    steps.append(Step("vector", 1, lambda e, inc, _h=h, _qc=qc:
                                      inc(e.tensor_mul(
                                          attT[:, _h * T + _qc * 512: _h * T + (_qc + 1) * 512],
                                          attT[:, _h * T + _qc * 512: _h * T + (_qc + 1) * 512],
                                          psRp[:, :]))))
            ch.run(steps)
            if att_t is not None:
                ch.run([Step("gpsimd", 16, lambda e, inc:
                             inc(e.dma_start(out=att_t[:, :], in_=attT[:, :]), 16))])

            if upto < 6:
                return _finish(nc, ch)
            ph4i.close()
            sb = sb4
            wost = [sb(f"wost{i}", [128, 2048], BF16) for i in range(3)]
            sstg = [sb(f"sstg{i}", [128, 512], BF16) for i in range(4)]
            psS = [psE0[0], psE0[1],
                   ph.enter_context(nc.psum_tensor("psS2", [128, 512], F32)),
                   ph.enter_context(nc.psum_tensor("psS3", [128, 512], F32))]
            steps = []

            def add5(stp):
                steps.append(stp)
                return len(steps) - 1

            i_ev_p = {i: -1 for i in range(4)}
            i_st_p = {i: -1 for i in range(4)}
            mm_ctr = [0]
            i_mm_last_p = {i: -1 for i in range(3)}
            for ot in range(16):
                wpar = ot % 3

                def wl5(e, inc, _ot=ot, _p=wpar):
                    for ec in range(16):
                        inc(e.dma_start(out=wost[_p][:, ec * 128:(ec + 1) * 128],
                                        in_=wo[ec * 128:(ec + 1) * 128,
                                               _ot * 128:(_ot + 1) * 128]), 16)
                i_wl = add5(Step("gpsimd", 16 * 16, wl5, after=[i_mm_last_p[wpar]]))
                for tc in range(4):
                    b = mm_ctr[0] % 4
                    mm_ctr[0] += 1

                    def mmo(e, inc, _tc=tc, _p=wpar, _b=b):
                        for ec in range(16):
                            mm = e.matmul(psS[_b][:, :], wost[_p][:, ec * 128:(ec + 1) * 128],
                                          attT[:, ec * T + _tc * 512: ec * T + (_tc + 1) * 512],
                                          start=(ec == 0), stop=(ec == 15))
                        inc(mm)
                    i_mm = add5(Step("tensor", 1, mmo, after=[i_wl, i_ev_p[b]]))
                    i_ev = add5(Step("scalar", 1, lambda e, inc, _b=b, _sp=b:
                                     inc(e.copy(out=sstg[_sp][:, :], in_=psS[_b][:, :])),
                                     after=[i_mm, i_st_p[b]]))
                    i_ev_p[b] = i_ev
                    i_st = add5(Step("gpsimd", 16, lambda e, inc, _ot=ot, _tc=tc, _sp=b:
                                     inc(e.dma_start(out=s_hbm[:, _ot * T + _tc * 512:
                                                              _ot * T + (_tc + 1) * 512],
                                                     in_=sstg[_sp][:, :]), 16),
                                     after=[i_ev]))
                    i_st_p[b] = i_st
                i_mm_last_p[wpar] = i_mm
            ch.run(steps)

        with ExitStack() as ph:
            sb = lambda n, sh, ty: ph.enter_context(nc.sbuf_tensor(n, sh, ty))
            xqs = sb("xqs", [128, NE * T], BF16)
            wstg = [sb(f"wstg{i}", [128, 2048], F32) for i in range(3)]
            i8sl = [sb(f"i8sl{i}", [128, 2048], I8) for i in range(3)]
            tsl = [sb(f"tsl{i}", [128, 2048], BF16) for i in range(3)]
            rlb = [sb(f"rlb{i}", [128, 512], F32) for i in range(4)]
            h2b = [sb(f"h2b{i}", [128, 512], F32) for i in range(4)]
            psU = [ph.enter_context(nc.psum_tensor(f"psU{i}", [128, 512], F32)) for i in range(4)]
            ch.run([Step("sync", 16 * 16, lambda e, inc: [
                inc(e.dma_start(out=xqs[:, f * T:(f + 1) * T], in_=xq_h[:, f * T:(f + 1) * T]), 16)
                for f in range(16)])])
            steps = []

            def add6(stp):
                steps.append(stp)
                return len(steps) - 1

            i_i8_p = {i: -1 for i in range(3)}
            i_cl_p = {i: -1 for i in range(3)}
            i_mml_p = {i: -1 for i in range(3)}
            i_tt_b = {i: -1 for i in range(4)}
            i_dma_b = {i: -1 for i in range(4)}
            mm_ctr = [0]
            for o in range(NF2):
                p = o % 3

                def wl6(e, inc, _o=o, _p=p):
                    for ec in range(16):
                        inc(e.dma_start(out=wstg[_p][:, ec * 128:(ec + 1) * 128],
                                        in_=wupT[ec * 128:(ec + 1) * 128,
                                                 _o * 128:(_o + 1) * 128]), 16)
                i_wl = add6(Step("sync", 16 * 16, wl6, after=[i_i8_p[p]]))
                i_i8 = add6(Step("scalar", 1, lambda e, inc, _p=p:
                                 inc(e.activation(i8sl[_p][:, :], wstg[_p][:, :], AF.Copy,
                                                  scale=rwsu[:, :])),
                                 after=[i_wl, i_cl_p[p]]))
                i_i8_p[p] = i_i8
                i_cl = add6(Step("vector", 1, lambda e, inc, _p=p:
                                 inc(e.tensor_scalar(tsl[_p][:, :], i8sl[_p][:, :], 1.0, -1.0,
                                                     AO.min, AO.max)),
                                 after=[i_i8, i_mml_p[p]]))
                i_cl_p[p] = i_cl
                for tc in range(4):
                    b = mm_ctr[0] % 4
                    mm_ctr[0] += 1

                    def mmu(e, inc, _tc=tc, _p=p, _b=b):
                        for ec in range(16):
                            mm = e.matmul(psU[_b][:, :], tsl[_p][:, ec * 128:(ec + 1) * 128],
                                          xqs[:, ec * T + _tc * 512: ec * T + (_tc + 1) * 512],
                                          start=(ec == 0), stop=(ec == 15))
                        inc(mm)
                    i_mm = add6(Step("tensor", 1, mmu, after=[i_cl, i_tt_b[b]]))
                    i_rl = add6(Step("scalar", 1, lambda e, inc, _b=b:
                                     inc(e.activation(rlb[_b][:, :], psU[_b][:, :], AF.Relu)),
                                     after=[i_mm]))
                    i_tt = add6(Step("vector", 1, lambda e, inc, _b=b:
                                     inc(e.tensor_mul(h2b[_b][:, :], rlb[_b][:, :], psU[_b][:, :])),
                                     after=[i_rl, i_dma_b[b]]))
                    i_tt_b[b] = i_tt
                    i_dma = add6(Step("gpsimd", 16, lambda e, inc, _o=o, _tc=tc, _b=b:
                                      inc(e.dma_start(out=h2_h[:, _o * T + _tc * 512:
                                                               _o * T + (_tc + 1) * 512],
                                                      in_=h2b[_b][:, :]), 16),
                                      after=[i_tt]))
                    i_dma_b[b] = i_dma
                i_mml_p[p] = i_mm
            ch.run(steps)

        with ExitStack() as ph:
            sb = lambda n, sh, ty: ph.enter_context(nc.sbuf_tensor(n, sh, ty))
            K2 = sb("K2", [128, NF2 * T], BF16)
            ph7 = ph.enter_context(ExitStack())
            sb7 = lambda n, sh, ty: ph7.enter_context(nc.sbuf_tensor(n, sh, ty))
            W = [sb7(f"W2f{i}", [128, 256], F32) for i in range(48)]
            mx = sb7("mx2", [128, 256], F32)
            mn = sb7("mn2", [128, 256], F32)
            ptmp = sb7("ptmp2", [64, 256], F32)
            Sb2 = sb7("Sb22", [128, 256], F32)
            qi8 = [sb7(f"qi82{i}", [128, 256], I8) for i in range(8)]
            mrow2 = sb7("mrow2", [1, 256], F32)
            psHn = [ph7.enter_context(nc.psum_tensor(f"psH2{i}", [128, 256], F32))
                    for i in range(4)]
            psR = ph7.enter_context(nc.psum_tensor("psR2", [128, 256], F32))


            def butterflies2(e, inc, nt=32):
                slot = list(range(nt))
                free = list(range(nt, nt + nt // 2))
                for p in range(5):
                    b = 1 << p
                    pairs = [(i, i | b) for i in range(nt) if not (i & b)]
                    last = None
                    for idx, (i, j) in enumerate(pairs):
                        last = e.tensor_sub(W[free[idx]][:, :], W[slot[i]][:, :],
                                            W[slot[j]][:, :])
                    vsync(e, last)
                    last = None
                    for i, j in pairs:
                        last = e.tensor_add(W[slot[i]][:, :], W[slot[i]][:, :],
                                            W[slot[j]][:, :])
                    vsync(e, last)
                    for idx, (i, j) in enumerate(pairs):
                        slot[j], free[idx] = free[idx], slot[j]
                for op, dstbuf, neg in ((AO.max, mx, False), (AO.min, mn, True)):
                    last = None
                    for t in range(nt // 2):
                        last = e.tensor_tensor(W[free[t]][:, :], W[slot[2 * t]][:, :],
                                               W[slot[2 * t + 1]][:, :], op)
                    vsync(e, last)
                    gap = 1
                    while gap < nt // 2:
                        last = None
                        for t in range(0, nt // 2, 2 * gap):
                            last = e.tensor_tensor(W[free[t]][:, :], W[free[t]][:, :],
                                                   W[free[t + gap]][:, :], op)
                        vsync(e, last)
                        gap *= 2
                    if neg:
                        vsync(e, e.tensor_scalar_mul(dstbuf[:, :], W[free[0]][:, :], -1.0))
                    else:
                        vsync(e, e.tensor_copy(dstbuf[:, :], W[free[0]][:, :]))
                inc(e.tensor_max(mx[:, :], mx[:, :], mn[:, :]))
                return slot

            ld_prev = [-1]
            mm_prev_b = {i: -1 for i in range(4)}
            ev_prev = [-1]
            qt_prev = [-1]
            srep_prev = [-1]
            steps = []

            def add7(stp):
                steps.append(stp)
                return len(steps) - 1

            for cc in range(8):
                def ld7(e, inc, _cc=cc):
                    for o in range(NF2):
                        inc(e.dma_start(out=W[o][:, :],
                                        in_=h2_h[:, o * T + _cc * 256:
                                                 o * T + (_cc + 1) * 256]), 16)
                i_ld = add7(Step("sync", 16 * 32, ld7, after=[qt_prev[0]]))
                i_ev_last = -1
                for o in range(NF2):
                    b = o % 4
                    i_mm = add7(Step("tensor", 1, lambda e, inc, _o=o, _b=b:
                                     inc(e.matmul(psHn[_b][:, :],
                                                  h128f_s[:, :], W[_o][:, :],
                                                  start=True, stop=True)),
                                     after=[i_ld, mm_prev_b[b]]))
                    i_ev = add7(Step("scalar", 1, lambda e, inc, _o=o, _b=b:
                                     inc(e.copy(out=W[_o][:, :], in_=psHn[_b][:, :])),
                                     after=[i_mm]))
                    mm_prev_b[b] = i_ev
                    i_ev_last = i_ev
                sstore = {}
                i_bf = add7(Step("vector", 1,
                                 lambda e, inc: sstore.__setitem__("s", butterflies2(e, inc)),
                                 after=[i_ev_last]))
                i_prev = i_bf
                for r in range(7):
                    p = 64 >> r
                    i_d = add7(Step("sync", 16, lambda e, inc, _p=p:
                                    inc(e.dma_start(out=ptmp[0:_p, :], in_=mx[_p:2 * _p, :]), 16),
                                    after=[i_prev]))
                    i_prev = add7(Step("vector", 1, lambda e, inc, _p=p:
                                       inc(e.tensor_max(mx[0:_p, :], mx[0:_p, :], ptmp[0:_p, :])),
                                       after=[i_d]))
                i_bt = add7(Step("vector", 1, lambda e, inc, _cc=cc: (
                    vsync(e, e.tensor_scalar_max(mrow2[0:1, :], mx[0:1, :], 1e-5)),
                    vsync(e, e.tensor_mul(betar[0:1, _cc * 256:(_cc + 1) * 256],
                                          mrow1[0:1, _cc * 256:(_cc + 1) * 256],
                                          mrow1[0:1, _cc * 256:(_cc + 1) * 256])),
                    vsync(e, e.tensor_mul(betar[0:1, _cc * 256:(_cc + 1) * 256],
                                          betar[0:1, _cc * 256:(_cc + 1) * 256], mrow2[0:1, :])),
                    inc(e.tensor_scalar(betar[0:1, _cc * 256:(_cc + 1) * 256],
                                        betar[0:1, _cc * 256:(_cc + 1) * 256],
                                        combo[0:1, :], None, AO.mult))),
                    after=[i_prev, srep_prev[0]]))
                i_rp = add7(Step("tensor", 1, lambda e, inc:
                                 inc(e.matmul(psR[:, 0:256], onr[:, :], mrow2[0:1, :],
                                              start=True, stop=True)),
                                 after=[i_bt]))
                i_sr = add7(Step("vector", 1, lambda e, inc: (
                    vsync(e, e.tensor_scalar_mul(Sb2[:, :], psR[:, 0:256], 1.0 / 127.0)),
                    inc(e.reciprocal(Sb2[:, :], Sb2[:, :]))),
                    after=[i_rp]))
                srep_prev[0] = i_sr
                def qf(e, inc, _cc=cc):
                    s = sstore["s"]
                    last = None
                    for g in range(4):
                        for j in range(8):
                            last = e.tensor_mul(qi8[j][:, :], W[s[g * 8 + j]][:, :],
                                                Sb2[:, :])
                        vsync(e, last)
                        for j in range(8):
                            o = g * 8 + j
                            last = e.tensor_copy(
                                K2[:, o * T + _cc * 256: o * T + (_cc + 1) * 256],
                                qi8[j][:, :])
                        if g < 3:
                            vsync(e, last)
                    inc(last)
                qt_prev[0] = add7(Step("vector", 1, qf, after=[i_sr]))
            ch.run(steps)
            if k2_t is not None:
                ch.run([Step("gpsimd", 16 * 8, lambda e, inc: [
                    inc(e.dma_start(out=k2_t[:, i * 8192:(i + 1) * 8192],
                                    in_=K2[:, i * 8192:(i + 1) * 8192]), 16) for i in range(8)])])

            ph7.close()

            if upto < 9:
                return _finish(nc, ch)
            brep = sb("brep", [128, T], F32)
            brepp = ph.enter_context(nc.psum_tensor("psB0", [128, 512], F32))
            steps = []
            for tc in range(4):
                steps.append(Step("tensor", 1, lambda e, inc, _tc=tc:
                                  inc(e.matmul(brepp[:, :], onr[:, :],
                                               betar[0:1, _tc * 512:(_tc + 1) * 512],
                                               start=True, stop=True))))
                steps.append(Step("scalar", 1, lambda e, inc, _tc=tc:
                                  inc(e.copy(out=brep[:, _tc * 512:(_tc + 1) * 512],
                                             in_=brepp[:, :]))))
            ch.run(steps)

            wst2 = sb("wst2", [128, 4096], F32)
            i8s2 = sb("i8s2", [128, 4096], I8)
            tsl2 = [sb(f"tsl2{i}", [128, 4096], BF16) for i in range(2)]
            ub = [sb(f"ub{i}", [128, 512], F32) for i in range(2)]
            gb = ub
            stile = [sb(f"stile{i}", [128, 512], BF16) for i in range(2)]
            sgb = [sb(f"sgb{i}", [128, 512], BF16) for i in range(2)]
            xblk4 = [sb(f"xblk4{i}", [128, 512], F32) for i in range(2)]
            obuf4 = [sb(f"obuf4{i}", [128, 512], F32) for i in range(2)]
            psD = [ph.enter_context(nc.psum_tensor(f"psD{i}", [128, 512], F32)) for i in range(3)]
            psX = [ph.enter_context(nc.psum_tensor(f"psX{i}", [128, 128], BF16)) for i in range(3)]
            steps = []

            def add8(stp):
                steps.append(stp)
                return len(steps) - 1

            i_i8_l = [-1]
            i_cl_p = {0: -1, 1: -1}
            i_mml_p = {0: -1, 1: -1}
            i_u_b = {i: -1 for i in range(3)}
            i_sg_p = {i: -1 for i in range(3)}
            i_tp_b = {i: -1 for i in range(3)}
            i_xa_b = {i: -1 for i in range(3)}
            i_od_p = {i: -1 for i in range(3)}
            mm_ctr = [0]
            tp_ctr = [0]
            for o2 in range(16):
                p = o2 % 2

                def wl8(e, inc, _o2=o2):
                    for ec in range(32):
                        inc(e.dma_start(out=wst2[:, ec * 128:(ec + 1) * 128],
                                        in_=wdnT[ec * 128:(ec + 1) * 128,
                                                 _o2 * 128:(_o2 + 1) * 128]), 16)
                i_wl = add8(Step("sync", 16 * 32, wl8, after=[i_i8_l[0]]))
                i_i8 = add8(Step("scalar", 1, lambda e, inc:
                                 inc(e.activation(i8s2[:, :], wst2[:, :], AF.Copy,
                                                  scale=rwsd[:, :])),
                                 after=[i_wl, i_cl_p[1 - p]]))
                i_i8_l[0] = i_i8
                i_cl = add8(Step("vector", 1, lambda e, inc, _p=p:
                                 inc(e.tensor_scalar(tsl2[_p][:, :], i8s2[:, :], 1.0, -1.0,
                                                     AO.min, AO.max)),
                                 after=[i_i8, i_mml_p[p]]))
                i_cl_p[p] = i_cl
                for tc in range(4):
                    b = mm_ctr[0] % 3
                    bb = mm_ctr[0] % 2
                    mm_ctr[0] += 1
                    i_sl = add8(Step("gpsimd", 16, lambda e, inc, _o2=o2, _tc=tc, _bb=bb:
                                     inc(e.dma_start(out=stile[_bb][:, :],
                                                     in_=s_hbm[:, _o2 * T + _tc * 512:
                                                              _o2 * T + (_tc + 1) * 512]), 16),
                                     after=[i_sg_p[bb]]))
                    i_xl = add8(Step("gpsimd", 16, lambda e, inc, _o2=o2, _tc=tc, _bb=bb:
                                     inc(e.dma_start(
                                         out=xblk4[_bb][:, :].rearrange("p (i c) -> p i c", i=4),
                                         in_=x[_tc * 512:(_tc + 1) * 512,
                                               _o2 * 128:(_o2 + 1) * 128]
                                         .rearrange("(i p) c -> p i c", i=4)), 16),
                                     after=[i_xa_b[bb]]))

                    def mmd(e, inc, _tc=tc, _p=p, _b=b):
                        for ec in range(32):
                            mm = e.matmul(psD[_b][:, :], tsl2[_p][:, ec * 128:(ec + 1) * 128],
                                          K2[:, ec * T + _tc * 512: ec * T + (_tc + 1) * 512],
                                          start=(ec == 0), stop=(ec == 31))
                        inc(mm)
                    i_mm = add8(Step("tensor", 1, mmd, after=[i_cl, i_u_b[b]]))
                    i_u = add8(Step("vector", 1, lambda e, inc, _tc=tc, _b=b, _bb=bb:
                                    inc(e.tensor_mul(ub[_bb][:, :], psD[_b][:, :],
                                                     brep[:, _tc * 512:(_tc + 1) * 512])),
                                    after=[i_mm, i_sg_p[bb]]))
                    i_u_b[b] = i_u
                    i_g = add8(Step("scalar", 1, lambda e, inc, _bb=bb:
                                    inc(e.activation(gb[_bb][:, :], ub[_bb][:, :], AF.Silu)),
                                    after=[i_u]))
                    i_sg = add8(Step("vector", 1, lambda e, inc, _bb=bb:
                                     inc(e.tensor_mul(sgb[_bb][:, :], gb[_bb][:, :],
                                                      stile[_bb][:, :])),
                                     after=[i_g, i_sl]))
                    i_sg_p[bb] = i_sg
                    i_xa_last = -1
                    for i in range(4):
                        xb = tp_ctr[0] % 3
                        tp_ctr[0] += 1
                        i_t = add8(Step("tensor", 1, lambda e, inc, _i=i, _xb=xb, _bb=bb:
                                        inc(e.transpose(psX[_xb][:, :],
                                                        sgb[_bb][:, _i * 128:(_i + 1) * 128],
                                                        idn_s[:, :])),
                                        after=[i_sg, i_tp_b[xb]]))
                        i_xa = add8(Step("vector", 1, lambda e, inc, _i=i, _xb=xb, _bb=bb:
                                         inc(e.tensor_add(obuf4[_bb][:, _i * 128:(_i + 1) * 128],
                                                          psX[_xb][:, :],
                                                          xblk4[_bb][:, _i * 128:(_i + 1) * 128])),
                                         after=[i_t, i_xl, i_od_p[bb]]))
                        i_tp_b[xb] = i_xa
                        i_xa_last = i_xa
                    i_xa_b[bb] = i_xa_last
                    i_od = add8(Step("gpsimd", 16, lambda e, inc, _o2=o2, _tc=tc, _bb=bb:
                                     inc(e.dma_start(
                                         out=out[_tc * 512:(_tc + 1) * 512,
                                                 _o2 * 128:(_o2 + 1) * 128]
                                         .rearrange("(i p) c -> p i c", i=4),
                                         in_=obuf4[_bb][:, :].rearrange("p (i c) -> p i c", i=4)), 16),
                                     after=[i_xa_last]))
                    i_od_p[bb] = i_od
                i_mml_p[p] = i_mm
            ch.run(steps)

        return _finish(nc, ch)


def _finish(nc, ch):
    fin = dict(ch.cnt)
    sems = dict(ch.sems)
    with nc.Block() as blk:
        @blk.gpsimd
        def _(gp):
            for ln, c in fin.items():
                if c > 0:
                    gp.wait_ge(sems[ln], c)
    return nc


def _prep_inputs(L0_attn_x, L0_attn_Wq, L0_attn_Wk, L0_attn_Wv, L0_attn_Wo,
                 L0_attn_lq1, L0_attn_lq2, L0_attn_lk1, L0_attn_lk2,
                 L0_attn_rms_w, L0_ffn_w_up, L0_ffn_w_down):
    cons = _host_consts()
    f32 = np.float32
    wq = np.ascontiguousarray(np.asarray(L0_attn_Wq, dtype=f32)[:, _perm_cols(H)])
    wk = np.ascontiguousarray(np.asarray(L0_attn_Wk, dtype=f32)[:, _perm_cols(KVH)])
    wv = np.ascontiguousarray(np.asarray(L0_attn_Wv, dtype=f32))
    wo = np.ascontiguousarray(np.asarray(L0_attn_Wo, dtype=f32))
    wupT = np.ascontiguousarray(np.asarray(L0_ffn_w_up, dtype=f32).T)
    wdnT = np.ascontiguousarray(np.asarray(L0_ffn_w_down, dtype=f32).T)
    base = {
        "wq": wq, "wk": wk, "wv": wv, "wo": wo, "wupT": wupT, "wdnT": wdnT,
        "lq1": np.ascontiguousarray(np.asarray(L0_attn_lq1, dtype=f32).T),
        "lq2": np.ascontiguousarray(np.asarray(L0_attn_lq2, dtype=f32).T),
        "lk1": np.ascontiguousarray(np.asarray(L0_attn_lk1, dtype=f32).T),
        "lk2": np.ascontiguousarray(np.asarray(L0_attn_lk2, dtype=f32).T),
        "rmsw": np.ascontiguousarray(np.asarray(L0_attn_rms_w, dtype=f32).reshape(128, 1)),
        "c_lo": cons["c_lo"], "s_lo": cons["s_lo"],
        "c_hi": cons["c_hi"], "s_hi": cons["s_hi"],
        "trid": cons["tri"], "h128bd": cons["h128b"], "h128fd": cons["h128f"],
        "idnd": cons["idn"], "ones_rd": cons["ones_r"], "ones_cd": cons["ones_c"],
        "eps1d": np.full((128, 1), 1e-5, dtype=f32),
        "eps2d": np.full((128, 1), 1e-8 / (CR * CR), dtype=f32),
        "ohd": (np.arange(127)[None, :] == 63).astype(f32) * np.ones((128, 1), f32),
    }
    xs = np.asarray(L0_attn_x, dtype=f32)
    in_maps = []
    for b in range(8):
        m = dict(base)
        m["x"] = np.ascontiguousarray(xs[b])
        in_maps.append(m)
    return in_maps


def kernel(L0_attn_x, L0_attn_Wq, L0_attn_Wk, L0_attn_Wv, L0_attn_Wo,
           L0_attn_lq1, L0_attn_lq2, L0_attn_lk1, L0_attn_lk2,
           L0_attn_rms_w, L0_ffn_w_up, L0_ffn_w_down):
    in_maps = _prep_inputs(L0_attn_x, L0_attn_Wq, L0_attn_Wk, L0_attn_Wv, L0_attn_Wo,
                           L0_attn_lq1, L0_attn_lq2, L0_attn_lk1, L0_attn_lk2,
                           L0_attn_rms_w, L0_ffn_w_up, L0_ffn_w_down)
    nc = build_program()
    res = run_bass_kernel_spmd(nc, in_maps, core_ids=list(range(8)))
    return np.stack([res.results[b]["out"] for b in range(8)], axis=0)


# version 21
# speedup vs baseline: 1.1118x; 1.0665x over previous
import numpy as np
import ml_dtypes
import concourse.bass as bass
import concourse.mybir as mybir
from concourse.bass_utils import run_bass_kernel_spmd
from concourse.alu_op_type import AluOpType as AO
from contextlib import ExitStack

F32 = mybir.dt.float32
BF16 = mybir.dt.bfloat16
I8 = mybir.dt.int8
AF = mybir.ActivationFunctionType
AX = mybir.AxisListType.X

T = 2048
E = 2048
H = 16
KVH = 4
NEG = -1e30
LAM = 0.8 - 0.6 * float(np.exp(-0.3 * 0))
CR = 1.0 - LAM
IFF = 4096
NE = E // 128
NF2 = IFF // 128
NW_ELEM = float(E * IFF)
C0 = 1.0 / (127.0 ** 3 * 2048.0 * 64.0)
ISQ = float(1.0 / np.sqrt(128.0))


def _host_consts():
    th = 10000.0 ** (-np.arange(128, dtype=np.float64) / 128.0)
    ang = th[:, None] * np.arange(T, dtype=np.float64)[None, :]
    cos, sin = np.cos(ang), np.sin(ang)
    c_lo = np.concatenate([cos[0:64], cos[0:64]], 0).astype(np.float32)
    s_lo = np.concatenate([-sin[0:64], sin[0:64]], 0).astype(np.float32)
    c_hi = np.concatenate([cos[64:128], cos[64:128]], 0).astype(np.float32)
    s_hi = np.concatenate([-sin[64:128], sin[64:128]], 0).astype(np.float32)
    tri = np.where(np.arange(128)[None, :] > np.arange(128)[:, None],
                   np.float32(NEG), np.float32(0.0)).astype(np.float32)
    ii, jj = np.meshgrid(np.arange(128), np.arange(128), indexing="ij")
    v = (ii & jj).astype(np.int64)
    par = np.zeros((128, 128), dtype=np.int64)
    while v.max() > 0:
        par ^= (v & 1)
        v >>= 1
    h128 = np.where(par == 0, 1.0, -1.0)
    return {"c_lo": c_lo, "s_lo": s_lo, "c_hi": c_hi, "s_hi": s_hi, "tri": tri,
            "h128b": h128.astype(ml_dtypes.bfloat16),
            "h128f": h128.astype(np.float32),
            "idn": np.eye(128).astype(ml_dtypes.bfloat16),
            "ones_r": np.ones((1, 128), dtype=np.float32),
            "ones_c": np.ones((128, 1), dtype=np.float32)}


def _perm_cols(nheads):
    cols = []
    for h in range(nheads):
        b = h * 256
        cols += [b + 2 * j for j in range(64)] + [b + 2 * j + 1 for j in range(64)]
        cols += [b + 2 * j for j in range(64, 128)] + [b + 2 * j + 1 for j in range(64, 128)]
    return np.array(cols)


class Step:
    __slots__ = ("eng", "inc", "fn", "after", "lane", "_lane")

    def __init__(self, eng, inc, fn, after=None, lane=None):
        self.eng, self.inc, self.fn, self.after, self.lane = eng, inc, fn, after, lane


class Ch:
    ENG_LANE = {"tensor": "te", "vector": "ve", "scalar": "sc",
                "sync": "sy", "gpsimd": "gd"}

    def __init__(self, nc, mksem):
        self.nc = nc
        self.mksem = mksem
        self.sems = {}
        self.cnt = {}

    def lane(self, name):
        if name not in self.sems:
            self.sems[name] = self.mksem(name)
            self.cnt[name] = 0
        return self.sems[name]

    def run(self, steps):
        nc = self.nc
        lanes = []
        for s in steps:
            s._lane = s.lane if getattr(s, "lane", None) else self.ENG_LANE[s.eng]
            self.lane(s._lane)
            lanes.append(s._lane)
        base = dict(self.cnt)
        run_cnt = dict(self.cnt)
        post = []
        for s in steps:
            run_cnt[s._lane] += s.inc
            post.append((s._lane, run_cnt[s._lane]))
        waited = set()
        def _preds(i, s):
            a = s.after
            if a is None:
                return [i - 1] if i > 0 else []
            if isinstance(a, int):
                return [a] if a >= 0 else []
            return [j for j in a if j >= 0]
        for i, s in enumerate(steps):
            pr = _preds(i, s)
            if not pr:
                continue
            best = {}
            for j in pr:
                ln, c = post[j]
                if c > best.get(ln, (-1, -1))[0]:
                    best[ln] = (c, j)
            for ln, (c, j) in best.items():
                waited.add(j)
        engs = []
        for e in ("sync", "tensor", "vector", "scalar", "gpsimd"):
            if any(s.eng == e for s in steps):
                engs.append(e)
        with nc.Block() as blk:
            for ename in engs:
                def body(eng, _e=ename):
                    for i, s in enumerate(steps):
                        if s.eng != _e:
                            continue
                        preds = _preds(i, s)
                        if preds:
                            tg = {}
                            for j in preds:
                                ln, c = post[j]
                                tg[ln] = max(tg.get(ln, 0), c)
                            for ln, c in tg.items():
                                eng.wait_ge(self.sems[ln], c)
                        else:
                            for ln, c in base.items():
                                if c > 0:
                                    eng.wait_ge(self.sems[ln], c)
                        mysem = self.sems[s._lane]
                        got = {"n": 0}

                        def inc(instr, k=1, _g=got, _s=mysem):
                            _g["n"] += k
                            return instr.then_inc(_s, k)
                        s.fn(eng, inc)
                        assert got["n"] == s.inc, \
                            f"step {i} ({s.eng}): declared inc {s.inc} != emitted {got['n']}"
                        if s.eng in ("sync", "gpsimd") and i in waited:
                            eng.wait_ge(mysem, post[i][1])
                getattr(blk, ename)(body)
        self.cnt = run_cnt


def build_program(taps=(), upto=99):
    nc = bass.Bass("TRN2", target_bir_lowering=False)
    dt = lambda n, sh, ty, kind: nc.dram_tensor(n, sh, ty, kind=kind).ap()
    tk = lambda n: "ExternalOutput" if n in taps else "Internal"

    x = dt("x", [T, E], F32, "ExternalInput")
    wq = dt("wq", [E, H * 256], F32, "ExternalInput")
    wk = dt("wk", [E, KVH * 256], F32, "ExternalInput")
    wv = dt("wv", [E, KVH * 128], F32, "ExternalInput")
    wo = dt("wo", [E, E], F32, "ExternalInput")
    wupT = dt("wupT", [E, IFF], F32, "ExternalInput")
    wdnT = dt("wdnT", [IFF, E], F32, "ExternalInput")
    lq1d = dt("lq1", [128, 16], F32, "ExternalInput")
    lq2d = dt("lq2", [128, 16], F32, "ExternalInput")
    lk1d = dt("lk1", [128, 16], F32, "ExternalInput")
    lk2d = dt("lk2", [128, 16], F32, "ExternalInput")
    rmswd = dt("rmsw", [128, 1], F32, "ExternalInput")
    c_lod = dt("c_lo", [128, T], F32, "ExternalInput")
    s_lod = dt("s_lo", [128, T], F32, "ExternalInput")
    c_hid = dt("c_hi", [128, T], F32, "ExternalInput")
    s_hid = dt("s_hi", [128, T], F32, "ExternalInput")
    trid = dt("trid", [128, 128], F32, "ExternalInput")
    h128bd = dt("h128bd", [128, 128], BF16, "ExternalInput")
    h128fd = dt("h128fd", [128, 128], F32, "ExternalInput")
    idnd = dt("idnd", [128, 128], BF16, "ExternalInput")
    ones_rd = dt("ones_rd", [1, 128], F32, "ExternalInput")
    ones_cd = dt("ones_cd", [128, 1], F32, "ExternalInput")
    eps1d = dt("eps1d", [128, 1], F32, "ExternalInput")
    eps2d = dt("eps2d", [128, 1], F32, "ExternalInput")
    ohd = dt("ohd", [128, 127], F32, "ExternalInput")
    out = dt("out", [T, E], F32, "ExternalOutput")

    xnT_h = dt("xnT_h", [128, NE * T], BF16, tk("xnT_h"))
    xq_h = dt("xq_h", [128, NE * T], BF16, tk("xq_h"))
    qT_h = dt("qT_h", [128, H * 2 * T], BF16, tk("qT_h"))
    kT_h = dt("kT_h", [128, KVH * 2 * T], BF16, tk("kT_h"))
    v_h = dt("v_h", [128, KVH * T], BF16, tk("v_h"))
    s_hbm = dt("s_hbm", [128, NE * T], BF16, tk("s_hbm"))
    h2_h = dt("h2_h", [128, NF2 * T], F32, tk("h2_h"))
    att_t = dt("att_t", [128, H * T], BF16, "ExternalOutput") if "att_t" in taps else None
    k2_t = dt("k2_t", [128, NF2 * T], BF16, "ExternalOutput") if "k2_t" in taps else None
    aux_t = dt("aux_t", [8, 4096], F32, "ExternalOutput") if "aux_t" in taps else None

    st = ExitStack()
    with st:
        sbuf = lambda n, sh, ty, **kw: st.enter_context(nc.sbuf_tensor(n, sh, ty, **kw))
        ch = Ch(nc, lambda n: st.enter_context(nc.semaphore("L_" + n)))
        vssem = st.enter_context(nc.semaphore("vs"))
        vs_n = [0]

        def vsync(e, instr):
            instr.then_inc(vssem, 1)
            vs_n[0] += 1
            e.wait_ge(vssem, vs_n[0])
            return instr

        lamrep = sbuf("lamrep", [128, 16], F32)
        rms_s = sbuf("rms_s", [128, 1], F32)
        rwsu = sbuf("rwsu", [128, 1], F32)
        rwsd = sbuf("rwsd", [128, 1], F32)
        wsu_s = sbuf("wsu_s", [1, 1], F32)
        wsd_s = sbuf("wsd_s", [1, 1], F32)
        combo = sbuf("combo", [1, 1], F32)
        mrow1 = sbuf("mrow1", [1, T], F32)
        betar = sbuf("betar", [1, T], F32)
        tri_s = sbuf("tri_s", [128, 128], F32)
        h128b_s = sbuf("h128b_s", [128, 128], BF16)
        h128f_s = sbuf("h128f_s", [128, 128], F32)
        idn_s = sbuf("idn_s", [128, 128], BF16)
        onr = sbuf("onr", [1, 128], F32)
        onc = sbuf("onc", [128, 1], F32)
        eps1_s = sbuf("eps1_s", [128, 1], F32)
        eps2_s = sbuf("eps2_s", [128, 1], F32)
        oh_s = sbuf("oh_s", [128, 127], F32)

        with ExitStack() as ph:
            sb = lambda n, sh, ty: ph.enter_context(nc.sbuf_tensor(n, sh, ty))
            pt = lambda n, sh, ty=F32: ph.enter_context(nc.psum_tensor(n, sh, ty))
            l1, l2 = sb("l1", [128, 16], F32), sb("l2", [128, 16], F32)
            l3, l4 = sb("l3", [128, 16], F32), sb("l4", [128, 16], F32)
            p0, p1, p2 = pt("p0", [128, 16]), pt("p1", [128, 16]), pt("p2", [128, 16])
            row1, row2 = sb("row1", [1, 16], F32), sb("row2", [1, 16], F32)
            lamrow = sb("lamrow", [1, 16], F32)

            loads = [(tri_s, trid), (h128b_s, h128bd), (h128f_s, h128fd),
                     (idn_s, idnd), (onr, ones_rd), (onc, ones_cd),
                     (rms_s, rmswd), (l1, lq1d), (l2, lq2d), (l3, lk1d), (l4, lk2d),
                     (eps1_s, eps1d), (eps2_s, eps2d), (oh_s, ohd)]
            steps = [Step("sync", 16 * len(loads),
                          lambda e, inc, _l=loads: [inc(e.dma_start(out=d[:, :], in_=s[:, :]), 16)
                                                    for d, s in _l])]
            steps += [
                Step("vector", 1, lambda e, inc: (
                    e.tensor_mul(l1[:, :], l1[:, :], l3[:, :]),
                    inc(e.tensor_mul(l2[:, :], l2[:, :], l4[:, :])))),
                Step("tensor", 1, lambda e, inc: (
                    e.matmul(p0[0:1, :], onc[:, :], l1[:, :], start=True, stop=True),
                    inc(e.matmul(p1[0:1, :], onc[:, :], l2[:, :], start=True, stop=True)))),
                Step("scalar", 1, lambda e, inc: (
                    e.activation(row1[:, :], p0[0:1, :], AF.Exp),
                    inc(e.activation(row2[:, :], p1[0:1, :], AF.Exp)))),
                Step("vector", 1, lambda e, inc: (
                    vsync(e, e.tensor_sub(lamrow[:, :], row1[:, :], row2[:, :])),
                    inc(e.tensor_scalar_add(lamrow[:, :], lamrow[:, :], float(LAM))))),
                Step("tensor", 1, lambda e, inc:
                     inc(e.matmul(p2[:, :], onr[:, :], lamrow[:, :], start=True, stop=True))),
                Step("scalar", 1, lambda e, inc:
                     inc(e.copy(out=lamrep[:, :], in_=p2[:, :]))),
            ]
            ch.run(steps)

        if upto < 1:
            return _finish(nc, ch)
        with ExitStack() as ph:
            sb = lambda n, sh, ty: ph.enter_context(nc.sbuf_tensor(n, sh, ty))
            pt = lambda n, sh: ph.enter_context(nc.psum_tensor(n, sh, F32))
            wst = [sb(f"wst{i}", [128, 2048], F32) for i in range(2)]
            acc = sb("acc", [128, 64], F32)
            ucol, dcol = sb("ucol", [128, 1], F32), sb("dcol", [128, 1], F32)
            rtmp = sb("rtmp", [1, 1], F32)
            pA, pC = pt("pA", [128, 1]), pt("pC", [128, 1])
            tiles = [(wupT, r, c) for r in range(16) for c in range(2)] + \
                    [(wdnT, r, 0) for r in range(32)]
            steps = []
            ld_idx = {}
    
            for i, (w, r, c) in enumerate(tiles):
                eng = "sync" if i % 2 == 0 else "gpsimd"
                prev_red = 2 * (i - 2) + 1 if i >= 2 else -1
                ld_idx[i] = len(steps)
                steps.append(Step(eng, 16,
                                  lambda e, inc, _w=w, _r=r, _c=c, _i=i:
                                  inc(e.dma_start(out=wst[_i % 2][:, :],
                                                  in_=_w[_r * 128:(_r + 1) * 128,
                                                         _c * 2048:(_c + 1) * 2048]), 16),
                                  after=[prev_red] if prev_red >= 0 else -1))
                steps.append(Step("vector", 1,
                                  lambda e, inc, _i=i:
                                  inc(e.reduce_sum(acc[:, _i:_i + 1], wst[_i % 2][:, :],
                                                   axis=AX, apply_absolute_value=True)),
                                  after=[ld_idx[i]]))
            for name, lo, hi, col, wsx, rws in [("u", 0, 32, ucol, wsu_s, rwsu),
                                                ("d", 32, 64, dcol, wsd_s, rwsd)]:
                steps += [
                    Step("vector", 1, lambda e, inc, _lo=lo, _hi=hi, _col=col:
                         inc(e.reduce_sum(_col[:, :], acc[:, _lo:_hi], axis=AX))),
                    Step("tensor", 1, lambda e, inc, _col=col:
                         inc(e.matmul(pA[0:1, :], onc[:, :], _col[:, :], start=True, stop=True))),
                    Step("vector", 1, lambda e, inc, _w=wsx:
                         inc(e.tensor_scalar(_w[:, :], pA[0:1, :], 1.0 / NW_ELEM, 1e-5,
                                             AO.mult, AO.max))),
                    Step("vector", 1, lambda e, inc, _w=wsx:
                         inc(e.reciprocal(rtmp[:, :], _w[:, :]))),
                    Step("tensor", 1, lambda e, inc:
                         inc(e.matmul(pC[:, :], onr[:, :], rtmp[:, :], start=True, stop=True))),
                    Step("scalar", 1, lambda e, inc, _r=rws:
                         inc(e.copy(out=_r[:, :], in_=pC[:, :]))),
                ]
            steps.append(Step("vector", 1, lambda e, inc: (
                vsync(e, e.tensor_mul(combo[:, :], wsu_s[:, :], wsu_s[:, :])),
                vsync(e, e.tensor_mul(combo[:, :], combo[:, :], wsd_s[:, :])),
                inc(e.tensor_scalar_mul(combo[:, :], combo[:, :], float(C0))))))
            ch.run(steps)

        if upto < 2:
            return _finish(nc, ch)
        with ExitStack() as ph:
            sb = lambda n, sh, ty: ph.enter_context(nc.sbuf_tensor(n, sh, ty))
            xt = [sb(f"xt{i}", [128, 2048], F32) for i in range(2)]
            dump = sb("dump", [128, 2048], F32)
            sqc = [sb(f"sqc{i}", [128, 1], F32) for i in range(2)]
            sumc = [sb(f"sumc{i}", [128, 1], F32) for i in range(2)]
            mean = [sb(f"mean{i}", [128, 1], F32) for i in range(2)]
            msq = [sb(f"msq{i}", [128, 1], F32) for i in range(2)]
            var = [sb(f"var{i}", [128, 1], F32) for i in range(2)]
            rstd = [sb(f"rstd{i}", [128, 1], F32) for i in range(2)]
            negmr = [sb(f"negmr{i}", [128, 1], F32) for i in range(2)]
            xnbf = [sb(f"xnbf{i}", [128, 2048], BF16) for i in range(2)]
            xstg = [sb(f"xstg{i}", [128, 2048], BF16) for i in range(2)]
            psT = [ph.enter_context(nc.psum_tensor(f"psT{i}", [128, 128], BF16)) for i in range(4)]
            steps = []

            def add1(stp):
                steps.append(stp)
                return len(steps) - 1

            pv_v1 = {0: -1, 1: -1}
            pv_v2 = {0: -1, 1: -1}
            pv_tpl = {0: -1, 1: -1}
            pv_st = {0: -1, 1: -1}
            pv_c = {0: -1, 1: -1}
            for tt in range(16):
                p = tt % 2
                i_ld = add1(Step("sync", 16, lambda e, inc, _tt=tt, _p=p:
                            inc(e.dma_start(out=xt[_p][:, :],
                                            in_=x[_tt * 128:(_tt + 1) * 128, :]), 16),
                            after=[pv_v2[p]]))
                i_sq = add1(Step("scalar", 1, lambda e, inc, _p=p:
                            inc(e.activation(dump[:, :], xt[_p][:, :], AF.Square,
                                             accum_out=sqc[_p][:, :])),
                            after=[i_ld, pv_v1[p]]))
                i_v1 = add1(Step("vector", 1, lambda e, inc, _p=p: (
                    vsync(e, e.reduce_sum(sumc[_p][:, :], xt[_p][:, :], axis=AX)),
                    vsync(e, e.tensor_scalar_mul(mean[_p][:, :], sumc[_p][:, :], 1.0 / 2048.0)),
                    vsync(e, e.tensor_mul(msq[_p][:, :], mean[_p][:, :], mean[_p][:, :])),
                    vsync(e, e.tensor_scalar_mul(var[_p][:, :], sqc[_p][:, :], 1.0 / 2048.0)),
                    inc(e.tensor_sub(var[_p][:, :], var[_p][:, :], msq[_p][:, :]))),
                    after=[i_sq]))
                pv_v1[p] = i_v1
                i_sr = add1(Step("scalar", 1, lambda e, inc, _p=p:
                            inc(e.activation(rstd[_p][:, :], var[_p][:, :], AF.Sqrt,
                                             bias=eps1_s[:, :])),
                            after=[i_v1]))
                i_v2 = add1(Step("vector", 1, lambda e, inc, _p=p: (
                    vsync(e, e.reciprocal(rstd[_p][:, :], rstd[_p][:, :])),
                    vsync(e, e.tensor_mul(negmr[_p][:, :], mean[_p][:, :], rstd[_p][:, :])),
                    vsync(e, e.tensor_scalar_mul(negmr[_p][:, :], negmr[_p][:, :], -1.0)),
                    inc(e.tensor_scalar(xnbf[_p][:, :], xt[_p][:, :], rstd[_p][:, :],
                                        negmr[_p][:, :], AO.mult, AO.add))),
                    after=[i_sr, pv_tpl[p]]))
                pv_v2[p] = i_v2
                i_c = -1
                for fp in range(8):
                    fq = fp % 2
                    i_t = add1(Step("tensor", 1, lambda e, inc, _f=2 * fp, _p=p, _q=fq: (
                        e.transpose(psT[2 * _q][:, :], xnbf[_p][:, _f * 128:(_f + 1) * 128],
                                    idn_s[:, :]),
                        inc(e.transpose(psT[2 * _q + 1][:, :],
                                        xnbf[_p][:, (_f + 1) * 128:(_f + 2) * 128],
                                        idn_s[:, :]))),
                        after=[i_v2, pv_c[fq]]))
                    i_c = add1(Step("scalar", 1, lambda e, inc, _f=2 * fp, _p=p, _q=fq: (
                        e.copy(out=xstg[_p][:, _f * 128:(_f + 1) * 128], in_=psT[2 * _q][:, :]),
                        inc(e.copy(out=xstg[_p][:, (_f + 1) * 128:(_f + 2) * 128],
                                   in_=psT[2 * _q + 1][:, :]))),
                        after=[i_t] + ([pv_st[p]] if fp == 0 else [])))
                    pv_c[fq] = i_c
                pv_tpl[p] = i_c
                i_st = add1(Step("gpsimd", 16, lambda e, inc, _tt=tt, _p=p:
                            inc(e.dma_start(
                                out=xnT_h.rearrange("p (f t) -> p f t", f=16)[
                                    :, :, _tt * 128:(_tt + 1) * 128],
                                in_=xstg[_p][:, :].rearrange("p (f c) -> p f c", f=16)), 16),
                            after=[i_c]))
                pv_st[p] = i_st
            ch.run(steps)

        if upto < 3:
            return _finish(nc, ch)
        with ExitStack() as ph:
            sb = lambda n, sh, ty: ph.enter_context(nc.sbuf_tensor(n, sh, ty))
            W = [sb(f"Wf{i}", [128, 512], F32) for i in range(48)]
            xsl = [sb(f"xsl{i}", [128, 512], BF16) for i in range(4)]
            mx = [sb(f"mx{i}", [128, 512], F32) for i in range(2)]
            mn = [sb(f"mn{i}", [128, 512], F32) for i in range(2)]
            ptmp = [sb(f"ptmp{i}", [64, 512], F32) for i in range(2)]
            Sb2 = [sb(f"Sb2{i}", [128, 512], F32) for i in range(2)]
            qi8 = [sb(f"qi8{i}", [128, 512], I8) for i in range(4)]
            qbf = [sb(f"qbf{i}", [128, 16 * 512], BF16) for i in range(2)]
            psH2 = [ph.enter_context(nc.psum_tensor(f"psH{i}", [128, 512], F32))
                    for i in range(4)]
            psR = ph.enter_context(nc.psum_tensor("psR", [128, 512], F32))

            def butterflies(e, inc, base, bk, nt=16):
                slot = list(range(nt))
                free = list(range(nt, nt + nt // 2))
                nbits = int(np.log2(nt))
                for p in range(nbits):
                    b = 1 << p
                    pairs = [(i, i | b) for i in range(nt) if not (i & b)]
                    last = None
                    for idx, (i, j) in enumerate(pairs):
                        last = e.tensor_sub(W[base + free[idx]][:, :], W[base + slot[i]][:, :],
                                            W[base + slot[j]][:, :])
                    vsync(e, last)
                    last = None
                    for i, j in pairs:
                        last = e.tensor_add(W[base + slot[i]][:, :], W[base + slot[i]][:, :],
                                            W[base + slot[j]][:, :])
                    vsync(e, last)
                    for idx, (i, j) in enumerate(pairs):
                        slot[j], free[idx] = free[idx], slot[j]
                for op, dstbuf, neg in ((AO.max, mx[bk], False), (AO.min, mn[bk], True)):
                    last = None
                    for t in range(nt // 2):
                        last = e.tensor_tensor(W[base + free[t]][:, :],
                                               W[base + slot[2 * t]][:, :],
                                               W[base + slot[2 * t + 1]][:, :], op)
                    vsync(e, last)
                    gap = 1
                    while gap < nt // 2:
                        last = None
                        for t in range(0, nt // 2, 2 * gap):
                            last = e.tensor_tensor(W[base + free[t]][:, :],
                                                   W[base + free[t]][:, :],
                                                   W[base + free[t + gap]][:, :], op)
                        vsync(e, last)
                        gap *= 2
                    if neg:
                        vsync(e, e.tensor_scalar_mul(dstbuf[:, :], W[base + free[0]][:, :], -1.0))
                    else:
                        vsync(e, e.tensor_copy(dstbuf[:, :], W[base + free[0]][:, :]))
                inc(e.tensor_max(mx[bk][:, :], mx[bk][:, :], mn[bk][:, :]))
                return slot

            steps = []

            def add2(stp):
                steps.append(stp)
                return len(steps) - 1

            slot_store = {}
            pv_xl = {i: -1 for i in range(4)}
            pv_mm = {i: -1 for i in range(4)}
            pv_q = {0: -1, 1: -1}
            pv_qs = {0: -1, 1: -1}
            pv_tree = {0: -1, 1: -1}
            pv_psr = [-1]
            for tc in range(4):
                bk = tc % 2
                base = bk * 24
                i_ev_last = -1
                for f in range(16):
                    r4 = f % 4
                    i_xl = add2(Step("sync", 16, lambda e, inc, _f=f, _tc=tc, _r=r4:
                                     inc(e.dma_start(out=xsl[_r][:, :],
                                                     in_=xnT_h[:, _f * T + _tc * 512:
                                                               _f * T + (_tc + 1) * 512]), 16),
                                     after=[pv_xl[r4]]))
                    i_mm = add2(Step("tensor", 1, lambda e, inc, _r=r4:
                                     inc(e.matmul(psH2[_r][:, :], h128b_s[:, :], xsl[_r][:, :],
                                                  start=True, stop=True)),
                                     after=[i_xl, pv_mm[r4]]))
                    pv_xl[r4] = i_mm
                    i_ev = add2(Step("scalar", 1, lambda e, inc, _f=f, _r=r4, _base=base:
                                     inc(e.copy(out=W[_base + _f][:, :], in_=psH2[_r][:, :])),
                                     after=[i_mm] + ([pv_q[bk]] if f == 0 else [])))
                    pv_mm[r4] = i_ev
                    i_ev_last = i_ev
                i_bf = add2(Step("vector", 1,
                                 lambda e, inc, _base=base, _bk=bk:
                                 slot_store.__setitem__(_bk, butterflies(e, inc, _base, _bk)),
                                 after=[i_ev_last, pv_tree[bk]]))
                i_prev = i_bf
                for r in range(7):
                    p2 = 64 >> r
                    i_d = add2(Step("sync", 16, lambda e, inc, _p=p2, _bk=bk:
                                    inc(e.dma_start(out=ptmp[_bk][0:_p, :],
                                                    in_=mx[_bk][_p:2 * _p, :]), 16),
                                    after=[i_prev]))
                    i_prev = add2(Step("vector", 1, lambda e, inc, _p=p2, _bk=bk:
                                       inc(e.tensor_max(mx[_bk][0:_p, :], mx[_bk][0:_p, :],
                                                        ptmp[_bk][0:_p, :])),
                                       after=[i_d]))
                i_mrow = add2(Step("vector", 1, lambda e, inc, _tc=tc, _bk=bk:
                                   inc(e.tensor_scalar_max(
                                       mrow1[0:1, _tc * 512:(_tc + 1) * 512],
                                       mx[_bk][0:1, :], 1e-5)),
                                   after=[i_prev]))
                pv_tree[bk] = i_mrow
                i_rp = add2(Step("tensor", 1, lambda e, inc, _tc=tc:
                                 inc(e.matmul(psR[:, :], onr[:, :],
                                              mrow1[0:1, _tc * 512:(_tc + 1) * 512],
                                              start=True, stop=True)),
                                 after=[i_mrow, pv_psr[0]]))
                i_sb = add2(Step("vector", 1, lambda e, inc, _bk=bk: (
                    vsync(e, e.tensor_scalar_mul(Sb2[_bk][:, :], psR[:, :], 1.0 / 127.0)),
                    inc(e.reciprocal(Sb2[_bk][:, :], Sb2[_bk][:, :]))),
                    after=[i_rp, pv_q[bk]]))
                pv_psr[0] = i_sb

                def qf2(e, inc, _bk=bk, _base=base):
                    s = slot_store[_bk]
                    last = None
                    for g in range(4):
                        for j in range(4):
                            last = e.tensor_mul(qi8[j][:, :],
                                                W[_base + s[g * 4 + j]][:, :],
                                                Sb2[_bk][:, :])
                        vsync(e, last)
                        for j in range(4):
                            f = g * 4 + j
                            last = e.tensor_copy(qbf[_bk][:, f * 512:(f + 1) * 512],
                                                 qi8[j][:, :])
                        if g < 3:
                            vsync(e, last)
                    inc(last)
                i_q = add2(Step("vector", 1, qf2, after=[i_sb, pv_qs[bk]]))
                pv_q[bk] = i_q
                i_qs = add2(Step("gpsimd", 16 * 16, lambda e, inc, _tc=tc, _bk=bk: [
                    inc(e.dma_start(out=xq_h[:, _f * T + _tc * 512:
                                             _f * T + (_tc + 1) * 512],
                                    in_=qbf[_bk][:, _f * 512:(_f + 1) * 512]), 16)
                    for _f in range(16)], after=[i_q]))
                pv_qs[bk] = i_qs
            ch.run(steps)

        if upto < 4:
            return _finish(nc, ch)
        with ExitStack() as ph:
            sb = lambda n, sh, ty: ph.enter_context(nc.sbuf_tensor(n, sh, ty))
            ctab = [sb("ctab0", [128, T], F32), sb("ctab1", [128, T], F32)]
            stab = [sb("stab0", [128, T], F32), sb("stab1", [128, T], F32)]
            wqst = [sb(f"wqst{i}", [128, 8 * 2048], BF16) for i in range(2)]
            wkst = [sb(f"wkst{i}", [128, 2 * 2048], BF16) for i in range(2)]
            wvst = [sb(f"wvst{i}", [128, 2048], BF16) for i in range(2)]
            xns = [sb(f"xns{i}", [128, 16 * 512], BF16) for i in range(2)]
            Xc = [sb(f"Xc{i}", [128, 512], F32) for i in range(3)]
            Xs = [sb(f"Xs{i}", [128, 512], F32) for i in range(3)]
            m1b = [sb(f"m1b{i}", [128, 512], F32) for i in range(3)]
            qkst = [sb(f"qkst{i}", [128, 512], BF16) for i in range(3)]
            vbfb = sb("vbfb", [128, 512], BF16)
            vstg = [sb(f"vstg{i}", [128, 128], BF16) for i in range(4)]
            psP = [ph.enter_context(nc.psum_tensor(f"psP{i}", [128, 512], F32)) for i in range(4)]
            psV = [ph.enter_context(nc.psum_tensor(f"psV{i}", [128, 128], BF16)) for i in range(4)]

            ch.run([Step("sync", 16 * 4, lambda e, inc: [
                inc(e.dma_start(out=ctab[0][:, :], in_=c_lod[:, :]), 16),
                inc(e.dma_start(out=stab[0][:, :], in_=s_lod[:, :]), 16),
                inc(e.dma_start(out=ctab[1][:, :], in_=c_hid[:, :]), 16),
                inc(e.dma_start(out=stab[1][:, :], in_=s_hid[:, :]), 16)])])

            steps = []

            def add(stp):
                steps.append(stp)
                return len(steps) - 1

            i_wl_prev = {0: -1, 1: -1}
            i_xl_prev = {0: -1, 1: -1}
            i_ev_bank = {i: -1 for i in range(4)}
            i_vv_bank = {i: -1 for i in range(4)}
            i_rope_buf = {i: -1 for i in range(3)}
            i_vstg_buf = {i: -1 for i in range(4)}
            i_vbf = [-1]
            ot_ctr = [0]
            v_ctr = [0]
            for g in range(KVH):
                gpar = g % 2

                def wload(e, inc, _g=g, _p=gpar):
                    for c2 in range(2):
                        for ec in range(16):
                            inc(e.dma_start(
                                out=wkst[_p][:, c2 * 2048 + ec * 128: c2 * 2048 + (ec + 1) * 128],
                                in_=wk[ec * 128:(ec + 1) * 128,
                                       _g * 256 + c2 * 128: _g * 256 + (c2 + 1) * 128]), 16)
                    for ec in range(16):
                        inc(e.dma_start(out=wvst[_p][:, ec * 128:(ec + 1) * 128],
                                        in_=wv[ec * 128:(ec + 1) * 128,
                                               _g * 128:(_g + 1) * 128]), 16)
                    for hl in range(4):
                        for c2 in range(2):
                            for ec in range(16):
                                col = (4 * _g + hl) * 256 + c2 * 128
                                inc(e.dma_start(
                                    out=wqst[_p][:, (hl * 2 + c2) * 2048 + ec * 128:
                                                 (hl * 2 + c2) * 2048 + (ec + 1) * 128],
                                    in_=wq[ec * 128:(ec + 1) * 128, col:col + 128]), 16)
                i_wl = add(Step("gpsimd", 16 * 176, wload, after=[i_wl_prev[gpar]]))

                for tc in range(4):
                    xpar = (g * 4 + tc) % 2

                    def xload(e, inc, _tc=tc, _p=xpar):
                        for f in range(16):
                            inc(e.dma_start(out=xns[_p][:, f * 512:(f + 1) * 512],
                                            in_=xnT_h[:, f * T + _tc * 512:
                                                      f * T + (_tc + 1) * 512]), 16)
                    i_xl = add(Step("sync", 16 * 16, xload, after=[i_xl_prev[xpar]]))

                    ots = [("k", c2, None) for c2 in range(2)] + [("v", None, None)] + \
                          [("q", hl, c2) for hl in range(4) for c2 in range(2)]
                    for kind, a1, a2 in ots:
                        bank = ot_ctr[0] % 4
                        ot_ctr[0] += 1
                        if kind == "v":
                            def mmv(e, inc, _p=gpar, _xp=xpar):
                                for ec in range(16):
                                    mm = e.matmul(psP[0][:, :] if False else psP[_b][:, :],
                                                  wvst[_p][:, ec * 128:(ec + 1) * 128],
                                                  xns[_xp][:, ec * 512:(ec + 1) * 512],
                                                  start=(ec == 0), stop=(ec == 15))
                                inc(mm)
                            def mmv2(e, inc, _p=gpar, _xp=xpar, _b=bank):
                                for ec in range(16):
                                    mm = e.matmul(psP[_b][:, :],
                                                  wvst[_p][:, ec * 128:(ec + 1) * 128],
                                                  xns[_xp][:, ec * 512:(ec + 1) * 512],
                                                  start=(ec == 0), stop=(ec == 15))
                                inc(mm)
                            i_mm = add(Step("tensor", 1, mmv2,
                                            after=[i_wl, i_xl, i_ev_bank[bank]]))
                            i_ev = add(Step("scalar", 1, lambda e, inc, _b=bank:
                                            inc(e.copy(out=vbfb[:, :], in_=psP[_b][:, :])),
                                            after=[i_mm, i_vbf[0]]))
                            i_ev_bank[bank] = i_ev
                            prev_tp = i_ev
                            for i in range(4):
                                vb = v_ctr[0] % 4
                                v_ctr[0] += 1
                                i_t = add(Step("tensor", 1, lambda e, inc, _i=i, _vb=vb:
                                               inc(e.transpose(psV[_vb][:, :],
                                                               vbfb[:, _i * 128:(_i + 1) * 128],
                                                               idn_s[:, :])),
                                               after=[prev_tp, i_vv_bank[vb]]))
                                i_e2 = add(Step("scalar", 1, lambda e, inc, _vb=vb:
                                               inc(e.copy(out=vstg[_vb][:, :], in_=psV[_vb][:, :])),
                                               after=[i_t, i_vstg_buf[vb]]))
                                i_vv_bank[vb] = i_e2
                                i_d = add(Step("gpsimd", 16, lambda e, inc, _g=g, _tc=tc, _i=i, _vb=vb:
                                               inc(e.dma_start(
                                                   out=v_h[:, _g * T + (_tc * 4 + _i) * 128:
                                                           _g * T + (_tc * 4 + _i + 1) * 128],
                                                   in_=vstg[_vb][:, :]), 16),
                                               after=[i_e2]))
                                i_vstg_buf[vb] = i_d
                            i_vbf[0] = i_t
                        else:
                            if kind == "k":
                                wsrc, wcol = wkst[gpar], a1 * 2048
                                comp = a1
                                dsth, dcol = kT_h, (g * 2 + a1) * T + tc * 512
                            else:
                                wsrc, wcol = wqst[gpar], (a1 * 2 + a2) * 2048
                                comp = a2
                                dsth, dcol = qT_h, ((4 * g + a1) * 2 + a2) * T + tc * 512
                            rpar = ot_ctr[0] % 3

                            def mmp(e, inc, _w=wsrc, _wc=wcol, _xp=xpar, _b=bank):
                                for ec in range(16):
                                    mm = e.matmul(psP[_b][:, :],
                                                  _w[:, _wc + ec * 128: _wc + (ec + 1) * 128],
                                                  xns[_xp][:, ec * 512:(ec + 1) * 512],
                                                  start=(ec == 0), stop=(ec == 15))
                                inc(mm)
                            i_mm = add(Step("tensor", 1, mmp,
                                            after=[i_wl, i_xl, i_ev_bank[bank]]))
                            i_ev = add(Step("scalar", 1, lambda e, inc, _b=bank, _r=rpar: (
                                e.copy(out=Xc[_r][:, :], in_=psP[_b][:, :]),
                                e.copy(out=Xs[_r][0:64, :], in_=psP[_b][64:128, :]),
                                inc(e.copy(out=Xs[_r][64:128, :], in_=psP[_b][0:64, :]))),
                                after=[i_mm, i_rope_buf[rpar]]))
                            i_ev_bank[bank] = i_ev
                            i_ro = add(Step("vector", 1, lambda e, inc, _c=comp, _tc=tc, _r=rpar: (
                                e.tensor_mul(m1b[_r][:, :], Xc[_r][:, :],
                                             ctab[_c][:, _tc * 512:(_tc + 1) * 512]),
                                vsync(e, e.tensor_mul(Xs[_r][:, :], Xs[_r][:, :],
                                                      stab[_c][:, _tc * 512:(_tc + 1) * 512])),
                                inc(e.tensor_add(qkst[_r][:, :], m1b[_r][:, :], Xs[_r][:, :]))),
                                after=[i_ev]))
                            i_d = add(Step("gpsimd", 16, lambda e, inc, _d=dsth, _dc=dcol, _r=rpar:
                                           inc(e.dma_start(out=_d[:, _dc:_dc + 512],
                                                           in_=qkst[_r][:, :]), 16),
                                           after=[i_ro]))
                            i_rope_buf[rpar] = i_d
                    i_xl_prev[xpar] = i_mm if True else i_xl
                i_wl_prev[gpar] = i_mm
            ch.run(steps)

        with ExitStack() as ph:
            sb = lambda n, sh, ty, **kw: ph.enter_context(nc.sbuf_tensor(n, sh, ty, **kw))
            attT = sb("attT", [128, H * T], BF16)
            ph4i = ph.enter_context(ExitStack())
            sb4 = sb
            sb = lambda n, sh, ty, **kw: ph4i.enter_context(nc.sbuf_tensor(n, sh, ty, **kw))
            kg = [sb(f"kg{i}", [128, 2 * T], BF16) for i in range(2)]
            vg = [sb(f"vg{i}", [128, T], BF16) for i in range(2)]
            qh = [sb(f"qh{i}", [128, 2 * T], BF16) for i in range(2)]
            e0b = sb("e0b", [128, 2 * T], BF16)
            e1b = sb("e1b", [128, 2 * T], BF16)
            Pf = sb("Pf", [128, T], BF16)
            Pb = sb("Pb", [128, 8 * T], BF16)
            PT = sb("PT", [128, 16 * 512], BF16, is_transpose_dest=True)
            A0 = sb("A0", [128, 8], F32)
            A1 = sb("A1", [128, 8], F32)
            Zc = sb("Zc", [128, 8], F32)
            S0 = sb("S0", [128, 2], F32)
            S1 = sb("S1", [128, 2], F32)
            SZ = sb("SZ", [128, 2], F32)
            r0 = sb("r0", [128, 2], F32)
            lr1 = sb("lr1", [128, 2], F32)
            r2 = sb("r2", [128, 2], F32)
            sqb = sb("sqb", [128, 512], F32)
            msb = sb("msb", [128, 512], F32)
            msrow = sb("msrow", [1, 512], F32)
            taub = sb("taub", [128, 512], F32)
            taurow = sb("taurow", [1, 2 * 512], F32)
            psE0 = [ph.enter_context(nc.psum_tensor(f"psE0{i}", [128, 512], F32)) for i in range(2)]
            psE1 = [ph4i.enter_context(nc.psum_tensor(f"psE1{i}", [128, 512], F32)) for i in range(2)]
            psT4 = [ph4i.enter_context(nc.psum_tensor(f"psT4{i}", [128, 128], BF16))
                    for i in range(2)]
            psAt = ph4i.enter_context(nc.psum_tensor("psAt", [128, 512], F32))
            psRo = ph4i.enter_context(nc.psum_tensor("psRo", [128, 512], F32))
            psRp = psRo

            qc_glob = [0]
            pair_glob = [0]
            tptb_ctr = [0]
            tp_evac_b = {0: -1, 1: -1}
            steps = []
            idx_exp_bank = {0: -1, 1: -1}
            idx_attmm_prev = -1
            idx_attmm_prev2 = -1
            idx_attts_prev = -1
            idx_sq_prev = -1
            idx_rowmm_prev = -1
            idx_mscp_prev = -1
            idx_msdma_prev = -1
            idx_ptp_prev = {0: [], 1: []}
            i_kvload_p = {0: -1, 1: -1}
            i_qload_p = {0: -1, 1: -1}
            kv_used = {0: [], 1: []}
            q_used = {0: [], 1: []}

            import os
            _SER = os.environ.get("PH4_SER", "")
            _PIPE = os.environ.get("PH4_PIPE")

            def add(stp, tag=""):
                if os.environ.get("PH4_SERIAL") or (tag and tag in _SER):
                    stp.after = None
                if _PIPE is not None and (not tag or tag not in _PIPE):
                    stp.after = None
                steps.append(stp)
                return len(steps) - 1

            for g in range(KVH):
                gp_ = g % 2
                for hl in range(4):
                    h = 4 * g + hl
                    hp_ = h % 2
                    last_pn = {0: -1, 1: -1}
                    last_tp = {0: -1, 1: -1}

                    if hl == 0:
                        i_kvl = add(Step("sync", 48, lambda e, inc, _g=g, _p=gp_: [
                            inc(e.dma_start(out=kg[_p][:, 0:T],
                                            in_=kT_h[:, (2 * _g) * T:(2 * _g + 1) * T]), 16),
                            inc(e.dma_start(out=kg[_p][:, T:2 * T],
                                            in_=kT_h[:, (2 * _g + 1) * T:(2 * _g + 2) * T]), 16),
                            inc(e.dma_start(out=vg[_p][:, :],
                                            in_=v_h[:, _g * T:(_g + 1) * T]), 16)],
                            after=kv_used[gp_] if kv_used[gp_] else -1))
                        i_kvload_p[gp_] = i_kvl
                        kv_used[gp_] = []
                    i_ql = add(Step("sync", 32, lambda e, inc, _h=h, _p=hp_: [
                        inc(e.dma_start(out=qh[_p][:, 0:T],
                                        in_=qT_h[:, (2 * _h) * T:(2 * _h + 1) * T]), 16),
                        inc(e.dma_start(out=qh[_p][:, T:2 * T],
                                        in_=qT_h[:, (2 * _h + 1) * T:(2 * _h + 2) * T]), 16)],
                        after=q_used[hp_] if q_used[hp_] else -1))
                    i_qload_p[hp_] = i_ql
                    q_used[hp_] = []
                    i_qload = i_ql
                    i_kvload = i_kvload_p[gp_]

                    for qc in range(4):
                        ptpar = 0
                        qc_glob[0] += 1
                        PTof = 0
                        idx_ptp_cur = []
                        idx_expP_last = -1
                        for qs in range(4):
                            qt = 4 * qc + qs
                            par = qt % 2
                            nch = qt // 4 + 1
                            idx_exp_last = -1
                            for kc in range(nch):
                                cw = 512 if kc < qt // 4 else (qt % 4 + 1) * 128
                                isdiag = (kc == qt // 4)
                                bank = pair_glob[0] % 2
                                pair_glob[0] += 1

                                def mme(e, inc, _qt=qt, _kc=kc, _cw=cw, _b=bank,
                                        _gp=gp_, _hp=hp_):
                                    e.matmul(psE0[_b][:, 0:_cw],
                                             qh[_hp][:, _qt * 128:(_qt + 1) * 128],
                                             kg[_gp][:, _kc * 512:_kc * 512 + _cw],
                                             start=True, stop=True)
                                    inc(e.matmul(psE1[_b][:, 0:_cw],
                                                 qh[_hp][:, T + _qt * 128: T + (_qt + 1) * 128],
                                                 kg[_gp][:, T + _kc * 512: T + _kc * 512 + _cw],
                                                 start=True, stop=True))
                                i_mm = add(Step("tensor", 1, mme,
                                                after=[i_qload, i_kvload,
                                                       idx_exp_bank[bank]]), tag="E")
                                kv_used[gp_].append(i_mm)
                                q_used[hp_].append(i_mm)
                                i_pre = i_mm
                                if isdiag:
                                    def trif(e, inc, _cw=cw, _b=bank):
                                        e.tensor_add(psE0[_b][:, _cw - 128:_cw],
                                                     psE0[_b][:, _cw - 128:_cw], tri_s[:, :])
                                        inc(e.tensor_add(psE1[_b][:, _cw - 128:_cw],
                                                         psE1[_b][:, _cw - 128:_cw], tri_s[:, :]))
                                    i_pre = add(Step("vector", 1, trif, after=[i_mm]), tag="E")

                                def expf(e, inc, _kc=kc, _cw=cw, _b=bank, _p=par):
                                    e.activation(e0b[:, _p * T + _kc * 512: _p * T + _kc * 512 + _cw],
                                                 psE0[_b][:, 0:_cw], AF.Exp, scale=ISQ,
                                                 accum_out=A0[:, _p * 4 + _kc:_p * 4 + _kc + 1])
                                    inc(e.activation(e1b[:, _p * T + _kc * 512: _p * T + _kc * 512 + _cw],
                                                     psE1[_b][:, 0:_cw], AF.Exp, scale=ISQ,
                                                     accum_out=A1[:, _p * 4 + _kc:_p * 4 + _kc + 1]))
                                i_exp = add(Step("scalar", 1, expf,
                                                             after=[i_pre, last_pn[par]]), tag="E")
                                idx_exp_bank[bank] = i_exp
                                idx_exp_last = i_exp

                            def rstep(e, inc, _n=nch, _h=h, _p=par):
                                e.reduce_sum(S0[:, _p:_p + 1], A0[:, _p * 4:_p * 4 + _n], axis=AX)
                                vsync(e, e.reduce_sum(S1[:, _p:_p + 1], A1[:, _p * 4:_p * 4 + _n],
                                                      axis=AX))
                                e.reciprocal(r0[:, _p:_p + 1], S0[:, _p:_p + 1])
                                vsync(e, e.reciprocal(lr1[:, _p:_p + 1], S1[:, _p:_p + 1]))
                                inc(e.tensor_mul(lr1[:, _p:_p + 1], lr1[:, _p:_p + 1],
                                                 lamrep[:, _h:_h + 1]))
                            i_r = add(Step("vector", 1, rstep,
                                           after=[idx_exp_last, last_pn[par]]), tag="R")

                            kspan = (qt + 1) * 128

                            def dif(e, inc, _ks=kspan, _qt=qt, _p=par):
                                sl = slice(_p * T, _p * T + _ks)
                                e.tensor_scalar(e1b[:, sl], e1b[:, sl], lr1[:, _p:_p + 1],
                                                None, AO.mult)
                                vsync(e, e.tensor_scalar(e0b[:, sl], e0b[:, sl],
                                                         r0[:, _p:_p + 1], None, AO.mult))
                                vsync(e, e.tensor_sub(e0b[:, sl], e0b[:, sl], e1b[:, sl]))
                                dsl = slice(_p * T + _qt * 128, _p * T + (_qt + 1) * 128)
                                inc(e.tensor_add(e0b[:, dsl], e0b[:, dsl], tri_s[:, :]))
                            i_dif = add(Step("vector", 1, dif, after=[i_r]), tag="D")

                            def expP(e, inc, _ks=kspan, _p=par):
                                inc(e.activation(Pf[:, 0:_ks], e0b[:, _p * T: _p * T + _ks],
                                                 AF.Exp,
                                                 accum_out=SZ[:, _p:_p + 1]))
                            idx_expP_last = add(Step("scalar", 1, expP,
                                                     after=[i_dif, last_pn[0], last_pn[1]]),
                                                tag="D")

                            pbslot = (qc % 2) * 4 + qs

                            def pnorm(e, inc, _qt=qt, _p=par, _sl=pbslot):
                                vsync(e, e.reciprocal(r2[:, _p:_p + 1], SZ[:, _p:_p + 1]))
                                inc(e.tensor_scalar(Pb[:, _sl * T: _sl * T + (_qt + 1) * 128],
                                                    Pf[:, 0:(_qt + 1) * 128],
                                                    r2[:, _p:_p + 1], None, AO.mult))
                            i_pn = add(Step("vector", 1, pnorm,
                                            after=[idx_expP_last, idx_attmm_prev2]), tag="P")
                            last_pn[par] = i_pn

                            n_even = (qt + 2) // 2

                            def ptp_e(e, inc, _qt=qt, _qs=qs, _of=PTof, _sl=pbslot):
                                for kt in range(0, _qt + 1, 2):
                                    inc(e.dma_start_transpose(
                                        PT[:, _of + kt * 512 + _qs * 128: _of + kt * 512 + (_qs + 1) * 128],
                                        Pb[:, _sl * T + kt * 128: _sl * T + (kt + 1) * 128]), 16)
                            i_tp = add(Step("sync", 16 * n_even, ptp_e,
                                            after=[i_pn] + idx_ptp_prev[ptpar]), tag="T")
                            idx_ptp_cur.append(i_tp)
                            last_tp[par] = i_tp
                            for kt in range(1, qt + 1, 2):
                                tb = tptb_ctr[0] % 2
                                tptb_ctr[0] += 1
                                i_tt = add(Step("tensor", 1,
                                               lambda e, inc, _kt=kt, _sl=pbslot, _tb=tb:
                                               inc(e.transpose(psT4[_tb][:, :],
                                                               Pb[:, _sl * T + _kt * 128:
                                                                  _sl * T + (_kt + 1) * 128],
                                                               idn_s[:, :])),
                                               after=[i_pn, tp_evac_b[tb]] + idx_ptp_prev[ptpar]),
                                           tag="T")
                                ee = "scalar" if (tptb_ctr[0] % 2 == 0) else "vector"

                                def tevac_s(e, inc, _kt=kt, _qs=qs, _of=PTof, _tb=tb):
                                    inc(e.copy(out=PT[:, _of + _kt * 512 + _qs * 128:
                                                      _of + _kt * 512 + (_qs + 1) * 128],
                                               in_=psT4[_tb][:, :]))

                                def tevac_v(e, inc, _kt=kt, _qs=qs, _of=PTof, _tb=tb):
                                    inc(e.tensor_copy(
                                        PT[:, _of + _kt * 512 + _qs * 128:
                                           _of + _kt * 512 + (_qs + 1) * 128],
                                        psT4[_tb][:, :]))
                                i_ev4 = add(Step(ee, 1, tevac_s if ee == "scalar" else tevac_v,
                                                 after=[i_tt]), tag="T")
                                tp_evac_b[tb] = i_ev4
                                idx_ptp_cur.append(i_ev4)
                        zblocks = [(kt, qs) for kt in range(4 * qc + 4) for qs in range(4)
                                   if 4 * qc + qs < kt]

                        def zset(e, inc, _zb=zblocks, _of=PTof):
                            last = None
                            for kt, qs in _zb:
                                last = e.memset(
                                    PT[:, _of + kt * 512 + qs * 128: _of + kt * 512 + (qs + 1) * 128], 0.0)
                            if last is None:
                                last = e.tensor_copy(S0[:, :], S0[:, :])
                            inc(last)
                        i_z = add(Step("vector", 1, zset,
                                       after=[steps.__len__() - 1] + idx_ptp_prev[ptpar]), tag="Z")

                        def attmm(e, inc, _qc=qc, _of=PTof, _gp=gp_):
                            for kt in range(4 * _qc + 4):
                                mm = e.matmul(psAt[:, :], vg[_gp][:, kt * 128:(kt + 1) * 128],
                                              PT[:, _of + kt * 512: _of + (kt + 1) * 512],
                                              start=(kt == 0), stop=(kt == 4 * _qc + 3))
                            inc(mm)
                        i_am = add(Step("tensor", 1, attmm,
                                        after=idx_ptp_cur + [i_z, idx_attts_prev, idx_sq_prev]),
                                   tag="A")
                        idx_ptp_prev[ptpar] = [i_am]

                        i_sq = add(Step("scalar", 1, lambda e, inc:
                                        inc(e.activation(sqb[:, :], psAt[:, :], AF.Square)),
                                        after=[i_am, idx_rowmm_prev]), tag="A")
                        i_row = add(Step("tensor", 1, lambda e, inc:
                                         inc(e.matmul(psRo[0:1, :], onc[:, :], sqb[:, :],
                                                      start=True, stop=True)),
                                         after=[i_sq, idx_mscp_prev]), tag="Q")
                        i_mscp = add(Step("scalar", 1, lambda e, inc:
                                          inc(e.copy(out=msrow[0:1, :], in_=psRo[0:1, :])),
                                          after=[i_row, idx_msdma_prev]), tag="Q")
                        i_msdma = add(Step("sync", 16, lambda e, inc, _h=h, _qc=qc:
                                           inc(e.dma_start(
                                               out=msb[_h * 4 + _qc:_h * 4 + _qc + 1, :],
                                               in_=msrow[0:1, :]), 16),
                                           after=[i_mscp]), tag="Q")
                        i_ts = add(Step("vector", 1, lambda e, inc, _h=h, _qc=qc:
                                        inc(e.tensor_scalar(
                                            attT[:, _h * T + _qc * 512: _h * T + (_qc + 1) * 512],
                                            psAt[:, :], rms_s[:, :], None, AO.mult)),
                                        after=[i_am, i_sq]), tag="S")
                        idx_attmm_prev2 = idx_attmm_prev
                        idx_attmm_prev = i_am
                        idx_attts_prev = i_ts
                        idx_sq_prev = i_sq
                        idx_rowmm_prev = i_row
                        idx_mscp_prev = i_mscp
                        idx_msdma_prev = i_msdma
                        kv_used[gp_].append(i_am)
            ch.run(steps)

            steps = [
                Step("scalar", 1, lambda e, inc:
                     inc(e.activation(taub[0:64, :], msb[0:64, :], AF.Sqrt,
                                      scale=1.0 / (128.0 * CR * CR), bias=eps2_s[0:64, :]))),
                Step("vector", 1, lambda e, inc:
                     inc(e.reciprocal(taub[0:64, :], taub[0:64, :]))),
            ]
            for sb8 in range(32):
                steps.append(Step("sync", 16, lambda e, inc, _s=sb8:
                                  inc(e.dma_start(
                                      out=taurow[0:1, :].rearrange("p (r c) -> p r c", r=2),
                                      in_=taub[2 * _s:2 * _s + 2, :]), 16)))
                for rr in range(2):
                    r = sb8 * 2 + rr
                    steps.append(Step("tensor", 1, lambda e, inc, _rr=rr:
                                      inc(e.matmul(psRp[:, :], onr[:, :],
                                                   taurow[0:1, _rr * 512:(_rr + 1) * 512],
                                                   start=True, stop=True))))
                    h, qc = r // 4, r % 4
                    steps.append(Step("vector", 1, lambda e, inc, _h=h, _qc=qc:
                                      inc(e.tensor_mul(
                                          attT[:, _h * T + _qc * 512: _h * T + (_qc + 1) * 512],
                                          attT[:, _h * T + _qc * 512: _h * T + (_qc + 1) * 512],
                                          psRp[:, :]))))
            ch.run(steps)
            if att_t is not None:
                ch.run([Step("gpsimd", 16, lambda e, inc:
                             inc(e.dma_start(out=att_t[:, :], in_=attT[:, :]), 16))])

            if upto < 6:
                return _finish(nc, ch)
            ph4i.close()
            sb = sb4
            wost = [sb(f"wost{i}", [128, 2048], BF16) for i in range(3)]
            sstg = [sb(f"sstg{i}", [128, 512], BF16) for i in range(4)]
            psS = [psE0[0], psE0[1],
                   ph.enter_context(nc.psum_tensor("psS2", [128, 512], F32)),
                   ph.enter_context(nc.psum_tensor("psS3", [128, 512], F32))]
            steps = []

            def add5(stp):
                steps.append(stp)
                return len(steps) - 1

            i_ev_p = {i: -1 for i in range(4)}
            i_st_p = {i: -1 for i in range(4)}
            mm_ctr = [0]
            i_mm_last_p = {i: -1 for i in range(3)}
            for ot in range(16):
                wpar = ot % 3

                def wl5(e, inc, _ot=ot, _p=wpar):
                    for ec in range(16):
                        inc(e.dma_start(out=wost[_p][:, ec * 128:(ec + 1) * 128],
                                        in_=wo[ec * 128:(ec + 1) * 128,
                                               _ot * 128:(_ot + 1) * 128]), 16)
                i_wl = add5(Step("gpsimd", 16 * 16, wl5, after=[i_mm_last_p[wpar]]))
                for tc in range(4):
                    b = mm_ctr[0] % 4
                    mm_ctr[0] += 1

                    def mmo(e, inc, _tc=tc, _p=wpar, _b=b):
                        for ec in range(16):
                            mm = e.matmul(psS[_b][:, :], wost[_p][:, ec * 128:(ec + 1) * 128],
                                          attT[:, ec * T + _tc * 512: ec * T + (_tc + 1) * 512],
                                          start=(ec == 0), stop=(ec == 15))
                        inc(mm)
                    i_mm = add5(Step("tensor", 1, mmo, after=[i_wl, i_ev_p[b]]))
                    i_ev = add5(Step("scalar", 1, lambda e, inc, _b=b, _sp=b:
                                     inc(e.copy(out=sstg[_sp][:, :], in_=psS[_b][:, :])),
                                     after=[i_mm, i_st_p[b]]))
                    i_ev_p[b] = i_ev
                    i_st = add5(Step("gpsimd", 16, lambda e, inc, _ot=ot, _tc=tc, _sp=b:
                                     inc(e.dma_start(out=s_hbm[:, _ot * T + _tc * 512:
                                                              _ot * T + (_tc + 1) * 512],
                                                     in_=sstg[_sp][:, :]), 16),
                                     after=[i_ev]))
                    i_st_p[b] = i_st
                i_mm_last_p[wpar] = i_mm
            ch.run(steps)

        with ExitStack() as ph:
            sb = lambda n, sh, ty: ph.enter_context(nc.sbuf_tensor(n, sh, ty))
            xqs = sb("xqs", [128, NE * T], BF16)
            wstg = [sb(f"wstg{i}", [128, 2048], F32) for i in range(3)]
            i8sl = [sb(f"i8sl{i}", [128, 2048], I8) for i in range(3)]
            tsl = [sb(f"tsl{i}", [128, 2048], BF16) for i in range(3)]
            rlb = [sb(f"rlb{i}", [128, 512], F32) for i in range(4)]
            h2b = [sb(f"h2b{i}", [128, 512], F32) for i in range(4)]
            psU = [ph.enter_context(nc.psum_tensor(f"psU{i}", [128, 512], F32)) for i in range(4)]
            ch.run([Step("sync", 16 * 16, lambda e, inc: [
                inc(e.dma_start(out=xqs[:, f * T:(f + 1) * T], in_=xq_h[:, f * T:(f + 1) * T]), 16)
                for f in range(16)])])
            steps = []

            def add6(stp):
                steps.append(stp)
                return len(steps) - 1

            i_i8_p = {i: -1 for i in range(3)}
            i_cl_p = {i: -1 for i in range(3)}
            i_mml_p = {i: -1 for i in range(3)}
            i_tt_b = {i: -1 for i in range(4)}
            i_dma_b = {i: -1 for i in range(4)}
            mm_ctr = [0]
            for o in range(NF2):
                p = o % 3

                def wl6(e, inc, _o=o, _p=p):
                    for ec in range(16):
                        inc(e.dma_start(out=wstg[_p][:, ec * 128:(ec + 1) * 128],
                                        in_=wupT[ec * 128:(ec + 1) * 128,
                                                 _o * 128:(_o + 1) * 128]), 16)
                i_wl = add6(Step("sync", 16 * 16, wl6, after=[i_i8_p[p]]))
                i_i8 = add6(Step("scalar", 1, lambda e, inc, _p=p:
                                 inc(e.activation(i8sl[_p][:, :], wstg[_p][:, :], AF.Copy,
                                                  scale=rwsu[:, :])),
                                 after=[i_wl, i_cl_p[p]]))
                i_i8_p[p] = i_i8
                i_cl = add6(Step("vector", 1, lambda e, inc, _p=p:
                                 inc(e.tensor_scalar(tsl[_p][:, :], i8sl[_p][:, :], 1.0, -1.0,
                                                     AO.min, AO.max)),
                                 after=[i_i8, i_mml_p[p]]))
                i_cl_p[p] = i_cl
                for tc in range(4):
                    b = mm_ctr[0] % 4
                    mm_ctr[0] += 1

                    def mmu(e, inc, _tc=tc, _p=p, _b=b):
                        for ec in range(16):
                            mm = e.matmul(psU[_b][:, :], tsl[_p][:, ec * 128:(ec + 1) * 128],
                                          xqs[:, ec * T + _tc * 512: ec * T + (_tc + 1) * 512],
                                          start=(ec == 0), stop=(ec == 15))
                        inc(mm)
                    i_mm = add6(Step("tensor", 1, mmu, after=[i_cl, i_tt_b[b]]))
                    i_rl = add6(Step("scalar", 1, lambda e, inc, _b=b:
                                     inc(e.activation(rlb[_b][:, :], psU[_b][:, :], AF.Relu)),
                                     after=[i_mm]))
                    i_tt = add6(Step("vector", 1, lambda e, inc, _b=b:
                                     inc(e.tensor_mul(h2b[_b][:, :], rlb[_b][:, :], psU[_b][:, :])),
                                     after=[i_rl, i_dma_b[b]]))
                    i_tt_b[b] = i_tt
                    i_dma = add6(Step("gpsimd", 16, lambda e, inc, _o=o, _tc=tc, _b=b:
                                      inc(e.dma_start(out=h2_h[:, _o * T + _tc * 512:
                                                               _o * T + (_tc + 1) * 512],
                                                      in_=h2b[_b][:, :]), 16),
                                      after=[i_tt]))
                    i_dma_b[b] = i_dma
                i_mml_p[p] = i_mm
            ch.run(steps)

        with ExitStack() as ph:
            sb = lambda n, sh, ty: ph.enter_context(nc.sbuf_tensor(n, sh, ty))
            K2 = sb("K2", [128, NF2 * T], BF16)
            ph7 = ph.enter_context(ExitStack())
            sb7 = lambda n, sh, ty: ph7.enter_context(nc.sbuf_tensor(n, sh, ty))
            W = [sb7(f"W2f{i}", [128, 256], F32) for i in range(48)]
            mx = sb7("mx2", [128, 256], F32)
            mn = sb7("mn2", [128, 256], F32)
            ptmp = sb7("ptmp2", [64, 256], F32)
            Sb2 = sb7("Sb22", [128, 256], F32)
            qi8 = [sb7(f"qi82{i}", [128, 256], I8) for i in range(8)]
            mrow2 = sb7("mrow2", [1, 256], F32)
            psHn = [ph7.enter_context(nc.psum_tensor(f"psH2{i}", [128, 256], F32))
                    for i in range(4)]
            psR = ph7.enter_context(nc.psum_tensor("psR2", [128, 256], F32))


            def butterflies2(e, inc, nt=32):
                slot = list(range(nt))
                free = list(range(nt, nt + nt // 2))
                for p in range(5):
                    b = 1 << p
                    pairs = [(i, i | b) for i in range(nt) if not (i & b)]
                    last = None
                    for idx, (i, j) in enumerate(pairs):
                        last = e.tensor_sub(W[free[idx]][:, :], W[slot[i]][:, :],
                                            W[slot[j]][:, :])
                    vsync(e, last)
                    last = None
                    for i, j in pairs:
                        last = e.tensor_add(W[slot[i]][:, :], W[slot[i]][:, :],
                                            W[slot[j]][:, :])
                    vsync(e, last)
                    for idx, (i, j) in enumerate(pairs):
                        slot[j], free[idx] = free[idx], slot[j]
                for op, dstbuf, neg in ((AO.max, mx, False), (AO.min, mn, True)):
                    last = None
                    for t in range(nt // 2):
                        last = e.tensor_tensor(W[free[t]][:, :], W[slot[2 * t]][:, :],
                                               W[slot[2 * t + 1]][:, :], op)
                    vsync(e, last)
                    gap = 1
                    while gap < nt // 2:
                        last = None
                        for t in range(0, nt // 2, 2 * gap):
                            last = e.tensor_tensor(W[free[t]][:, :], W[free[t]][:, :],
                                                   W[free[t + gap]][:, :], op)
                        vsync(e, last)
                        gap *= 2
                    if neg:
                        vsync(e, e.tensor_scalar_mul(dstbuf[:, :], W[free[0]][:, :], -1.0))
                    else:
                        vsync(e, e.tensor_copy(dstbuf[:, :], W[free[0]][:, :]))
                inc(e.tensor_max(mx[:, :], mx[:, :], mn[:, :]))
                return slot

            ld_prev = [-1]
            mm_prev_b = {i: -1 for i in range(4)}
            ev_prev = [-1]
            qt_prev = [-1]
            srep_prev = [-1]
            steps = []

            def add7(stp):
                steps.append(stp)
                return len(steps) - 1

            for cc in range(8):
                def ld7(e, inc, _cc=cc):
                    for o in range(NF2):
                        inc(e.dma_start(out=W[o][:, :],
                                        in_=h2_h[:, o * T + _cc * 256:
                                                 o * T + (_cc + 1) * 256]), 16)
                i_ld = add7(Step("sync", 16 * 32, ld7, after=[qt_prev[0]]))
                i_ev_last = -1
                for o in range(NF2):
                    b = o % 4
                    i_mm = add7(Step("tensor", 1, lambda e, inc, _o=o, _b=b:
                                     inc(e.matmul(psHn[_b][:, :],
                                                  h128f_s[:, :], W[_o][:, :],
                                                  start=True, stop=True)),
                                     after=[i_ld, mm_prev_b[b]]))
                    i_ev = add7(Step("scalar", 1, lambda e, inc, _o=o, _b=b:
                                     inc(e.copy(out=W[_o][:, :], in_=psHn[_b][:, :])),
                                     after=[i_mm]))
                    mm_prev_b[b] = i_ev
                    i_ev_last = i_ev
                sstore = {}
                i_bf = add7(Step("vector", 1,
                                 lambda e, inc: sstore.__setitem__("s", butterflies2(e, inc)),
                                 after=[i_ev_last]))
                i_prev = i_bf
                for r in range(7):
                    p = 64 >> r
                    i_d = add7(Step("sync", 16, lambda e, inc, _p=p:
                                    inc(e.dma_start(out=ptmp[0:_p, :], in_=mx[_p:2 * _p, :]), 16),
                                    after=[i_prev]))
                    i_prev = add7(Step("vector", 1, lambda e, inc, _p=p:
                                       inc(e.tensor_max(mx[0:_p, :], mx[0:_p, :], ptmp[0:_p, :])),
                                       after=[i_d]))
                i_bt = add7(Step("vector", 1, lambda e, inc, _cc=cc: (
                    vsync(e, e.tensor_scalar_max(mrow2[0:1, :], mx[0:1, :], 1e-5)),
                    vsync(e, e.tensor_mul(betar[0:1, _cc * 256:(_cc + 1) * 256],
                                          mrow1[0:1, _cc * 256:(_cc + 1) * 256],
                                          mrow1[0:1, _cc * 256:(_cc + 1) * 256])),
                    vsync(e, e.tensor_mul(betar[0:1, _cc * 256:(_cc + 1) * 256],
                                          betar[0:1, _cc * 256:(_cc + 1) * 256], mrow2[0:1, :])),
                    inc(e.tensor_scalar(betar[0:1, _cc * 256:(_cc + 1) * 256],
                                        betar[0:1, _cc * 256:(_cc + 1) * 256],
                                        combo[0:1, :], None, AO.mult))),
                    after=[i_prev, srep_prev[0]]))
                i_rp = add7(Step("tensor", 1, lambda e, inc:
                                 inc(e.matmul(psR[:, 0:256], onr[:, :], mrow2[0:1, :],
                                              start=True, stop=True)),
                                 after=[i_bt]))
                i_sr = add7(Step("vector", 1, lambda e, inc: (
                    vsync(e, e.tensor_scalar_mul(Sb2[:, :], psR[:, 0:256], 1.0 / 127.0)),
                    inc(e.reciprocal(Sb2[:, :], Sb2[:, :]))),
                    after=[i_rp]))
                srep_prev[0] = i_sr
                def qf(e, inc, _cc=cc):
                    s = sstore["s"]
                    last = None
                    for g in range(4):
                        for j in range(8):
                            last = e.tensor_mul(qi8[j][:, :], W[s[g * 8 + j]][:, :],
                                                Sb2[:, :])
                        vsync(e, last)
                        for j in range(8):
                            o = g * 8 + j
                            last = e.tensor_copy(
                                K2[:, o * T + _cc * 256: o * T + (_cc + 1) * 256],
                                qi8[j][:, :])
                        if g < 3:
                            vsync(e, last)
                    inc(last)
                qt_prev[0] = add7(Step("vector", 1, qf, after=[i_sr]))
            ch.run(steps)
            if k2_t is not None:
                ch.run([Step("gpsimd", 16 * 8, lambda e, inc: [
                    inc(e.dma_start(out=k2_t[:, i * 8192:(i + 1) * 8192],
                                    in_=K2[:, i * 8192:(i + 1) * 8192]), 16) for i in range(8)])])

            ph7.close()

            if upto < 9:
                return _finish(nc, ch)
            brep = sb("brep", [128, T], F32)
            brepp = ph.enter_context(nc.psum_tensor("psB0", [128, 512], F32))
            steps = []
            for tc in range(4):
                steps.append(Step("tensor", 1, lambda e, inc, _tc=tc:
                                  inc(e.matmul(brepp[:, :], onr[:, :],
                                               betar[0:1, _tc * 512:(_tc + 1) * 512],
                                               start=True, stop=True))))
                steps.append(Step("scalar", 1, lambda e, inc, _tc=tc:
                                  inc(e.copy(out=brep[:, _tc * 512:(_tc + 1) * 512],
                                             in_=brepp[:, :]))))
            ch.run(steps)

            wst2 = sb("wst2", [128, 4096], F32)
            i8s2 = sb("i8s2", [128, 4096], I8)
            tsl2 = [sb(f"tsl2{i}", [128, 4096], BF16) for i in range(2)]
            ub = [sb(f"ub{i}", [128, 512], F32) for i in range(2)]
            gb = ub
            stile = [sb(f"stile{i}", [128, 512], BF16) for i in range(2)]
            sgb = [sb(f"sgb{i}", [128, 512], BF16) for i in range(2)]
            xblk4 = [sb(f"xblk4{i}", [128, 512], F32) for i in range(2)]
            obuf4 = [sb(f"obuf4{i}", [128, 512], F32) for i in range(2)]
            psD = [ph.enter_context(nc.psum_tensor(f"psD{i}", [128, 512], F32)) for i in range(3)]
            psX = [ph.enter_context(nc.psum_tensor(f"psX{i}", [128, 128], BF16)) for i in range(3)]
            steps = []

            def add8(stp):
                steps.append(stp)
                return len(steps) - 1

            i_i8_l = [-1]
            i_cl_p = {0: -1, 1: -1}
            i_mml_p = {0: -1, 1: -1}
            i_u_b = {i: -1 for i in range(3)}
            i_sg_p = {i: -1 for i in range(3)}
            i_tp_b = {i: -1 for i in range(3)}
            i_xa_b = {i: -1 for i in range(3)}
            i_od_p = {i: -1 for i in range(3)}
            mm_ctr = [0]
            tp_ctr = [0]
            for o2 in range(16):
                p = o2 % 2

                def wl8(e, inc, _o2=o2):
                    for ec in range(32):
                        inc(e.dma_start(out=wst2[:, ec * 128:(ec + 1) * 128],
                                        in_=wdnT[ec * 128:(ec + 1) * 128,
                                                 _o2 * 128:(_o2 + 1) * 128]), 16)
                i_wl = add8(Step("sync", 16 * 32, wl8, after=[i_i8_l[0]]))
                i_i8 = add8(Step("scalar", 1, lambda e, inc:
                                 inc(e.activation(i8s2[:, :], wst2[:, :], AF.Copy,
                                                  scale=rwsd[:, :])),
                                 after=[i_wl, i_cl_p[1 - p]]))
                i_i8_l[0] = i_i8
                i_cl = add8(Step("vector", 1, lambda e, inc, _p=p:
                                 inc(e.tensor_scalar(tsl2[_p][:, :], i8s2[:, :], 1.0, -1.0,
                                                     AO.min, AO.max)),
                                 after=[i_i8, i_mml_p[p]]))
                i_cl_p[p] = i_cl
                for tc in range(4):
                    b = mm_ctr[0] % 3
                    bb = mm_ctr[0] % 2
                    mm_ctr[0] += 1
                    i_sl = add8(Step("gpsimd", 16, lambda e, inc, _o2=o2, _tc=tc, _bb=bb:
                                     inc(e.dma_start(out=stile[_bb][:, :],
                                                     in_=s_hbm[:, _o2 * T + _tc * 512:
                                                              _o2 * T + (_tc + 1) * 512]), 16),
                                     after=[i_sg_p[bb]]))
                    i_xl = add8(Step("gpsimd", 16, lambda e, inc, _o2=o2, _tc=tc, _bb=bb:
                                     inc(e.dma_start(
                                         out=xblk4[_bb][:, :].rearrange("p (i c) -> p i c", i=4),
                                         in_=x[_tc * 512:(_tc + 1) * 512,
                                               _o2 * 128:(_o2 + 1) * 128]
                                         .rearrange("(i p) c -> p i c", i=4)), 16),
                                     after=[i_xa_b[bb]]))

                    def mmd(e, inc, _tc=tc, _p=p, _b=b):
                        for ec in range(32):
                            mm = e.matmul(psD[_b][:, :], tsl2[_p][:, ec * 128:(ec + 1) * 128],
                                          K2[:, ec * T + _tc * 512: ec * T + (_tc + 1) * 512],
                                          start=(ec == 0), stop=(ec == 31))
                        inc(mm)
                    i_mm = add8(Step("tensor", 1, mmd, after=[i_cl, i_u_b[b]]))
                    i_u = add8(Step("vector", 1, lambda e, inc, _tc=tc, _b=b, _bb=bb:
                                    inc(e.tensor_mul(ub[_bb][:, :], psD[_b][:, :],
                                                     brep[:, _tc * 512:(_tc + 1) * 512])),
                                    after=[i_mm, i_sg_p[bb]]))
                    i_u_b[b] = i_u
                    i_g = add8(Step("scalar", 1, lambda e, inc, _bb=bb:
                                    inc(e.activation(gb[_bb][:, :], ub[_bb][:, :], AF.Silu)),
                                    after=[i_u]))
                    i_sg = add8(Step("vector", 1, lambda e, inc, _bb=bb:
                                     inc(e.tensor_mul(sgb[_bb][:, :], gb[_bb][:, :],
                                                      stile[_bb][:, :])),
                                     after=[i_g, i_sl]))
                    i_sg_p[bb] = i_sg
                    i_xa_last = -1
                    for i in range(4):
                        xb = tp_ctr[0] % 3
                        tp_ctr[0] += 1
                        i_t = add8(Step("tensor", 1, lambda e, inc, _i=i, _xb=xb, _bb=bb:
                                        inc(e.transpose(psX[_xb][:, :],
                                                        sgb[_bb][:, _i * 128:(_i + 1) * 128],
                                                        idn_s[:, :])),
                                        after=[i_sg, i_tp_b[xb]]))
                        i_xa = add8(Step("vector", 1, lambda e, inc, _i=i, _xb=xb, _bb=bb:
                                         inc(e.tensor_add(obuf4[_bb][:, _i * 128:(_i + 1) * 128],
                                                          psX[_xb][:, :],
                                                          xblk4[_bb][:, _i * 128:(_i + 1) * 128])),
                                         after=[i_t, i_xl, i_od_p[bb]]))
                        i_tp_b[xb] = i_xa
                        i_xa_last = i_xa
                    i_xa_b[bb] = i_xa_last
                    i_od = add8(Step("gpsimd", 16, lambda e, inc, _o2=o2, _tc=tc, _bb=bb:
                                     inc(e.dma_start(
                                         out=out[_tc * 512:(_tc + 1) * 512,
                                                 _o2 * 128:(_o2 + 1) * 128]
                                         .rearrange("(i p) c -> p i c", i=4),
                                         in_=obuf4[_bb][:, :].rearrange("p (i c) -> p i c", i=4)), 16),
                                     after=[i_xa_last]))
                    i_od_p[bb] = i_od
                i_mml_p[p] = i_mm
            ch.run(steps)

        return _finish(nc, ch)


def _finish(nc, ch):
    fin = dict(ch.cnt)
    sems = dict(ch.sems)
    with nc.Block() as blk:
        @blk.gpsimd
        def _(gp):
            for ln, c in fin.items():
                if c > 0:
                    gp.wait_ge(sems[ln], c)
    return nc


def _prep_inputs(L0_attn_x, L0_attn_Wq, L0_attn_Wk, L0_attn_Wv, L0_attn_Wo,
                 L0_attn_lq1, L0_attn_lq2, L0_attn_lk1, L0_attn_lk2,
                 L0_attn_rms_w, L0_ffn_w_up, L0_ffn_w_down):
    cons = _host_consts()
    f32 = np.float32
    wq = np.ascontiguousarray(np.asarray(L0_attn_Wq, dtype=f32)[:, _perm_cols(H)])
    wk = np.ascontiguousarray(np.asarray(L0_attn_Wk, dtype=f32)[:, _perm_cols(KVH)])
    wv = np.ascontiguousarray(np.asarray(L0_attn_Wv, dtype=f32))
    wo = np.ascontiguousarray(np.asarray(L0_attn_Wo, dtype=f32))
    wupT = np.ascontiguousarray(np.asarray(L0_ffn_w_up, dtype=f32).T)
    wdnT = np.ascontiguousarray(np.asarray(L0_ffn_w_down, dtype=f32).T)
    base = {
        "wq": wq, "wk": wk, "wv": wv, "wo": wo, "wupT": wupT, "wdnT": wdnT,
        "lq1": np.ascontiguousarray(np.asarray(L0_attn_lq1, dtype=f32).T),
        "lq2": np.ascontiguousarray(np.asarray(L0_attn_lq2, dtype=f32).T),
        "lk1": np.ascontiguousarray(np.asarray(L0_attn_lk1, dtype=f32).T),
        "lk2": np.ascontiguousarray(np.asarray(L0_attn_lk2, dtype=f32).T),
        "rmsw": np.ascontiguousarray(np.asarray(L0_attn_rms_w, dtype=f32).reshape(128, 1)),
        "c_lo": cons["c_lo"], "s_lo": cons["s_lo"],
        "c_hi": cons["c_hi"], "s_hi": cons["s_hi"],
        "trid": cons["tri"], "h128bd": cons["h128b"], "h128fd": cons["h128f"],
        "idnd": cons["idn"], "ones_rd": cons["ones_r"], "ones_cd": cons["ones_c"],
        "eps1d": np.full((128, 1), 1e-5, dtype=f32),
        "eps2d": np.full((128, 1), 1e-8 / (CR * CR), dtype=f32),
        "ohd": (np.arange(127)[None, :] == 63).astype(f32) * np.ones((128, 1), f32),
    }
    xs = np.asarray(L0_attn_x, dtype=f32)
    in_maps = []
    for b in range(8):
        m = dict(base)
        m["x"] = np.ascontiguousarray(xs[b])
        in_maps.append(m)
    return in_maps


def kernel(L0_attn_x, L0_attn_Wq, L0_attn_Wk, L0_attn_Wv, L0_attn_Wo,
           L0_attn_lq1, L0_attn_lq2, L0_attn_lk1, L0_attn_lk2,
           L0_attn_rms_w, L0_ffn_w_up, L0_ffn_w_down):
    in_maps = _prep_inputs(L0_attn_x, L0_attn_Wq, L0_attn_Wk, L0_attn_Wv, L0_attn_Wo,
                           L0_attn_lq1, L0_attn_lq2, L0_attn_lk1, L0_attn_lk2,
                           L0_attn_rms_w, L0_ffn_w_up, L0_ffn_w_down)
    nc = build_program()
    res = run_bass_kernel_spmd(nc, in_maps, core_ids=list(range(8)))
    return np.stack([res.results[b]["out"] for b in range(8)], axis=0)
